# version 01
# baseline (speedup 1.0000x reference)
import numpy as np
import ml_dtypes
import concourse.bass as bass
import concourse.mybir as mybir
from concourse.bass_utils import run_bass_kernel_spmd

F32 = mybir.dt.float32
BF16 = mybir.dt.bfloat16
AF = mybir.ActivationFunctionType
ALU = mybir.AluOpType
AX = mybir.AxisListType.X
HB = ml_dtypes.bfloat16

NFC = 8
TPC = 2048
GT = 256
NGRP = TPC // GT
CH = 128
NCH = TPC // CH
NH = 16
HD = 64
NE = 8
FTG = 256
NFT = TPC // FTG

ENG2SEM = {"pe": "pe", "ac": "ac", "dv": "dv", "gp": "gp", "sp": "sq"}


class Buf:
    __slots__ = ("w", "rs", "nm")

    def __init__(self, nm=""):
        self.w = None
        self.rs = {}
        self.nm = nm


class BK:
    def __init__(self, nc):
        self.nc = nc
        self.cnt = {}
        self.recs = {"pe": [], "ac": [], "dv": [], "gp": [], "sp": []}

    def S(self, name):
        self.cnt.setdefault(name, 0)
        return name

    def now(self, name):
        return (name, self.cnt.get(name, 0))

    def op(self, eng, fn, reads=(), writes=(), sem=None, inc=1, waits=()):
        w = {}

        def merge(tok):
            if tok is None:
                return
            s, v = tok
            if v > 0 and w.get(s, 0) < v:
                w[s] = v

        for b in reads:
            merge(b.w)
        for b in writes:
            merge(b.w)
            for s, v in b.rs.items():
                merge((s, v))
        for t in waits:
            merge(t)
        sem = sem or ENG2SEM[eng]
        self.S(sem)
        self.cnt[sem] += inc
        tok = (sem, self.cnt[sem])
        wset = set(map(id, writes))
        for b in writes:
            b.w = tok
            b.rs = {}
        for b in reads:
            if id(b) not in wset:
                if b.rs.get(tok[0], 0) < tok[1]:
                    b.rs[tok[0]] = tok[1]
        self.recs[eng].append((sorted(w.items()), fn, (sem, inc)))
        return tok

    def replay(self, block, sems):
        def mk(recs):
            def body(eng):
                lastw = {}
                for waits, fn, (s, n) in recs:
                    for sname, v in waits:
                        if lastw.get(sname, -1) < v:
                            eng.wait_ge(sems[sname], v)
                            lastw[sname] = v
                    inst = fn(eng)
                    if inst is not None and n > 0:
                        inst.then_inc(sems[s], n)
            return body

        block.tensor(mk(self.recs["pe"]))
        block.scalar(mk(self.recs["ac"]))
        block.vector(mk(self.recs["dv"]))
        block.gpsimd(mk(self.recs["gp"]))
        block.sync(mk(self.recs["sp"]))


def mmg(bk, mms, reads, writes, waits=()):
    def fn(te, mms=tuple(mms)):
        inst = None
        for (o, l, r, st, sp) in mms:
            inst = te.matmul(o, l, r, start=st, stop=sp)
        return inst
    return bk.op("pe", fn, reads=reads, writes=writes, waits=waits)


def trg(bk, trs, reads, writes, waits=()):
    def fn(te, trs=tuple(trs)):
        inst = None
        for (o, i, d) in trs:
            inst = te.transpose(o, i, d)
        return inst
    return bk.op("pe", fn, reads=reads, writes=writes, waits=waits)



def _host_consts():
    cbf_cols = {}
    parts = []

    def add(name, arr):
        arr = np.asarray(arr, np.float32)
        a = np.zeros((128, arr.shape[1]), np.float32)
        a[: arr.shape[0]] = arr
        cbf_cols[name] = sum(p.shape[1] for p in parts)
        parts.append(a)

    ident = np.eye(128, dtype=np.float32)
    add("ident", ident)
    t = np.tril(np.ones((128, 128), np.float32), -1)
    add("mask_nl", np.concatenate([-t, -t], 1))
    add("mask_nu", np.concatenate([-t.T, -t.T], 1))
    u = np.triu(np.ones((128, 128), np.float32), 0)
    add("mask_ui", np.concatenate([u, u], 1))
    e8 = np.zeros((8, 8 * 128), np.float32)
    for e in range(8):
        e8[e, e * 128:(e + 1) * 128] = 1.0
    add("e8sel", e8)
    efc = np.zeros((128, 8 * 16), np.float32)
    for fc in range(8):
        for p in range(128):
            efc[p, fc * 16 + 2 * fc + (1 if p >= 64 else 0)] = 1.0
    add("efc", efc)
    eft = np.zeros((16, 8 * 128), np.float32)
    for fc in range(8):
        for p in range(128):
            eft[2 * fc + (1 if p >= 64 else 0), fc * 128 + p] = 1.0
    add("eft", eft)
    add("ones", np.ones((128, 1), np.float32))
    lo = np.zeros((128, 1), np.float32); lo[:64] = 1.0
    hi = np.zeros((128, 1), np.float32); hi[64:] = 1.0
    add("oneslh", np.concatenate([lo, hi], 1))
    e2 = np.zeros((2, 128), np.float32)
    e2[0, :64] = 1.0
    e2[1, 64:] = 1.0
    add("e2", e2)
    phii = np.zeros((64, 16 * 64), np.float32)
    for p in range(64):
        for j in range(16):
            phii[p, j * 64 + p] = 1.0
    add("phiinit", phii)
    cbf = np.concatenate(parts, 1).astype(HB)

    cf_cols = {}
    fparts = []

    def addf(name, arr):
        arr = np.asarray(arr, np.float32)
        a = np.zeros((128, arr.shape[1]), np.float32)
        a[: arr.shape[0]] = arr
        cf_cols[name] = sum(p.shape[1] for p in fparts)
        fparts.append(a)

    addf("identf", ident)
    addf("onesf", np.ones((128, 1), np.float32))
    addf("onesrow", np.ones((1, 128), np.float32))
    addf("eps_ln", np.full((128, 1), 1e-5, np.float32))
    addf("eps_a", np.full((128, 1), 1e-6, np.float32))
    addf("inv_hd", np.full((128, 1), 1.0 / 64.0, np.float32))
    cf = np.concatenate(fparts, 1).astype(np.float32)
    return cbf, cbf_cols, cf, cf_cols


CBF_HOST, CBF_COLS, CF_HOST, CF_COLS = _host_consts()



def build_nc(stop="full"):
    import contextlib
    nc = bass.Bass("TRN2", target_bir_lowering=False)
    D = {}
    D["xt"] = nc.dram_tensor("xt", [1024, TPC], F32, kind="ExternalInput").ap()
    D["w1t"] = nc.dram_tensor("w1t", [8 * 1024, 2048], F32, kind="ExternalInput").ap()
    D["w2t"] = nc.dram_tensor("w2t", [8 * 2048, 1024], F32, kind="ExternalInput").ap()
    for nm in ("wqt", "wkt", "wvt", "wot"):
        D[nm] = nc.dram_tensor(nm, [1024, 1024], F32, kind="ExternalInput").ap()
    D["wbt"] = nc.dram_tensor("wbt", [1024, 16], F32, kind="ExternalInput").ap()
    D["rwt"] = nc.dram_tensor("rwt", [1024, 8], F32, kind="ExternalInput").ap()
    D["cwt"] = nc.dram_tensor("cwt", [1024, 12], F32, kind="ExternalInput").ap()
    D["onw"] = nc.dram_tensor("onw", [128, 1], F32, kind="ExternalInput").ap()
    D["cbf"] = nc.dram_tensor("cbf", list(CBF_HOST.shape), BF16, kind="ExternalInput").ap()
    D["cf"] = nc.dram_tensor("cf", list(CF_HOST.shape), F32, kind="ExternalInput").ap()
    D["pcm"] = nc.dram_tensor("pcm", [128, 2], F32, kind="ExternalInput").ap()
    D["y"] = nc.dram_tensor("y", [1024, TPC], F32, kind="ExternalOutput").ap()
    if stop != "full":
        D["dbg"] = nc.dram_tensor("dbg", [128, NFC * TPC], F32, kind="ExternalOutput").ap()
    D["harc"] = nc.dram_tensor("harc", [128, 72], F32, kind="Internal").ap()
    D["haro"] = nc.dram_tensor("haro", [128, 72], F32, kind="Internal").ap()
    D["sarc"] = nc.dram_tensor("sarc", [64, 1024], F32, kind="Internal").ap()
    D["saro"] = nc.dram_tensor("saro", [64, 1024], F32, kind="Internal").ap()
    D["ot"] = nc.dram_tensor("ot", [TPC, 1024], BF16, kind="Internal").ap()
    D["dt"] = nc.dram_tensor("dt", [TPC, 1024], BF16, kind="Internal").ap()

    bk = BK(nc)
    PAIRS = [[0, 1], [2, 3], [4, 5], [6, 7]]
    ctx = contextlib.ExitStack()
    sb = lambda nm, sh, dt: ctx.enter_context(nc.sbuf_tensor(nm, sh, dt))
    pst = lambda nm, sh, dt: ctx.enter_context(nc.psum_tensor(nm, sh, dt))

    x_sb = sb("x_sb", [128, NFC * TPC], F32)
    xn = sb("xn", [128, NFC * TPC], BF16)
    slots = [sb(f"wsl{i}", [128, 4096], BF16) for i in range(5)]
    cbf = sb("cbfs", [128, CBF_HOST.shape[1]], BF16)
    cf = sb("cfs", [128, CF_HOST.shape[1]], F32)
    cw = sb("cw_s", [128, 96], F32)
    onw = sb("onw_s", [128, 1], F32)
    pcm = sb("pcm_s", [128, 2], F32)
    grow = sb("grow", [8, TPC], BF16)
    lnrow = sb("lnrow", [128, 512], F32)
    hsb_t = [sb(f"hsb{i}", [128, FTG], BF16) for i in range(2)]
    gbb_t = [sb(f"gbb{i}", [128, FTG], BF16) for i in range(2)]
    hgt_t = [sb(f"hgt{i}", [128, 8 * FTG], BF16) for i in range(2)]
    _xsq0 = sb("xsq0", [128, FTG], BF16)
    xsq_t = [_xsq0, _xsq0]

    PS1 = pst("PS1", [128, 2048], F32)
    PS2 = pst("PS2", [128, 1024], F32)
    PS3 = pst("PS3", [128, 1024], BF16)
    PS4 = pst("PS4", [128, 512], F32)

    B = {}
    for i in range(NFC):
        B[f"x{i}"] = Buf(f"x{i}")
        B[f"xn{i}"] = Buf(f"xn{i}")
    for i in range(5):
        B[f"sl{i}"] = Buf(f"sl{i}")
    for nm in ("cst", "grow", "lnrow", "onwb", "cwd",
               "harc", "haro", "sarc", "saro", "ot", "dt"):
        B[nm] = Buf(nm)
    for i in range(8):
        B[f"bk{i}"] = Buf(f"bk{i}")
    Bh = [Buf("h0"), Buf("h1")]
    Bgb = [Buf("gb0"), Buf("gb1")]
    Bhgt = [Buf("hgt0"), Buf("hgt1")]
    _bx = Buf("xsq0")
    Bxsq = [_bx, _bx]

    def bkb(*idx):
        return [B[f"bk{i}"] for i in idx]

    def ps1(c0, c1):
        return PS1[:, c0:c1]

    def CB(name, w=1, c0=0):
        c = CBF_COLS[name]
        return cbf[:, c + c0:c + c0 + w]

    def CBr(name, r0, r1, c0, w):
        c = CBF_COLS[name]
        return cbf[r0:r1, c + c0:c + c0 + w]

    def CF_(name, w=1):
        c = CF_COLS[name]
        return cf[:, c:c + w]

    def CFr(name, r0, r1, c0, w):
        c = CF_COLS[name]
        return cf[r0:r1, c + c0:c + c0 + w]

    for i in range(5):
        B[f"ini{i}"] = Buf(f"ini{i}")
    bk.op("sp", lambda sp: sp.dma_start(out=cbf[:, :], in_=D["cbf"]), writes=[B["ini0"]], inc=16)
    bk.op("sp", lambda sp: sp.dma_start(out=cf[:, :], in_=D["cf"]), writes=[B["ini1"]], inc=16)
    bk.op("sp", lambda sp: sp.dma_start(
        out=cw[:, :].rearrange("p (fc t) -> p fc t", fc=8),
        in_=D["cwt"].rearrange("(fc p) t -> p fc t", p=128)), writes=[B["ini2"]], inc=16)
    bk.op("sp", lambda sp: sp.dma_start(out=onw[:, :], in_=D["onw"]), writes=[B["ini3"]], inc=16)
    bk.op("sp", lambda sp: sp.dma_start(out=pcm[:, :], in_=D["pcm"]), writes=[B["ini4"]], inc=16)
    for i in range(NFC):
        bk.op("sp", lambda sp, i=i: sp.dma_start(
            out=x_sb[:, i * TPC:(i + 1) * TPC],
            in_=D["xt"][i * 128:(i + 1) * 128, :]),
            writes=[B[f"x{i}"]], inc=16)
    init_tok = bk.now("sq")
    for nm in ["cst", "onwb"] + [f"x{i}" for i in range(NFC)]:
        B[nm].w = init_tok

    ring = {"next": 0, "chunks": []}

    def ring_add(kind, arg):
        ring["chunks"].append((kind, arg))
        return len(ring["chunks"]) - 1

    def _load_chunk(ci):
        kind, arg = ring["chunks"][ci]
        sl = ci % 5
        slot = slots[sl]
        sem = bk.S(f"wr{sl}")
        if kind == "w1":
            e, q = arg
            src = D["w1t"][e * 1024:(e + 1) * 1024, q * 512:(q + 1) * 512] \
                .rearrange("(f p) hd -> p f hd", p=128)
            fn = lambda gs, s=slot, src=src: gs.dma_start(
                out=s[:, :].rearrange("p (f hd) -> p f hd", f=8), in_=src)
            n = 1
        elif kind == "w2":
            e, q = arg
            src = D["w2t"][e * 2048 + q * 512: e * 2048 + (q + 1) * 512, :] \
                .rearrange("(h p) fd -> p h fd", p=128)
            fn = lambda gs, s=slot, src=src: gs.dma_start(
                out=s[:, :].rearrange("p (h fd) -> p h fd", h=4), in_=src)
            n = 1
        elif kind == "sq":
            nm, hf = arg
            src = D[nm][:, hf * 512:(hf + 1) * 512] \
                .rearrange("(f p) od -> p f od", p=128)
            fn = lambda gs, s=slot, src=src: gs.dma_start(
                out=s[:, :].rearrange("p (f od) -> p f od", f=8), in_=src)
            n = 1
        elif kind == "miscb":
            fn = lambda gs, s=slot: gs.dma_start(
                out=s[:, 64:192].rearrange("p (f h) -> p f h", f=8),
                in_=D["wbt"].rearrange("(f p) h -> p f h", p=128))
            n = 1
        elif kind == "misc":
            bk.op("gp", lambda gs, s=slot: gs.dma_start(
                out=s[:, 0:64].rearrange("p (f e) -> p f e", f=8),
                in_=D["rwt"].rearrange("(f p) e -> p f e", p=128)),
                writes=[B[f"sl{sl}"]], sem=sem, inc=16)
            fn = lambda gs, s=slot: gs.dma_start(
                out=s[:, 64:192].rearrange("p (f h) -> p f h", f=8),
                in_=D["wbt"].rearrange("(f p) h -> p f h", p=128))
            n = 1
        else:
            raise KeyError(kind)
        bk.op("gp", fn, writes=[B[f"sl{sl}"]], sem=sem, inc=16 * n)
        if kind == "sq" and arg[0] == "wot":
            bk.op("dv", lambda ve, s=slot: ve.tensor_scalar(
                s[:, :], s[:, :], onw[:, 0:1], None, ALU.mult),
                reads=[B[f"sl{sl}"], B["onwb"]], writes=[B[f"sl{sl}"]])

    def ring_use(ci):
        while ring["next"] <= min(ci + 1, len(ring["chunks"]) - 1):
            _load_chunk(ring["next"])
            ring["next"] += 1
        return slots[ci % 5], B[f"sl{ci % 5}"]

    def emit_ln(t0, w):
        for fc in range(NFC):
            xa = x_sb[:, fc * TPC + t0: fc * TPC + t0 + w]
            mmg(bk, [(ps1(1024, 1024 + w)[0:1, :], CF_("onesf"), xa, fc == 0, fc == 7)],
                reads=[B[f"x{fc}"], B["cst"]], writes=bkb(2))
        for fc in range(NFC):
            xa = x_sb[:, fc * TPC + t0: fc * TPC + t0 + w]
            par = fc % 2
            bk.op("ac", lambda sc, o=xsq_t[par][:, 0:w], i=xa: sc.activation(o, i, AF.Square),
                  reads=[B[f"x{fc}"]], writes=[Bxsq[par]])
            mmg(bk, [(ps1(1536, 1536 + w)[0:1, :], CB("ones"), xsq_t[par][:, 0:w],
                      fc == 0, fc == 7)],
                reads=[Bxsq[par], B["cst"]], writes=bkb(3))
        bk.op("dv", lambda ve: ve.tensor_scalar(
            lnrow[0:1, 0:w], ps1(1024, 1024 + w)[0:1, :], 1.0 / 1024.0, None, ALU.mult),
            reads=bkb(2), writes=[B["lnrow"]])
        bk.op("dv", lambda ve: ve.tensor_mul(
            lnrow[0:1, 256:256 + w], lnrow[0:1, 0:w], lnrow[0:1, 0:w]),
            reads=[B["lnrow"]], writes=[B["lnrow"]])
        bk.op("dv", lambda ve: ve.scalar_tensor_tensor(
            lnrow[0:1, 256:256 + w], ps1(1536, 1536 + w)[0:1, :], 1.0 / 1024.0,
            lnrow[0:1, 256:256 + w], ALU.mult, ALU.subtract),
            reads=bkb(3) + [B["lnrow"]], writes=[B["lnrow"]])
        bk.op("ac", lambda sc: sc.activation(
            lnrow[0:1, 256:256 + w], lnrow[0:1, 256:256 + w], AF.Sqrt,
            bias=CFr("eps_ln", 0, 1, 0, 1)),
            reads=[B["lnrow"], B["cst"]], writes=[B["lnrow"]])
        bk.op("dv", lambda ve: ve.reciprocal(lnrow[0:1, 256:256 + w], lnrow[0:1, 256:256 + w]),
              reads=[B["lnrow"]], writes=[B["lnrow"]])
        mmg(bk, [(ps1(0, w), CFr("onesrow", 0, 1, 0, 128), lnrow[0:1, 0:w], True, True)],
            reads=[B["lnrow"], B["cst"]], writes=bkb(0))
        mmg(bk, [(ps1(512, 512 + w), CFr("onesrow", 0, 1, 0, 128), lnrow[0:1, 256:256 + w],
                  True, True)],
            reads=[B["lnrow"], B["cst"]], writes=bkb(1))
        for fc in range(NFC):
            par = fc % 2
            xa = x_sb[:, fc * TPC + t0: fc * TPC + t0 + w]
            xo = xn[:, fc * TPC + t0: fc * TPC + t0 + w]
            bk.op("dv", lambda ve, xa=xa, par=par: ve.tensor_sub(
                xsq_t[par][:, 0:w], xa, ps1(0, w)),
                reads=[B[f"x{fc}"]] + bkb(0), writes=[Bxsq[par]])
            bk.op("dv", lambda ve, xo=xo, par=par: ve.tensor_mul(
                xo, xsq_t[par][:, 0:w], ps1(512, 512 + w)),
                reads=[Bxsq[par]] + bkb(1), writes=[B[f"xn{fc}"]])

    def emit_gates(misc_ci):
        slot, slb = ring_use(misc_ci)
        for ci in range(16):
            t0 = ci * 128
            mms = []
            for fc in range(NFC):
                mms.append((PS4[:, 0:8],
                            xn[:, fc * TPC + t0: fc * TPC + t0 + 128],
                            slot[:, fc * 8: fc * 8 + 8], fc == 0, fc == 7))
            mmg(bk, mms, reads=[B[f"xn{i}"] for i in range(NFC)] + [slb], writes=bkb(7))
            bk.op("dv", lambda ve: ve.tensor_copy(lgt[:, 0:8], PS4[:, 0:8]),
                  reads=bkb(7), writes=[B["lgt"]])

            def st(fn):
                bk.op("dv", fn, reads=[B["lgt"]], writes=[B["lgt"]])
            st(lambda ve: ve.reduce_max(lgt[:, 8:9], lgt[:, 0:8], axis=AX))
            st(lambda ve: ve.tensor_scalar(lgt[:, 16:24], lgt[:, 0:8], lgt[:, 8:9],
                                           None, ALU.is_equal))
            st(lambda ve: ve.scalar_tensor_tensor(lgt[:, 24:32], lgt[:, 16:24], -1e30,
                                                  lgt[:, 0:8], ALU.mult, ALU.add))
            st(lambda ve: ve.reduce_max(lgt[:, 9:10], lgt[:, 24:32], axis=AX))
            st(lambda ve: ve.tensor_sub(lgt[:, 10:11], lgt[:, 9:10], lgt[:, 8:9]))
            bk.op("ac", lambda sc: sc.activation(lgt[:, 11:12], lgt[:, 10:11], AF.Exp),
                  reads=[B["lgt"]], writes=[B["lgt"]])
            st(lambda ve: ve.tensor_scalar(lgt[:, 12:13], lgt[:, 11:12], 1.0, None, ALU.add))
            st(lambda ve: ve.reciprocal(lgt[:, 13:14], lgt[:, 12:13]))
            st(lambda ve: ve.tensor_scalar(lgt[:, 14:15], lgt[:, 13:14], 0.5, None, ALU.mult))
            st(lambda ve: ve.tensor_scalar(lgt[:, 15:16], lgt[:, 14:15], -1.0, 0.5,
                                           ALU.mult, ALU.add))
            st(lambda ve: ve.tensor_scalar(lgt[:, 32:40], lgt[:, 24:32], lgt[:, 9:10],
                                           None, ALU.is_equal))
            st(lambda ve: ve.tensor_scalar(lgt[:, 40:48], lgt[:, 16:24], lgt[:, 14:15],
                                           None, ALU.mult))
            st(lambda ve: ve.scalar_tensor_tensor(lgt[:, 40:48], lgt[:, 32:40],
                                                  lgt[:, 15:16], lgt[:, 40:48],
                                                  ALU.mult, ALU.add))
            trg(bk, [(PS4[0:8, 256:384], lgt[:, 40:48], CF_("identf", 128))],
                reads=[B["lgt"], B["cst"]], writes=bkb(7))
            bk.op("ac", lambda sc, t0=t0: sc.activation(
                grow[0:8, t0:t0 + 128], PS4[0:8, 256:384], AF.Copy),
                reads=bkb(7), writes=[B["grow"]])

    def emit_ffn():
        misc_ci = ring_add("misc", None)
        for tg in range(NFT):
            emit_ln(tg * FTG, FTG)
        emit_gates(misc_ci)
        ch = {}
        for e in range(NE):
            for q in range(4):
                ch[(e, "w1", q)] = ring_add("w1", (e, q))
                ch[(e, "w2", q)] = ring_add("w2", (e, q))
        xn_reads = [B[f"xn{i}"] for i in range(NFC)]
        for e in range(NE):
            for hcg in range(2):
                for tg in range(NFT):
                    gpar = tg % 2
                    mmg(bk, [(PS4[:, 0:FTG], CBr("e8sel", 0, 8, e * 128, 128),
                              grow[0:8, tg * FTG:(tg + 1) * FTG], True, True)],
                        reads=[B["grow"], B["cst"]], writes=bkb(7))
                    bk.op("ac", lambda sc, gpar=gpar: sc.activation(
                        gbb_t[gpar][:, :], PS4[:, 0:FTG], AF.Copy),
                        reads=bkb(7), writes=[Bgb[gpar]])
                    hpar = tg % 2
                    for i in range(8):
                        hc = hcg * 8 + i
                        slot, slb = ring_use(ch[(e, "w1", hc // 4)])
                        wpar = i % 2
                        c0 = (hc % 4) * 128
                        mms = []
                        for fc in range(NFC):
                            mms.append((PS2[:, wpar * 512: wpar * 512 + FTG],
                                        slot[:, fc * 512 + c0: fc * 512 + c0 + 128],
                                        xn[:, fc * TPC + tg * FTG: fc * TPC + (tg + 1) * FTG],
                                        fc == 0, fc == 7))
                        mmg(bk, mms, reads=[slb] + xn_reads, writes=bkb(4 + wpar))
                        bk.op("ac", lambda sc, wpar=wpar: sc.activation(
                            hsb_t[wpar][:, :], PS2[:, wpar * 512: wpar * 512 + FTG], AF.Gelu),
                            reads=bkb(4 + wpar), writes=[Bh[wpar]])
                        bk.op("gp", lambda gs, wpar=wpar, i=i, hpar=hpar, gpar=gpar:
                              gs.tensor_mul(hgt_t[hpar][:, i * FTG:(i + 1) * FTG],
                                            hsb_t[wpar][:, :], gbb_t[gpar][:, :]),
                              reads=[Bh[wpar], Bgb[gpar]], writes=[Bhgt[hpar]])
                    for s in range(4):
                        bp = (s % 2) * 2
                        slota, slba = ring_use(ch[(e, "w2", hcg * 2)])
                        slotb, slbb = ring_use(ch[(e, "w2", hcg * 2 + 1)])
                        mms = []
                        for i in range(8):
                            hc = hcg * 8 + i
                            slot = slota if i < 4 else slotb
                            c0 = (hc % 4) * 1024
                            for j in range(2):
                                fc = s * 2 + j
                                mms.append((ps1((bp + j) * 512, (bp + j) * 512 + FTG),
                                            slot[:, c0 + fc * 128: c0 + fc * 128 + 128],
                                            hgt_t[hpar][:, i * FTG:(i + 1) * FTG],
                                            i == 0, i == 7))
                        mmg(bk, mms, reads=[slba, slbb, Bhgt[hpar]], writes=bkb(bp, bp + 1))

                        def fev(ve, s=s, bp=bp, tg=tg):
                            xo = x_sb[:, :].rearrange("p (fc t) -> p fc t", fc=8)[
                                :, s * 2: s * 2 + 2, tg * FTG:(tg + 1) * FTG]
                            pin = ps1(bp * 512, (bp + 2) * 512).rearrange(
                                "p (j t) -> p j t", j=2)[:, :, 0:FTG]
                            return ve.tensor_add(xo, xo, pin)
                        bk.op("dv", fev,
                              reads=bkb(bp, bp + 1) + [B[f"x{s * 2}"], B[f"x{s * 2 + 1}"]],
                              writes=[B[f"x{s * 2}"], B[f"x{s * 2 + 1}"]])

    qg = sb("qg", [128, 8 * 259], BF16)
    kg = sb("kg", [128, 8 * 259], BF16)
    vg = sb("vg", [128, 8 * 259], BF16)
    XK = sb("XK", [128, 2048], BF16)
    Xs = sb("Xs", [128, 2048], BF16)
    Pp = sb("Pp", [128, 2048], BF16)
    Pt = sb("Pt", [128, 2048], BF16)

    Ssb = sb("Ssb", [64, 1024], F32)
    Phi = sb("Phi", [64, 1024], F32)
    qnc = sb("qnc", [64, 2048], BF16)
    brows = sb("brows", [128, 512], F32)
    brb = sb("brb", [128, 768], BF16)
    contrib = nc.alloc_sbuf_tensor_at("contribov", [128, 72], F32,
                                      offset=nc.lookup_mloc(hsb_t[0]).addr)
    halo = nc.alloc_sbuf_tensor_at("haloov", [128, 72], F32,
                                   offset=nc.lookup_mloc(hsb_t[1]).addr)
    carry = nc.alloc_sbuf_tensor_at("carryov", [128, 72], BF16,
                                    offset=nc.lookup_mloc(gbb_t[0]).addr)
    sqtm = nc.alloc_sbuf_tensor_at("sqtmov", [128, 16], F32,
                                   offset=nc.lookup_mloc(gbb_t[1]).addr)
    a_h0 = nc.lookup_mloc(hgt_t[0]).addr
    a_h1 = nc.lookup_mloc(hgt_t[1]).addr
    assert a_h1 == a_h0 + 4096, (a_h0, a_h1)
    QQ = nc.alloc_sbuf_tensor_at("QQov", [64, 2048], F32, offset=a_h0)
    a_pp = nc.lookup_mloc(Pp).addr
    a_pt = nc.lookup_mloc(Pt).addr
    a_xs = nc.lookup_mloc(Xs).addr
    kbc = nc.alloc_sbuf_tensor_at("kbcov", [64, 2048], BF16, offset=a_h0)
    Sin = nc.alloc_sbuf_tensor_at("Sinov", [64, 1024], BF16, offset=a_pp)
    cwd = nc.alloc_sbuf_tensor_at("cwdov", [128, 512], BF16, offset=a_pp)
    lgt = nc.alloc_sbuf_tensor_at("lgtov", [128, 48], F32, offset=a_pp + 2048)
    Mb = nc.alloc_sbuf_tensor_at("Mbov", [128, 2048], BF16, offset=a_pt)
    HTn = nc.alloc_sbuf_tensor_at("HTnov", [64, 1024], F32, offset=a_pt)
    qtm = nc.alloc_sbuf_tensor_at("qtmov", [128, 1024], BF16, offset=a_xs)
    aq = nc.lookup_mloc(qg).addr
    ak = nc.lookup_mloc(kg).addr
    av = nc.lookup_mloc(vg).addr
    ag = nc.lookup_mloc(grow).addr
    osb = nc.alloc_sbuf_tensor_at("osb", [128, 1024], BF16, offset=ag)
    dsb = nc.alloc_sbuf_tensor_at("dsb", [128, 1024], BF16, offset=ag + 2048)
    dbl = nc.alloc_sbuf_tensor_at("dbl", [64, 2048], BF16, offset=aq)
    db2 = nc.alloc_sbuf_tensor_at("db2", [64, 2048], BF16, offset=ak)
    obl = nc.alloc_sbuf_tensor_at("obl", [128, 1024], BF16, offset=av)
    ob2 = nc.alloc_sbuf_tensor_at("ob2", [128, 1024], BF16, offset=av + 2048)
    onb = nc.alloc_sbuf_tensor_at("onb", [128, 1024], BF16, offset=a_xs)

    for nm in ("qg", "kg", "vg", "XK", "Xs", "Pp", "Pt", "QQb",
               "S", "Phi", "Sin", "qnc",
               "brb", "brows", "osb", "obl", "ob2", "dsb", "dbl", "db2", "onb",
               "onr"):
        B[nm] = Buf(nm)
    B["lgt"] = B["Pp"]
    B["contribb"] = Bh[0]
    B["halo"] = Bh[1]
    B["carry"] = Bgb[0]
    B["sqtmb"] = Bgb[1]

    def hp(h):
        return (h % 2) * 64

    TGB = {"q": qg, "k": kg, "v": vg}
    TBB = {"q": B["qg"], "k": B["kg"], "v": B["vg"]}
    WNM = {"q": "wqt", "k": "wkt", "v": "wvt"}

    def emit_chunk(g, ci2, sub=99):
        c = g * 2 + ci2
        base = 3 + ci2 * CH
        cb0 = ci2 * CH
        trg(bk, [(PS3[0:128, 0:16], brb[0:16, 512 + cb0:512 + cb0 + CH],
                  CBr("ident", 0, 16, 0, 16))],
            reads=[B["brb"], B["cst"]], writes=bkb(6))
        bk.op("ac", lambda sc: sc.activation(sqtm[:, 0:16], PS3[0:128, 0:16], AF.Copy),
              reads=bkb(6), writes=[B["sqtmb"]])
        trs = [(PS3[:, fc * 128:(fc + 1) * 128],
                kg[:, fc * 259 + base: fc * 259 + base + CH], CB("ident", 128))
               for fc in range(NFC)]
        trg(bk, trs, reads=[B["kg"], B["cst"]], writes=bkb(6))

        def ft1(ve):
            inst = None
            for h in range(NH):
                inst = ve.tensor_scalar(
                    XK[:, h * 128 + 64:(h + 1) * 128],
                    PS3[:, (h // 2) * 128 + (h % 2) * 64:(h // 2) * 128 + (h % 2) * 64 + 64],
                    sqtm[:, h:h + 1], None, ALU.mult)
            return inst
        bk.op("dv", ft1, reads=bkb(6) + [B["sqtmb"]], writes=[B["XK"]])
        trs = [(PS3[:, fc * 128:(fc + 1) * 128],
                vg[:, fc * 259 + base: fc * 259 + base + CH], CB("ident", 128))
               for fc in range(NFC)]
        trg(bk, trs, reads=[B["vg"], B["cst"]], writes=bkb(6))

        def ft2(ve):
            inst = None
            for h in range(NH):
                inst = ve.tensor_scalar(
                    XK[:, h * 128:h * 128 + 64],
                    PS3[:, (h // 2) * 128 + (h % 2) * 64:(h // 2) * 128 + (h % 2) * 64 + 64],
                    sqtm[:, h:h + 1], None, ALU.mult)
            return inst
        bk.op("dv", ft2, reads=bkb(6) + [B["sqtmb"]], writes=[B["XK"]])
        for r in range(2):
            trs = [(PS3[0:64, i * 128:(i + 1) * 128],
                    XK[:, (r * 8 + i) * 128 + 64:(r * 8 + i) * 128 + 128],
                    CB("ident", 128)) for i in range(8)]
            trg(bk, trs, reads=[B["XK"], B["cst"]], writes=bkb(6))
            bk.op("ac", lambda sc, r=r: sc.activation(
                kbc[0:64, r * 1024:(r + 1) * 1024], PS3[0:64, 0:1024], AF.Copy),
                reads=bkb(6), writes=[B["QQb"]])
        trs = [(PS3[:, fc * 128:(fc + 1) * 128],
                qg[:, fc * 259 + base: fc * 259 + base + CH], CB("ident", 128))
               for fc in range(NFC)]
        trg(bk, trs, reads=[B["qg"], B["cst"]], writes=bkb(6))
        bk.op("dv", lambda ve: ve.tensor_copy(qtm[:, :], PS3[:, 0:1024]),
              reads=bkb(6), writes=[B["Xs"]])
        for r in range(2):
            trs = [(PS3[0:64, i * 128:(i + 1) * 128],
                    qtm[:, (r * 8 + i) * 64:(r * 8 + i) * 64 + 64],
                    CB("ident", 128)) for i in range(8)]
            trg(bk, trs, reads=[B["Xs"], B["cst"]], writes=bkb(6))
            bk.op("ac", lambda sc, r=r: sc.activation(
                qnc[0:64, r * 1024:(r + 1) * 1024], PS3[0:64, 0:1024], AF.Copy),
                reads=bkb(6), writes=[B["qnc"]])
        if sub <= 1:
            return
        mms = []
        for h in range(NH):
            kb_h = kbc[0:64, h * 128:(h + 1) * 128]
            mms.append((ps1(h * 128, (h + 1) * 128), kb_h, kb_h, True, True))
        mmg(bk, mms, reads=[B["QQb"]], writes=bkb(0, 1, 2, 3))
        for j in range(8):
            bk.op("dv", lambda ve, j=j: ve.tensor_mul(
                Pp[:, j * 256:(j + 1) * 256], ps1(j * 256, (j + 1) * 256),
                CB("mask_nl", 256)),
                reads=bkb(j // 2) + [B["cst"], B["Pp"]], writes=[B["Pp"]])
            bk.op("dv", lambda ve, j=j: ve.tensor_mul(
                Pt[:, j * 256:(j + 1) * 256], ps1(j * 256, (j + 1) * 256),
                CB("mask_nu", 256)),
                reads=bkb(j // 2) + [B["Pt"]], writes=[B["Pt"]])
        if sub <= 2:
            return

        def apply_level(src_t, src_b):
            mms = []
            for h in range(NH):
                mms.append((ps1(h * 128, (h + 1) * 128), CB("ident", 128),
                            src_t[:, h * 128:(h + 1) * 128], True, False))
                mms.append((ps1(h * 128, (h + 1) * 128), Pt[:, h * 128:(h + 1) * 128],
                            src_t[:, h * 128:(h + 1) * 128], False, True))
            mmg(bk, mms, reads=[src_b, B["Pt"], B["cst"]], writes=bkb(0, 1, 2, 3))
            bk.op("ac", lambda sc: sc.activation(Xs[:, :], ps1(0, 2048), AF.Copy),
                  reads=bkb(0, 1, 2, 3), writes=[B["Xs"]])
        apply_level(XK, B["XK"])
        for lev in range(1, 7):
            for half in range(2):
                hs = list(range(half * 8, half * 8 + 8))
                mms = []
                for h in hs:
                    mms.append((ps1((h % 8) * 128, (h % 8) * 128 + 128),
                                Pt[:, h * 128:(h + 1) * 128],
                                Pp[:, h * 128:(h + 1) * 128], True, True))
                if lev < 6:
                    for h in hs:
                        mms.append((ps1(1024 + (h % 8) * 128, 1024 + (h % 8) * 128 + 128),
                                    Pp[:, h * 128:(h + 1) * 128],
                                    Pt[:, h * 128:(h + 1) * 128], True, True))
                mmg(bk, mms, reads=[B["Pp"], B["Pt"]], writes=bkb(0, 1, 2, 3))
                bk.op("ac", lambda sc, half=half: sc.activation(
                    Pp[:, half * 1024:(half + 1) * 1024], ps1(0, 1024), AF.Copy),
                    reads=bkb(0, 1) + [B["Pp"]], writes=[B["Pp"]])
                if lev < 6:
                    bk.op("ac", lambda sc, half=half: sc.activation(
                        Pt[:, half * 1024:(half + 1) * 1024], ps1(1024, 2048), AF.Copy),
                        reads=bkb(2, 3) + [B["Pt"]], writes=[B["Pt"]])
            if lev == 6:
                for half in range(2):
                    hs = list(range(half * 8, half * 8 + 8))
                    trs = [(PS3[:, (h % 8) * 128:((h % 8) + 1) * 128],
                            Pp[:, h * 128:(h + 1) * 128], CB("ident", 128))
                           for h in hs]
                    trg(bk, trs, reads=[B["Pp"], B["cst"]], writes=bkb(6))
                    bk.op("ac", lambda sc, half=half: sc.activation(
                        Pt[:, half * 1024:(half + 1) * 1024], PS3[:, 0:1024], AF.Copy),
                        reads=bkb(6) + [B["Pt"]], writes=[B["Pt"]])
            apply_level(Xs, B["Xs"])
        if sub <= 3:
            return
        mms = []
        for h in range(NH):
            mms.append((ps1(h * 128, (h + 1) * 128), kbc[0:64, h * 128:(h + 1) * 128],
                        qnc[0:64, h * 128:(h + 1) * 128], True, True))
        mmg(bk, mms, reads=[B["QQb"], B["qnc"]], writes=bkb(0, 1, 2, 3))
        for j in range(8):
            bk.op("dv", lambda ve, j=j: ve.tensor_mul(
                Mb[:, j * 256:(j + 1) * 256], ps1(j * 256, (j + 1) * 256),
                CB("mask_ui", 256)),
                reads=bkb(j // 2) + [B["cst"], B["Pt"]], writes=[B["Pt"]])
        if sub <= 4:
            return
        mms = []
        for h in range(NH):
            mms.append((PS2[hp(h):hp(h) + 64, (h // 2) * 128:(h // 2) * 128 + 128],
                        Xs[:, h * 128:h * 128 + 64], Mb[:, h * 128:(h + 1) * 128],
                        True, True))
        mmg(bk, mms, reads=[B["Xs"], B["Pt"]], writes=bkb(4, 5))
        mms = []
        for h in range(NH):
            mms.append((ps1(h * 128, (h + 1) * 128)[0:64, :],
                        Xs[:, h * 128 + 64:(h + 1) * 128], Mb[:, h * 128:(h + 1) * 128],
                        True, True))
        mmg(bk, mms, reads=[B["Xs"], B["Pt"]], writes=bkb(0, 1, 2, 3))
        bk.op("dv", lambda ve: ve.tensor_sub(QQ[0:64, :], qnc[0:64, :],
                                             ps1(0, 2048)[0:64, :]),
              reads=[B["qnc"]] + bkb(0, 1, 2, 3), writes=[B["QQb"]])
        if sub <= 5:
            return
        mms = []
        for h in range(NH):
            mms.append((ps1(h * 64, (h + 1) * 64)[0:64, :],
                        Xs[:, h * 128 + 64:(h + 1) * 128], XK[:, h * 128 + 64:(h + 1) * 128],
                        True, True))
        mmg(bk, mms, reads=[B["Xs"], B["XK"]], writes=bkb(0, 1))
        bk.op("dv", lambda ve: ve.tensor_scalar(
            HTn[0:64, :], ps1(0, 1024)[0:64, :], -1.0, None, ALU.mult),
            reads=bkb(0, 1) + [B["Pt"]], writes=[B["Pt"]])
        mms = []
        for h in range(NH):
            mms.append((ps1(1024 + (h // 2) * 128, 1024 + (h // 2) * 128 + 128)[hp(h):hp(h) + 64, :],
                        Ssb[0:64, h * 64:(h + 1) * 64],
                        QQ[0:64, h * 128:(h + 1) * 128],
                        True, True))
        mmg(bk, mms, reads=[B["S"], B["QQb"]], writes=bkb(2, 3))
        bk.op("ac", lambda sc: sc.activation(
            osb[:, :], PS2[:, 0:1024], AF.Copy),
            reads=bkb(4, 5), writes=[B["osb"]])
        bk.op("dv", lambda ve: ve.tensor_add(
            osb[:, :], osb[:, :], ps1(1024, 2048)),
            reads=bkb(2, 3) + [B["osb"]], writes=[B["osb"]])
        bk.op("sp", lambda sp, c=c: sp.dma_start(
            out=D["ot"][c * 128:(c + 1) * 128, :], in_=osb[:, :]),
            reads=[B["osb"]], writes=[B["ot"]], sem=bk.S("svo"), inc=16)
        if sub <= 6:
            return
        mms = []
        for h in range(NH):
            mms.append((ps1((h // 2) * 128, (h // 2) * 128 + 128)[hp(h):hp(h) + 64, :],
                        Phi[0:64, h * 64:(h + 1) * 64],
                        QQ[0:64, h * 128:(h + 1) * 128],
                        True, True))
        mmg(bk, mms, reads=[B["Phi"], B["QQb"]], writes=bkb(0, 1))
        bk.op("dv", lambda ve: ve.tensor_copy(dsb[:, :], ps1(0, 1024)),
              reads=bkb(0, 1), writes=[B["dsb"]])
        bk.op("sp", lambda sp, c=c: sp.dma_start(
            out=D["dt"][c * 128:(c + 1) * 128, :], in_=dsb[:, :]),
            reads=[B["dsb"]], writes=[B["dt"]], sem=bk.S("svd"), inc=16)
        mms = []
        for h in range(NH):
            mms.append((ps1(1024 + h * 64, 1024 + (h + 1) * 64)[0:64, :],
                        XK[:, h * 128 + 64:(h + 1) * 128], Xs[:, h * 128:h * 128 + 64],
                        True, True))
        mmg(bk, mms, reads=[B["XK"], B["Xs"]], writes=bkb(2, 3))
        mms = []
        for h in range(NH):
            mms.append((ps1(h * 64, (h + 1) * 64)[0:64, :],
                        HTn[0:64, h * 64:(h + 1) * 64],
                        Ssb[0:64, h * 64:(h + 1) * 64],
                        True, True))
        mmg(bk, mms, reads=[B["Pt"], B["S"]], writes=bkb(0, 1))
        mms = []
        for h in range(NH):
            mms.append((PS2[0:64, h * 64:(h + 1) * 64],
                        HTn[0:64, h * 64:(h + 1) * 64],
                        Phi[0:64, h * 64:(h + 1) * 64],
                        True, True))
        mmg(bk, mms, reads=[B["Pt"], B["Phi"]], writes=bkb(4, 5))
        bk.op("dv", lambda ve: ve.tensor_add(Ssb[0:64, :], Ssb[0:64, :],
                                             ps1(1024, 2048)[0:64, :]),
              reads=bkb(2, 3) + [B["S"]], writes=[B["S"]])
        bk.op("dv", lambda ve: ve.tensor_add(Ssb[0:64, :], Ssb[0:64, :],
                                             ps1(0, 1024)[0:64, :]),
              reads=bkb(0, 1) + [B["S"]], writes=[B["S"]])
        bk.op("dv", lambda ve: ve.tensor_add(Phi[0:64, :], Phi[0:64, :],
                                             PS2[0:64, 0:1024]),
              reads=bkb(4, 5) + [B["Phi"]], writes=[B["Phi"]])

    def emit_attention(stop="full"):
        xnr = [B[f"xn{i}"] for i in range(NFC)]
        bk.op("dv", lambda ve: ve.engine_nop(),
              reads=[Bhgt[0], Bhgt[1], B["grow"]],
              writes=[B["QQb"], B["osb"], B["dsb"]])
        bk.op("dv", lambda ve: ve.memset(Ssb[:, :], 0.0), writes=[B["S"]])
        bk.op("dv", lambda ve: ve.tensor_copy(
            Phi[0:64, :], CBr("phiinit", 0, 64, 0, 1024)), reads=[B["cst"]], writes=[B["Phi"]])
        mini = {t: [ring_add("sq", (WNM[t], hf)) for hf in range(2)] for t in "qkv"}
        emit_ln(TPC - 3, 3)
        for ti, t in enumerate("qkv"):
            for dc in range(8):
                slot, slb = ring_use(mini[t][dc // 4])
                mms = []
                for fc in range(NFC):
                    mms.append((PS4[:, 0:3],
                                slot[:, fc * 512 + (dc % 4) * 128: fc * 512 + (dc % 4) * 128 + 128],
                                xn[:, fc * TPC + TPC - 3: fc * TPC + TPC],
                                fc == 0, fc == 7))
                mmg(bk, mms, reads=[slb] + xnr, writes=bkb(7))
                bk.op("ac", lambda sc, ti=ti, dc=dc: sc.activation(
                    contrib[:, ti * 24 + dc * 3: ti * 24 + dc * 3 + 3], PS4[:, 0:3],
                    AF.Copy), reads=bkb(7), writes=[B["contribb"]])
        bk.op("dv", lambda ve: ve.tensor_scalar(
            contrib[:, :], contrib[:, :], pcm[:, 0:1], None, ALU.mult),
            reads=[B["contribb"], B["cst"]], writes=[B["contribb"]])
        bk.op("gp", lambda gs: gs.dma_start(out=D["harc"], in_=contrib[:, :]),
              reads=[B["contribb"]], writes=[B["harc"]], sem=bk.S("gq"), inc=16)
        bk.op("gp", lambda gs: gs.collective_compute(
            "AllReduce", ALU.add, replica_groups=PAIRS,
            ins=[D["harc"].opt()], outs=[D["haro"].opt()]),
            reads=[B["harc"]], writes=[B["haro"]], sem=bk.S("cc"), inc=1)
        bk.op("gp", lambda gs: gs.dma_start(out=halo[:, :], in_=D["haro"]),
              reads=[B["haro"]], writes=[B["halo"]], sem="gq", inc=16)
        bk.op("dv", lambda ve: ve.tensor_scalar(
            halo[:, :], halo[:, :], pcm[:, 1:2], None, ALU.mult),
            reads=[B["halo"], B["cst"]], writes=[B["halo"]])
        if stop == "halo":
            return
        ngrp_emit = 1 if (stop == "grp0" or str(stop).startswith("ch")) else NGRP
        for g in range(ngrp_emit):
            t0 = g * GT
            emit_ln(t0, GT)
            gch = {t: [ring_add("sq", (WNM[t], hf)) for hf in range(2)] for t in "qkv"}
            mb_ci = ring_add("miscb", None)
            for ti, t in enumerate("qkv"):
                srcb = halo if g == 0 else carry
                srcbuf = B["halo"] if g == 0 else B["carry"]

                def fci(ve, ti=ti, t=t, srcb=srcb):
                    return ve.tensor_copy(
                        TGB[t][:, :].rearrange("p (f c) -> p f c", f=8)[:, :, 0:3],
                        srcb[:, ti * 24:(ti + 1) * 24].rearrange("p (f c) -> p f c", f=8))
                bk.op("dv", fci, reads=[srcbuf], writes=[TBB[t]])
            for t in "qkv":
                for dc in range(8):
                    slot, slb = ring_use(gch[t][dc // 4])
                    pb = dc % 2
                    mms = []
                    for fc in range(NFC):
                        mms.append((ps1(pb * 512, pb * 512 + GT),
                                    slot[:, fc * 512 + (dc % 4) * 128: fc * 512 + (dc % 4) * 128 + 128],
                                    xn[:, fc * TPC + t0: fc * TPC + t0 + GT],
                                    fc == 0, fc == 7))
                    mmg(bk, mms, reads=[slb] + xnr, writes=bkb(pb))
                    bk.op("ac", lambda sc, t=t, dc=dc, pb=pb: sc.activation(
                        TGB[t][:, dc * 259 + 3: dc * 259 + 3 + GT],
                        ps1(pb * 512, pb * 512 + GT), AF.Copy),
                        reads=bkb(pb), writes=[TBB[t]])
            slot, slb = ring_use(mb_ci)
            mms = []
            for fc in range(NFC):
                mms.append((PS4[0:16, 0:GT], slot[:, 64 + fc * 16: 64 + fc * 16 + 16],
                            xn[:, fc * TPC + t0: fc * TPC + t0 + GT], fc == 0, fc == 7))
            mmg(bk, mms, reads=[slb] + xnr, writes=bkb(7))
            bk.op("ac", lambda sc: sc.activation(brows[0:16, 0:GT], PS4[0:16, 0:GT],
                                                 AF.Sigmoid),
                  reads=bkb(7), writes=[B["brows"]])
            bk.op("ac", lambda sc: sc.activation(brows[0:16, 0:GT],
                                                 brows[0:16, 0:GT], AF.Sqrt),
                  reads=[B["brows"]], writes=[B["brows"]])
            for ti, t in enumerate("qkv"):
                for fc in range(NFC):
                    for k4 in range(4):
                        bk.op("dv", lambda ve, ti=ti, fc=fc, k4=k4: ve.tensor_scalar(
                            cwd[:, k4 * 128:(k4 + 1) * 128], CB("ident", 128),
                            cw[:, fc * 12 + ti * 4 + k4: fc * 12 + ti * 4 + k4 + 1],
                            None, ALU.mult),
                            reads=[B["cst"]], writes=[B["Pp"]])
                    mms = [(ps1(1024, 1024 + GT), cwd[:, k4 * 128:(k4 + 1) * 128],
                            TGB[t][:, fc * 259 + k4: fc * 259 + k4 + GT],
                            k4 == 0, k4 == 3) for k4 in range(4)]
                    mmg(bk, mms, reads=[B["Pp"], TBB[t]], writes=bkb(2))
                    bk.op("dv", lambda ve, ti=ti, t=t, fc=fc: ve.tensor_copy(
                        carry[:, ti * 24 + fc * 3: ti * 24 + fc * 3 + 3],
                        TGB[t][:, fc * 259 + GT: fc * 259 + GT + 3]),
                        reads=[TBB[t]], writes=[B["carry"]])
                    bk.op("ac", lambda sc, t=t, fc=fc: sc.activation(
                        TGB[t][:, fc * 259 + 3: fc * 259 + 3 + GT], ps1(1024, 1024 + GT),
                        AF.Silu), reads=bkb(2), writes=[TBB[t]])
            bk.op("ac", lambda sc: sc.activation(
                brb[0:16, 512:512 + GT], brows[0:16, 0:GT], AF.Copy),
                reads=[B["brows"]], writes=[B["brb"]])
            for t, rc in (("q", 256), ("k", 0)):
                for fc in range(NFC):
                    par = fc % 2
                    bk.op("ac", lambda sc, t=t, fc=fc, par=par: sc.activation(
                        xsq_t[par][:, 0:GT], TGB[t][:, fc * 259 + 3: fc * 259 + 3 + GT],
                        AF.Square), reads=[TBB[t]], writes=[Bxsq[par]])
                    mmg(bk, [(PS4[0:16, 256:256 + GT], CB("efc", 16, fc * 16),
                              xsq_t[par][:, 0:GT], fc == 0, fc == 7)],
                        reads=[Bxsq[par], B["cst"]], writes=bkb(7))
                bk.op("ac", lambda sc, rc=rc: sc.activation(
                    brows[0:16, rc:rc + GT], PS4[0:16, 256:256 + GT], AF.Sqrt),
                    reads=bkb(7), writes=[B["brows"]])
                bk.op("dv", lambda ve, rc=rc: ve.tensor_scalar(
                    brows[0:16, rc:rc + GT], brows[0:16, rc:rc + GT], 1e-6, None,
                    ALU.max), reads=[B["brows"]], writes=[B["brows"]])
                bk.op("dv", lambda ve, rc=rc: ve.reciprocal(
                    brows[0:16, rc:rc + GT], brows[0:16, rc:rc + GT]),
                    reads=[B["brows"]], writes=[B["brows"]])
            bk.op("dv", lambda ve: ve.tensor_scalar(
                brows[0:16, 256:256 + GT], brows[0:16, 256:256 + GT], 0.125, None,
                ALU.mult), reads=[B["brows"]], writes=[B["brows"]])
            for s_c, d_c in ((256, 0), (0, 256)):
                bk.op("ac", lambda sc, s=s_c, d=d_c: sc.activation(
                    brb[0:16, d:d + GT], brows[0:16, s:s + GT], AF.Copy),
                    reads=[B["brows"]], writes=[B["brb"]])
            for t, rc in (("q", 0), ("k", 256)):
                for fc in range(NFC):
                    pb = fc % 2
                    mmg(bk, [(ps1(pb * 512, pb * 512 + GT),
                              CBr("eft", 0, 16, fc * 128, 128),
                              brb[0:16, rc:rc + GT], True, True)],
                        reads=[B["brb"], B["cst"]], writes=bkb(pb))
                    bk.op("dv", lambda ve, t=t, fc=fc, pb=pb: ve.tensor_mul(
                        TGB[t][:, fc * 259 + 3: fc * 259 + 3 + GT],
                        TGB[t][:, fc * 259 + 3: fc * 259 + 3 + GT],
                        ps1(pb * 512, pb * 512 + GT)),
                        reads=[TBB[t]] + bkb(pb), writes=[TBB[t]])
            if stop == "noch":
                continue
            sub = 99
            if isinstance(stop, str) and stop.startswith("ch"):
                sub = int(stop[2:])
            for ci2 in range(2):
                emit_chunk(g, ci2, sub=sub)

    def emit_corr():
        bk.op("dv", lambda ve: ve.tensor_scalar(
            Phi[0:64, :], Ssb[0:64, :], pcm[0:64, 0:1], None, ALU.mult),
            reads=[B["S"], B["cst"], B["Phi"]], writes=[B["Phi"]])
        bk.op("gp", lambda gs: gs.dma_start(out=D["sarc"], in_=Phi[0:64, :]),
              reads=[B["Phi"]], writes=[B["sarc"]], sem="gq", inc=16)
        bk.op("gp", lambda gs: gs.collective_compute(
            "AllReduce", ALU.add, replica_groups=PAIRS,
            ins=[D["sarc"].opt()], outs=[D["saro"].opt()]),
            reads=[B["sarc"]], writes=[B["saro"]], sem="cc", inc=1)
        bk.op("gp", lambda gs: gs.dma_start(out=Sin[0:64, :], in_=D["saro"]),
              reads=[B["saro"], B["Pp"]], writes=[B["Pp"]], sem="gq", inc=16)
        bk.op("dv", lambda ve: ve.tensor_scalar(
            Sin[0:64, :], Sin[0:64, :], pcm[0:64, 1:2], None, ALU.mult),
            reads=[B["Pp"], B["cst"]], writes=[B["Pp"]])
        wo_ci = [ring_add("sq", ("wot", hf)) for hf in range(2)]
        bk.op("dv", lambda ve: ve.engine_nop(),
              reads=[B["qg"], B["kg"], B["vg"], B["Xs"]],
              writes=[B["obl"], B["ob2"], B["dbl"], B["db2"], B["onb"], B["onr"]])
        for c in range(NCH):
            par = c % 2
            oc, ocb = (obl, B["obl"]) if par == 0 else (ob2, B["ob2"])
            dc_, dcb = (dbl, B["dbl"]) if par == 0 else (db2, B["db2"])
            bk.op("sp", lambda sp, oc=oc, c=c: sp.dma_start(
                out=oc[:, :], in_=D["ot"][c * 128:(c + 1) * 128, :]),
                reads=[B["ot"]], writes=[ocb], sem=bk.S(f"clo{par}"), inc=16)

            def fdl(sp, dc_=dc_, c=c):
                return sp.dma_start(
                    out=dc_[0:64, :].rearrange("q (h2 blk j) -> q h2 blk j", h2=2, j=128),
                    in_=D["dt"][c * 128:(c + 1) * 128, :]
                    .rearrange("(h2 q) (blk j) -> q h2 blk j", h2=2, j=128))
            bk.op("sp", fdl, reads=[B["dt"]], writes=[dcb], sem=bk.S(f"cld{par}"), inc=16)
            mms = []
            for h in range(NH):
                mms.append((PS2[hp(h):hp(h) + 64, (h // 2) * 128:(h // 2) * 128 + 128],
                            Sin[0:64, h * 64:(h + 1) * 64],
                            dc_[0:64, (h % 2) * 1024 + (h // 2) * 128:(h % 2) * 1024 + (h // 2) * 128 + 128],
                            True, True))
            mmg(bk, mms, reads=[B["Pp"], dcb], writes=bkb(4, 5))
            bk.op("dv", lambda ve, oc=oc: ve.tensor_add(onb[:, :], oc[:, :],
                                                        PS2[:, 0:1024]),
                  reads=[ocb] + bkb(4, 5) + [B["onb"]], writes=[B["onb"]])
            bk.op("ac", lambda sc: sc.activation(XK[:, 0:1024], onb[:, :], AF.Square),
                  reads=[B["onb"], B["XK"]], writes=[B["XK"]])
            mms = [(ps1(0, 512)[0:2, :], CB("oneslh", 2), XK[:, 0:512], True, True),
                   (ps1(512, 1024)[0:2, :], CB("oneslh", 2), XK[:, 512:1024], True, True)]
            mmg(bk, mms, reads=[B["XK"], B["cst"]], writes=bkb(0, 1))
            for hv in range(2):
                bk.op("ac", lambda sc, hv=hv: sc.activation(
                    brows[0:2, 0:512], ps1(hv * 512, (hv + 1) * 512)[0:2, :], AF.Sqrt,
                    bias=CFr("eps_a", 0, 2, 0, 1), scale=CFr("inv_hd", 0, 2, 0, 1)),
                    reads=bkb(hv) + [B["cst"], B["brows"]], writes=[B["brows"]])
                bk.op("dv", lambda ve: ve.reciprocal(brows[0:2, 0:512], brows[0:2, 0:512]),
                      reads=[B["brows"]], writes=[B["brows"]])
                bk.op("ac", lambda sc, hv=hv: sc.activation(
                    Xs[0:2, 1024 + hv * 512:1536 + hv * 512], brows[0:2, 0:512], AF.Copy),
                    reads=[B["brows"], B["onr"]], writes=[B["onr"]])
            mms = [(ps1(1024, 1536), CBr("e2", 0, 2, 0, 128), Xs[0:2, 1024:1536], True, True),
                   (ps1(1536, 2048), CBr("e2", 0, 2, 0, 128), Xs[0:2, 1536:2048], True, True)]
            mmg(bk, mms, reads=[B["onr"], B["cst"]], writes=bkb(2, 3))
            bk.op("dv", lambda ve: ve.tensor_mul(onb[:, :], onb[:, :], ps1(1024, 2048)),
                  reads=[B["onb"]] + bkb(2, 3), writes=[B["onb"]])
            slot0, slb0 = ring_use(wo_ci[0])
            slot1, slb1 = ring_use(wo_ci[1])
            mms = []
            for dc2 in range(8):
                slot = slot0 if dc2 < 4 else slot1
                for fci_ in range(8):
                    mms.append((ps1(dc2 * 128, (dc2 + 1) * 128),
                                slot[:, fci_ * 512 + (dc2 % 4) * 128: fci_ * 512 + (dc2 % 4) * 128 + 128],
                                onb[:, fci_ * 128:(fci_ + 1) * 128],
                                fci_ == 0, fci_ == 7))
            mmg(bk, mms, reads=[B["onb"], slb0, slb1], writes=bkb(0, 1))

            def fadd(ve, c=c):
                xo = x_sb[:, :].rearrange("p (fc t) -> p fc t", fc=8)[:, :, c * 128:(c + 1) * 128]
                return ve.tensor_add(xo, xo, ps1(0, 1024).rearrange("p (f t) -> p f t", f=8))
            bk.op("dv", fadd, reads=bkb(0, 1) + [B[f"x{i}"] for i in range(NFC)],
                  writes=[B[f"x{i}"] for i in range(NFC)])


    emit_ffn()
    if stop not in ("ln1", "gates", "ffn1"):
        emit_attention(stop=stop)
        if stop not in ("halo", "grp0", "noch", "groups") and not str(stop).startswith("ch"):
            emit_corr()
        if stop in ("attn", "halo", "grp0", "noch", "groups") or str(stop).startswith("ch"):
            bk.op("sp", lambda sp: sp.dma_start(out=D["dbg"], in_=x_sb[:, :]),
                  reads=[B[f"x{i}"] for i in range(NFC)], sem=bk.S("so"), inc=16)
        else:
            bk.op("dv", lambda ve: ve.engine_nop(),
                  reads=[B["QQb"], B["osb"], B["obl"], B["ob2"], B["onr"],
                         B["dsb"], B["dbl"], B["db2"], B["onb"], B["Pp"], B["Pt"]],
                  writes=[Bhgt[0], Bhgt[1], Bh[0], Bh[1], Bgb[0], Bgb[1],
                          B["qg"], B["kg"], B["vg"], B["grow"], B["Xs"]])
            emit_ffn()

    if stop == "ffn1":
        bk.op("sp", lambda sp: sp.dma_start(
            out=D["dbg"], in_=x_sb[:, :]),
            reads=[B[f"x{i}"] for i in range(NFC)], sem=bk.S("so"), inc=16)
    elif stop == "ln1":
        bk.op("gp", lambda gs: gs.dma_start(out=D["dbg"], in_=xn[:, :]),
              reads=[B[f"xn{i}"] for i in range(NFC)], sem=bk.S("so"), inc=16)
    elif stop == "gates":
        bk.op("gp", lambda gs: gs.dma_start(out=D["dbg"][0:8, 0:TPC], in_=grow[:, :]),
              reads=[B["grow"]], sem=bk.S("so"), inc=16)

    bk.op("sp", lambda sp: sp.dma_start(
        out=D["y"].rearrange("(fc p) t -> p fc t", p=128),
        in_=x_sb[:, :].rearrange("p (fc t) -> p fc t", fc=8)),
        reads=[B[f"x{i}"] for i in range(NFC)], sem=bk.S("so"), inc=16)
    final_waits = [bk.now(s) for s in list(bk.cnt)]
    bk.op("sp", lambda sp: None, waits=final_waits, sem="so", inc=0)

    sems = {}
    import contextlib as _cl
    semctx = _cl.ExitStack()
    for name in bk.cnt:
        sems[name] = semctx.enter_context(nc.semaphore(name))
    with nc.Block() as block:
        bk.replay(block, sems)
    semctx.close()
    ctx.close()
    return nc



def _stage_inputs(inputs):
    x = np.asarray(inputs["L0_attn_x"], np.float32)
    shared = {
        "w1t": np.ascontiguousarray(
            np.asarray(inputs["L0_ffn_w1"], np.float32).transpose(0, 2, 1)).reshape(8 * 1024, 2048),
        "w2t": np.ascontiguousarray(
            np.asarray(inputs["L0_ffn_w2"], np.float32).transpose(0, 2, 1)).reshape(8 * 2048, 1024),
        "wqt": np.ascontiguousarray(np.asarray(inputs["L0_attn_Wq"], np.float32).T),
        "wkt": np.ascontiguousarray(np.asarray(inputs["L0_attn_Wk"], np.float32).T),
        "wvt": np.ascontiguousarray(np.asarray(inputs["L0_attn_Wv"], np.float32).T),
        "wot": np.ascontiguousarray(np.asarray(inputs["L0_attn_Wo"], np.float32).T),
        "wbt": np.ascontiguousarray(np.asarray(inputs["L0_attn_Wb"], np.float32).T),
        "rwt": np.ascontiguousarray(np.asarray(inputs["L0_ffn_router_w"], np.float32).T),
        "cwt": np.ascontiguousarray(np.concatenate(
            [np.asarray(inputs["L0_attn_conv_q"], np.float32),
             np.asarray(inputs["L0_attn_conv_k"], np.float32),
             np.asarray(inputs["L0_attn_conv_v"], np.float32)], 1)),
        "onw": np.ascontiguousarray(
            np.tile(np.asarray(inputs["L0_attn_o_norm_w"], np.float32), 2))[:, None],
        "cbf": CBF_HOST,
        "cf": CF_HOST,
    }
    ims = []
    for c in range(8):
        b, hf = c // 2, c % 2
        pcmask = np.zeros((128, 2), np.float32)
        pcmask[:, 0] = 1.0 if c % 2 == 0 else 0.0
        pcmask[:, 1] = 1.0 if c % 2 == 1 else 0.0
        im = dict(shared)
        im["xt"] = np.ascontiguousarray(x[b, hf * TPC:(hf + 1) * TPC].T)
        im["pcm"] = pcmask
        ims.append(im)
    return ims


def kernel(**inputs):
    ims = _stage_inputs(inputs)
    nc = build_nc()
    res = run_bass_kernel_spmd(nc, ims, core_ids=list(range(8)))
    out = np.empty((4, 4096, 1024), np.float32)
    for c in range(8):
        b, hf = c // 2, c % 2
        out[b, hf * TPC:(hf + 1) * TPC] = np.asarray(res.results[c]["y"], np.float32).T
    return out


# version 03
# speedup vs baseline: 1.0781x; 1.0781x over previous
import numpy as np
import ml_dtypes
import concourse.bass as bass
import concourse.mybir as mybir
from concourse.bass_utils import run_bass_kernel_spmd

F32 = mybir.dt.float32
BF16 = mybir.dt.bfloat16
AF = mybir.ActivationFunctionType
ALU = mybir.AluOpType
AX = mybir.AxisListType.X
HB = ml_dtypes.bfloat16

NFC = 8
TPC = 2048
GT = 256
NGRP = TPC // GT
CH = 128
NCH = TPC // CH
NH = 16
HD = 64
NE = 8
FTG = 256
NFT = TPC // FTG

ENG2SEM = {"pe": "pe", "ac": "ac", "dv": "dv", "gp": "gp", "sp": "sq"}


class Buf:
    __slots__ = ("w", "rs", "nm")

    def __init__(self, nm=""):
        self.w = None
        self.rs = {}
        self.nm = nm


class BK:
    def __init__(self, nc):
        self.nc = nc
        self.cnt = {}
        self.recs = {"pe": [], "ac": [], "dv": [], "gp": [], "sp": []}

    def S(self, name):
        self.cnt.setdefault(name, 0)
        return name

    def now(self, name):
        return (name, self.cnt.get(name, 0))

    def op(self, eng, fn, reads=(), writes=(), sem=None, inc=1, waits=()):
        w = {}

        def merge(tok):
            if tok is None:
                return
            s, v = tok
            if v > 0 and w.get(s, 0) < v:
                w[s] = v

        for b in reads:
            merge(b.w)
        for b in writes:
            merge(b.w)
            for s, v in b.rs.items():
                merge((s, v))
        for t in waits:
            merge(t)
        sem = sem or ENG2SEM[eng]
        self.S(sem)
        self.cnt[sem] += inc
        tok = (sem, self.cnt[sem])
        wset = set(map(id, writes))
        for b in writes:
            b.w = tok
            b.rs = {}
        for b in reads:
            if id(b) not in wset:
                if b.rs.get(tok[0], 0) < tok[1]:
                    b.rs[tok[0]] = tok[1]
        self.recs[eng].append((sorted(w.items()), fn, (sem, inc)))
        return tok

    def replay(self, block, sems):
        def mk(recs):
            def body(eng):
                lastw = {}
                for waits, fn, (s, n) in recs:
                    for sname, v in waits:
                        if lastw.get(sname, -1) < v:
                            eng.wait_ge(sems[sname], v)
                            lastw[sname] = v
                    inst = fn(eng)
                    if inst is not None and n > 0:
                        inst.then_inc(sems[s], n)
            return body

        block.tensor(mk(self.recs["pe"]))
        block.scalar(mk(self.recs["ac"]))
        block.vector(mk(self.recs["dv"]))
        block.gpsimd(mk(self.recs["gp"]))
        block.sync(mk(self.recs["sp"]))


def mmg(bk, mms, reads, writes, waits=()):
    def fn(te, mms=tuple(mms)):
        inst = None
        for (o, l, r, st, sp) in mms:
            inst = te.matmul(o, l, r, start=st, stop=sp)
        return inst
    return bk.op("pe", fn, reads=reads, writes=writes, waits=waits)


def trg(bk, trs, reads, writes, waits=()):
    def fn(te, trs=tuple(trs)):
        inst = None
        for (o, i, d) in trs:
            inst = te.transpose(o, i, d)
        return inst
    return bk.op("pe", fn, reads=reads, writes=writes, waits=waits)



def _host_consts():
    cbf_cols = {}
    parts = []

    def add(name, arr):
        arr = np.asarray(arr, np.float32)
        a = np.zeros((128, arr.shape[1]), np.float32)
        a[: arr.shape[0]] = arr
        cbf_cols[name] = sum(p.shape[1] for p in parts)
        parts.append(a)

    ident = np.eye(128, dtype=np.float32)
    add("ident", ident)
    t = np.tril(np.ones((128, 128), np.float32), -1)
    add("mask_nl", np.concatenate([-t, -t], 1))
    add("mask_nu", np.concatenate([-t.T, -t.T], 1))
    u = np.triu(np.ones((128, 128), np.float32), 0)
    add("mask_ui", np.concatenate([u, u], 1))
    e8 = np.zeros((8, 8 * 128), np.float32)
    for e in range(8):
        e8[e, e * 128:(e + 1) * 128] = 1.0
    add("e8sel", e8)
    efc = np.zeros((128, 8 * 16), np.float32)
    for fc in range(8):
        for p in range(128):
            efc[p, fc * 16 + 2 * fc + (1 if p >= 64 else 0)] = 1.0
    add("efc", efc)
    eft = np.zeros((16, 8 * 128), np.float32)
    for fc in range(8):
        for p in range(128):
            eft[2 * fc + (1 if p >= 64 else 0), fc * 128 + p] = 1.0
    add("eft", eft)
    add("ones", np.ones((128, 1), np.float32))
    lo = np.zeros((128, 1), np.float32); lo[:64] = 1.0
    hi = np.zeros((128, 1), np.float32); hi[64:] = 1.0
    add("oneslh", np.concatenate([lo, hi], 1))
    e2 = np.zeros((2, 128), np.float32)
    e2[0, :64] = 1.0
    e2[1, 64:] = 1.0
    add("e2", e2)
    phii = np.zeros((64, 16 * 64), np.float32)
    for p in range(64):
        for j in range(16):
            phii[p, j * 64 + p] = 1.0
    add("phiinit", phii)
    cbf = np.concatenate(parts, 1).astype(HB)

    cf_cols = {}
    fparts = []

    def addf(name, arr):
        arr = np.asarray(arr, np.float32)
        a = np.zeros((128, arr.shape[1]), np.float32)
        a[: arr.shape[0]] = arr
        cf_cols[name] = sum(p.shape[1] for p in fparts)
        fparts.append(a)

    addf("identf", ident)
    addf("onesf", np.ones((128, 1), np.float32))
    addf("onesrow", np.ones((1, 128), np.float32))
    addf("eps_ln", np.full((128, 1), 1e-5, np.float32))
    addf("eps_a", np.full((128, 1), 1e-6, np.float32))
    addf("inv_hd", np.full((128, 1), 1.0 / 64.0, np.float32))
    cf = np.concatenate(fparts, 1).astype(np.float32)
    return cbf, cbf_cols, cf, cf_cols


CBF_HOST, CBF_COLS, CF_HOST, CF_COLS = _host_consts()



def build_nc(stop="full"):
    import contextlib
    nc = bass.Bass("TRN2", target_bir_lowering=False)
    D = {}
    D["xt"] = nc.dram_tensor("xt", [1024, TPC], F32, kind="ExternalInput").ap()
    D["w1t"] = nc.dram_tensor("w1t", [8 * 1024, 2048], F32, kind="ExternalInput").ap()
    D["w2t"] = nc.dram_tensor("w2t", [8 * 2048, 1024], F32, kind="ExternalInput").ap()
    for nm in ("wqt", "wkt", "wvt", "wot"):
        D[nm] = nc.dram_tensor(nm, [1024, 1024], F32, kind="ExternalInput").ap()
    D["wbt"] = nc.dram_tensor("wbt", [1024, 16], F32, kind="ExternalInput").ap()
    D["rwt"] = nc.dram_tensor("rwt", [1024, 8], F32, kind="ExternalInput").ap()
    D["cwt"] = nc.dram_tensor("cwt", [1024, 12], F32, kind="ExternalInput").ap()
    D["onw"] = nc.dram_tensor("onw", [128, 1], F32, kind="ExternalInput").ap()
    D["cbf"] = nc.dram_tensor("cbf", list(CBF_HOST.shape), BF16, kind="ExternalInput").ap()
    D["cf"] = nc.dram_tensor("cf", list(CF_HOST.shape), F32, kind="ExternalInput").ap()
    D["pcm"] = nc.dram_tensor("pcm", [128, 2], F32, kind="ExternalInput").ap()
    D["y"] = nc.dram_tensor("y", [1024, TPC], F32, kind="ExternalOutput").ap()
    if stop != "full":
        D["dbg"] = nc.dram_tensor("dbg", [128, NFC * TPC], F32, kind="ExternalOutput").ap()
    D["harc"] = nc.dram_tensor("harc", [128, 72], F32, kind="Internal").ap()
    D["haro"] = nc.dram_tensor("haro", [128, 72], F32, kind="Internal").ap()
    D["sarc"] = nc.dram_tensor("sarc", [64, 1024], F32, kind="Internal").ap()
    D["saro"] = nc.dram_tensor("saro", [64, 1024], F32, kind="Internal").ap()
    D["ot"] = nc.dram_tensor("ot", [TPC, 1024], BF16, kind="Internal").ap()
    D["dt"] = nc.dram_tensor("dt", [TPC, 1024], BF16, kind="Internal").ap()

    bk = BK(nc)
    PAIRS = [[0, 1], [2, 3], [4, 5], [6, 7]]
    ctx = contextlib.ExitStack()
    sb = lambda nm, sh, dt: ctx.enter_context(nc.sbuf_tensor(nm, sh, dt))
    pst = lambda nm, sh, dt: ctx.enter_context(nc.psum_tensor(nm, sh, dt))

    x_sb = sb("x_sb", [128, NFC * TPC], F32)
    xn = sb("xn", [128, NFC * TPC], BF16)
    slots = [sb(f"wsl{i}", [128, 4096], BF16) for i in range(5)]
    cbf = sb("cbfs", [128, CBF_HOST.shape[1]], BF16)
    cf = sb("cfs", [128, CF_HOST.shape[1]], F32)
    cw = sb("cw_s", [128, 96], F32)
    onw = sb("onw_s", [128, 1], F32)
    pcm = sb("pcm_s", [128, 2], F32)
    grow = sb("grow", [8, TPC], BF16)
    lnrow = sb("lnrow", [128, 512], F32)
    hsb_t = [sb(f"hsb{i}", [128, FTG], BF16) for i in range(2)]
    gbb_t = [sb(f"gbb{i}", [128, FTG], BF16) for i in range(2)]
    hgt_t = [sb(f"hgt{i}", [128, 8 * FTG], BF16) for i in range(2)]
    _xsq0 = sb("xsq0", [128, FTG], BF16)
    xsq_t = [_xsq0, _xsq0]

    PS1 = pst("PS1", [128, 2048], F32)
    PS2 = pst("PS2", [128, 1024], F32)
    PS3 = pst("PS3", [128, 1024], BF16)
    PS4 = pst("PS4", [128, 512], F32)

    B = {}
    for i in range(NFC):
        B[f"x{i}"] = Buf(f"x{i}")
        B[f"xn{i}"] = Buf(f"xn{i}")
    for i in range(5):
        B[f"sl{i}"] = Buf(f"sl{i}")
    for nm in ("cst", "grow", "lnrow", "onwb", "cwd",
               "harc", "haro", "sarc", "saro", "ot", "dt"):
        B[nm] = Buf(nm)
    for i in range(8):
        B[f"bk{i}"] = Buf(f"bk{i}")
    Bh = [Buf("h0"), Buf("h1")]
    Bgb = [Buf("gb0"), Buf("gb1")]
    Bhgt = [Buf("hgt0"), Buf("hgt1")]
    _bx = Buf("xsq0")
    Bxsq = [_bx, _bx]

    def bkb(*idx):
        return [B[f"bk{i}"] for i in idx]

    def ps1(c0, c1):
        return PS1[:, c0:c1]

    def CB(name, w=1, c0=0):
        c = CBF_COLS[name]
        return cbf[:, c + c0:c + c0 + w]

    def CBr(name, r0, r1, c0, w):
        c = CBF_COLS[name]
        return cbf[r0:r1, c + c0:c + c0 + w]

    def CF_(name, w=1):
        c = CF_COLS[name]
        return cf[:, c:c + w]

    def CFr(name, r0, r1, c0, w):
        c = CF_COLS[name]
        return cf[r0:r1, c + c0:c + c0 + w]

    for i in range(5):
        B[f"ini{i}"] = Buf(f"ini{i}")
    bk.op("sp", lambda sp: sp.dma_start(out=cbf[:, :], in_=D["cbf"]), writes=[B["ini0"]], inc=16)
    bk.op("sp", lambda sp: sp.dma_start(out=cf[:, :], in_=D["cf"]), writes=[B["ini1"]], inc=16)
    bk.op("sp", lambda sp: sp.dma_start(
        out=cw[:, :].rearrange("p (fc t) -> p fc t", fc=8),
        in_=D["cwt"].rearrange("(fc p) t -> p fc t", p=128)), writes=[B["ini2"]], inc=16)
    bk.op("sp", lambda sp: sp.dma_start(out=onw[:, :], in_=D["onw"]), writes=[B["ini3"]], inc=16)
    bk.op("sp", lambda sp: sp.dma_start(out=pcm[:, :], in_=D["pcm"]), writes=[B["ini4"]], inc=16)
    for i in range(NFC):
        bk.op("sp", lambda sp, i=i: sp.dma_start(
            out=x_sb[:, i * TPC:(i + 1) * TPC],
            in_=D["xt"][i * 128:(i + 1) * 128, :]),
            writes=[B[f"x{i}"]], inc=16)
    init_tok = bk.now("sq")
    for nm in ["cst", "onwb"] + [f"x{i}" for i in range(NFC)]:
        B[nm].w = init_tok

    ring = {"next": 0, "chunks": []}

    def ring_add(kind, arg):
        ring["chunks"].append((kind, arg))
        return len(ring["chunks"]) - 1

    def _load_chunk(ci):
        kind, arg = ring["chunks"][ci]
        sl = ci % 5
        slot = slots[sl]
        sem = bk.S(f"wr{sl}")
        if kind == "w1":
            e, q = arg
            src = D["w1t"][e * 1024:(e + 1) * 1024, q * 512:(q + 1) * 512] \
                .rearrange("(f p) hd -> p f hd", p=128)
            fn = lambda gs, s=slot, src=src: gs.dma_start(
                out=s[:, :].rearrange("p (f hd) -> p f hd", f=8), in_=src)
            n = 1
        elif kind == "w2":
            e, q = arg
            src = D["w2t"][e * 2048 + q * 512: e * 2048 + (q + 1) * 512, :] \
                .rearrange("(h p) fd -> p h fd", p=128)
            fn = lambda gs, s=slot, src=src: gs.dma_start(
                out=s[:, :].rearrange("p (h fd) -> p h fd", h=4), in_=src)
            n = 1
        elif kind == "sq":
            nm, hf = arg
            src = D[nm][:, hf * 512:(hf + 1) * 512] \
                .rearrange("(f p) od -> p f od", p=128)
            fn = lambda gs, s=slot, src=src: gs.dma_start(
                out=s[:, :].rearrange("p (f od) -> p f od", f=8), in_=src)
            n = 1
        elif kind == "miscb":
            fn = lambda gs, s=slot: gs.dma_start(
                out=s[:, 64:192].rearrange("p (f h) -> p f h", f=8),
                in_=D["wbt"].rearrange("(f p) h -> p f h", p=128))
            n = 1
        elif kind == "misc":
            bk.op("gp", lambda gs, s=slot: gs.dma_start(
                out=s[:, 0:64].rearrange("p (f e) -> p f e", f=8),
                in_=D["rwt"].rearrange("(f p) e -> p f e", p=128)),
                writes=[B[f"sl{sl}"]], sem=sem, inc=16)
            fn = lambda gs, s=slot: gs.dma_start(
                out=s[:, 64:192].rearrange("p (f h) -> p f h", f=8),
                in_=D["wbt"].rearrange("(f p) h -> p f h", p=128))
            n = 1
        else:
            raise KeyError(kind)
        bk.op("gp", fn, writes=[B[f"sl{sl}"]], sem=sem, inc=16 * n)
        if kind == "sq" and arg[0] == "wot":
            bk.op("dv", lambda ve, s=slot: ve.tensor_scalar(
                s[:, :], s[:, :], onw[:, 0:1], None, ALU.mult),
                reads=[B[f"sl{sl}"], B["onwb"]], writes=[B[f"sl{sl}"]])

    def ring_use(ci):
        while ring["next"] <= min(ci + 1, len(ring["chunks"]) - 1):
            _load_chunk(ring["next"])
            ring["next"] += 1
        return slots[ci % 5], B[f"sl{ci % 5}"]

    def emit_ln(t0, w):
        for fc in range(NFC):
            xa = x_sb[:, fc * TPC + t0: fc * TPC + t0 + w]
            mmg(bk, [(ps1(1024, 1024 + w)[0:1, :], CF_("onesf"), xa, fc == 0, fc == 7)],
                reads=[B[f"x{fc}"], B["cst"]], writes=bkb(2))
        for fc in range(NFC):
            xa = x_sb[:, fc * TPC + t0: fc * TPC + t0 + w]
            par = fc % 2
            bk.op("ac", lambda sc, o=xsq_t[par][:, 0:w], i=xa: sc.activation(o, i, AF.Square),
                  reads=[B[f"x{fc}"]], writes=[Bxsq[par]])
            mmg(bk, [(ps1(1536, 1536 + w)[0:1, :], CB("ones"), xsq_t[par][:, 0:w],
                      fc == 0, fc == 7)],
                reads=[Bxsq[par], B["cst"]], writes=bkb(3))
        bk.op("dv", lambda ve: ve.tensor_scalar(
            lnrow[0:1, 0:w], ps1(1024, 1024 + w)[0:1, :], 1.0 / 1024.0, None, ALU.mult),
            reads=bkb(2), writes=[B["lnrow"]])
        bk.op("dv", lambda ve: ve.tensor_mul(
            lnrow[0:1, 256:256 + w], lnrow[0:1, 0:w], lnrow[0:1, 0:w]),
            reads=[B["lnrow"]], writes=[B["lnrow"]])
        bk.op("dv", lambda ve: ve.scalar_tensor_tensor(
            lnrow[0:1, 256:256 + w], ps1(1536, 1536 + w)[0:1, :], 1.0 / 1024.0,
            lnrow[0:1, 256:256 + w], ALU.mult, ALU.subtract),
            reads=bkb(3) + [B["lnrow"]], writes=[B["lnrow"]])
        bk.op("ac", lambda sc: sc.activation(
            lnrow[0:1, 256:256 + w], lnrow[0:1, 256:256 + w], AF.Sqrt,
            bias=CFr("eps_ln", 0, 1, 0, 1)),
            reads=[B["lnrow"], B["cst"]], writes=[B["lnrow"]])
        bk.op("dv", lambda ve: ve.reciprocal(lnrow[0:1, 256:256 + w], lnrow[0:1, 256:256 + w]),
              reads=[B["lnrow"]], writes=[B["lnrow"]])
        mmg(bk, [(ps1(0, w), CFr("onesrow", 0, 1, 0, 128), lnrow[0:1, 0:w], True, True)],
            reads=[B["lnrow"], B["cst"]], writes=bkb(0))
        mmg(bk, [(ps1(512, 512 + w), CFr("onesrow", 0, 1, 0, 128), lnrow[0:1, 256:256 + w],
                  True, True)],
            reads=[B["lnrow"], B["cst"]], writes=bkb(1))
        for fc in range(NFC):
            par = fc % 2
            xa = x_sb[:, fc * TPC + t0: fc * TPC + t0 + w]
            xo = xn[:, fc * TPC + t0: fc * TPC + t0 + w]
            bk.op("dv", lambda ve, xa=xa, par=par: ve.tensor_sub(
                xsq_t[par][:, 0:w], xa, ps1(0, w)),
                reads=[B[f"x{fc}"]] + bkb(0), writes=[Bxsq[par]])
            bk.op("dv", lambda ve, xo=xo, par=par: ve.tensor_mul(
                xo, xsq_t[par][:, 0:w], ps1(512, 512 + w)),
                reads=[Bxsq[par]] + bkb(1), writes=[B[f"xn{fc}"]])

    def emit_gates(misc_ci):
        slot, slb = ring_use(misc_ci)
        for ci in range(16):
            t0 = ci * 128
            mms = []
            for fc in range(NFC):
                mms.append((PS4[:, 0:8],
                            xn[:, fc * TPC + t0: fc * TPC + t0 + 128],
                            slot[:, fc * 8: fc * 8 + 8], fc == 0, fc == 7))
            mmg(bk, mms, reads=[B[f"xn{i}"] for i in range(NFC)] + [slb], writes=bkb(7))
            bk.op("dv", lambda ve: ve.tensor_copy(lgt[:, 0:8], PS4[:, 0:8]),
                  reads=bkb(7), writes=[B["lgt"]])

            def st(fn):
                bk.op("dv", fn, reads=[B["lgt"]], writes=[B["lgt"]])
            st(lambda ve: ve.reduce_max(lgt[:, 8:9], lgt[:, 0:8], axis=AX))
            st(lambda ve: ve.tensor_scalar(lgt[:, 16:24], lgt[:, 0:8], lgt[:, 8:9],
                                           None, ALU.is_equal))
            st(lambda ve: ve.scalar_tensor_tensor(lgt[:, 24:32], lgt[:, 16:24], -1e30,
                                                  lgt[:, 0:8], ALU.mult, ALU.add))
            st(lambda ve: ve.reduce_max(lgt[:, 9:10], lgt[:, 24:32], axis=AX))
            st(lambda ve: ve.tensor_sub(lgt[:, 10:11], lgt[:, 9:10], lgt[:, 8:9]))
            bk.op("ac", lambda sc: sc.activation(lgt[:, 11:12], lgt[:, 10:11], AF.Exp),
                  reads=[B["lgt"]], writes=[B["lgt"]])
            st(lambda ve: ve.tensor_scalar(lgt[:, 12:13], lgt[:, 11:12], 1.0, None, ALU.add))
            st(lambda ve: ve.reciprocal(lgt[:, 13:14], lgt[:, 12:13]))
            st(lambda ve: ve.tensor_scalar(lgt[:, 14:15], lgt[:, 13:14], 0.5, None, ALU.mult))
            st(lambda ve: ve.tensor_scalar(lgt[:, 15:16], lgt[:, 14:15], -1.0, 0.5,
                                           ALU.mult, ALU.add))
            st(lambda ve: ve.tensor_scalar(lgt[:, 32:40], lgt[:, 24:32], lgt[:, 9:10],
                                           None, ALU.is_equal))
            st(lambda ve: ve.tensor_scalar(lgt[:, 40:48], lgt[:, 16:24], lgt[:, 14:15],
                                           None, ALU.mult))
            st(lambda ve: ve.scalar_tensor_tensor(lgt[:, 40:48], lgt[:, 32:40],
                                                  lgt[:, 15:16], lgt[:, 40:48],
                                                  ALU.mult, ALU.add))
            trg(bk, [(PS4[0:8, 256:384], lgt[:, 40:48], CF_("identf", 128))],
                reads=[B["lgt"], B["cst"]], writes=bkb(7))
            bk.op("ac", lambda sc, t0=t0: sc.activation(
                grow[0:8, t0:t0 + 128], PS4[0:8, 256:384], AF.Copy),
                reads=bkb(7), writes=[B["grow"]])

    def emit_ffn():
        misc_ci = ring_add("misc", None)
        for tg in range(NFT):
            emit_ln(tg * FTG, FTG)
        emit_gates(misc_ci)
        ch = {}
        for e in range(NE):
            for q in range(4):
                ch[(e, "w1", q)] = ring_add("w1", (e, q))
                ch[(e, "w2", q)] = ring_add("w2", (e, q))
        xn_reads = [B[f"xn{i}"] for i in range(NFC)]
        for e in range(NE):
            for hcg in range(2):
                for tg in range(NFT):
                    gpar = tg % 2
                    mmg(bk, [(PS4[:, 0:FTG], CBr("e8sel", 0, 8, e * 128, 128),
                              grow[0:8, tg * FTG:(tg + 1) * FTG], True, True)],
                        reads=[B["grow"], B["cst"]], writes=bkb(7))
                    bk.op("ac", lambda sc, gpar=gpar: sc.activation(
                        gbb_t[gpar][:, :], PS4[:, 0:FTG], AF.Copy),
                        reads=bkb(7), writes=[Bgb[gpar]])
                    hpar = tg % 2
                    for i in range(8):
                        hc = hcg * 8 + i
                        slot, slb = ring_use(ch[(e, "w1", hc // 4)])
                        wpar = i % 2
                        c0 = (hc % 4) * 128
                        mms = []
                        for fc in range(NFC):
                            mms.append((PS2[:, wpar * 512: wpar * 512 + FTG],
                                        slot[:, fc * 512 + c0: fc * 512 + c0 + 128],
                                        xn[:, fc * TPC + tg * FTG: fc * TPC + (tg + 1) * FTG],
                                        fc == 0, fc == 7))
                        mmg(bk, mms, reads=[slb] + xn_reads, writes=bkb(4 + wpar))
                        bk.op("ac", lambda sc, wpar=wpar: sc.activation(
                            hsb_t[wpar][:, :], PS2[:, wpar * 512: wpar * 512 + FTG], AF.Gelu),
                            reads=bkb(4 + wpar), writes=[Bh[wpar]])
                        bk.op("gp", lambda gs, wpar=wpar, i=i, hpar=hpar, gpar=gpar:
                              gs.tensor_mul(hgt_t[hpar][:, i * FTG:(i + 1) * FTG],
                                            hsb_t[wpar][:, :], gbb_t[gpar][:, :]),
                              reads=[Bh[wpar], Bgb[gpar]], writes=[Bhgt[hpar]])
                    for s in range(4):
                        bp = (s % 2) * 2
                        slota, slba = ring_use(ch[(e, "w2", hcg * 2)])
                        slotb, slbb = ring_use(ch[(e, "w2", hcg * 2 + 1)])
                        mms = []
                        for i in range(8):
                            hc = hcg * 8 + i
                            slot = slota if i < 4 else slotb
                            c0 = (hc % 4) * 1024
                            for j in range(2):
                                fc = s * 2 + j
                                mms.append((ps1((bp + j) * 512, (bp + j) * 512 + FTG),
                                            slot[:, c0 + fc * 128: c0 + fc * 128 + 128],
                                            hgt_t[hpar][:, i * FTG:(i + 1) * FTG],
                                            i == 0, i == 7))
                        mmg(bk, mms, reads=[slba, slbb, Bhgt[hpar]], writes=bkb(bp, bp + 1))

                        def fev(ve, s=s, bp=bp, tg=tg):
                            xo = x_sb[:, :].rearrange("p (fc t) -> p fc t", fc=8)[
                                :, s * 2: s * 2 + 2, tg * FTG:(tg + 1) * FTG]
                            pin = ps1(bp * 512, (bp + 2) * 512).rearrange(
                                "p (j t) -> p j t", j=2)[:, :, 0:FTG]
                            return ve.tensor_add(xo, xo, pin)
                        bk.op("dv", fev,
                              reads=bkb(bp, bp + 1) + [B[f"x{s * 2}"], B[f"x{s * 2 + 1}"]],
                              writes=[B[f"x{s * 2}"], B[f"x{s * 2 + 1}"]])

    qg = sb("qg", [128, 8 * 259], BF16)
    kg = sb("kg", [128, 8 * 259], BF16)
    vg = sb("vg", [128, 8 * 259], BF16)
    XK = sb("XK", [128, 2048], BF16)
    Xs = sb("Xs", [128, 2048], BF16)
    Pp = sb("Pp", [128, 2048], BF16)
    Pt = sb("Pt", [128, 2048], BF16)

    Ssb = sb("Ssb", [64, 1024], F32)
    Phi = sb("Phi", [64, 1024], F32)
    qnc = sb("qnc", [64, 2048], BF16)
    brows = sb("brows", [128, 512], F32)
    brb = sb("brb", [128, 768], BF16)
    contrib = nc.alloc_sbuf_tensor_at("contribov", [128, 72], F32,
                                      offset=nc.lookup_mloc(hsb_t[0]).addr)
    halo = nc.alloc_sbuf_tensor_at("haloov", [128, 72], F32,
                                   offset=nc.lookup_mloc(hsb_t[1]).addr)
    carry = nc.alloc_sbuf_tensor_at("carryov", [128, 72], BF16,
                                    offset=nc.lookup_mloc(gbb_t[0]).addr)
    sqtm = nc.alloc_sbuf_tensor_at("sqtmov", [128, 16], F32,
                                   offset=nc.lookup_mloc(gbb_t[1]).addr)
    a_h0 = nc.lookup_mloc(hgt_t[0]).addr
    a_h1 = nc.lookup_mloc(hgt_t[1]).addr
    assert a_h1 == a_h0 + 4096, (a_h0, a_h1)
    QQ = nc.alloc_sbuf_tensor_at("QQov", [64, 2048], F32, offset=a_h0)
    a_pp = nc.lookup_mloc(Pp).addr
    a_pt = nc.lookup_mloc(Pt).addr
    a_xs = nc.lookup_mloc(Xs).addr
    kbc = nc.alloc_sbuf_tensor_at("kbcov", [64, 2048], BF16, offset=a_h0)
    Sin = nc.alloc_sbuf_tensor_at("Sinov", [64, 1024], BF16, offset=a_pp)
    cwd = nc.alloc_sbuf_tensor_at("cwdov", [128, 512], BF16, offset=a_pp)
    lgt = nc.alloc_sbuf_tensor_at("lgtov", [128, 48], F32, offset=a_pp + 2048)
    Mb = nc.alloc_sbuf_tensor_at("Mbov", [128, 2048], BF16, offset=a_pt)
    HTn = nc.alloc_sbuf_tensor_at("HTnov", [64, 1024], F32, offset=a_pt)
    qtm = nc.alloc_sbuf_tensor_at("qtmov", [128, 1024], BF16, offset=a_xs)
    aq = nc.lookup_mloc(qg).addr
    ak = nc.lookup_mloc(kg).addr
    av = nc.lookup_mloc(vg).addr
    ag = nc.lookup_mloc(grow).addr
    osb = nc.alloc_sbuf_tensor_at("osb", [128, 1024], BF16, offset=ag)
    dsb = nc.alloc_sbuf_tensor_at("dsb", [128, 1024], BF16, offset=ag + 2048)
    dbl = nc.alloc_sbuf_tensor_at("dbl", [64, 2048], BF16, offset=aq)
    db2 = nc.alloc_sbuf_tensor_at("db2", [64, 2048], BF16, offset=ak)
    obl = nc.alloc_sbuf_tensor_at("obl", [128, 1024], BF16, offset=av)
    ob2 = nc.alloc_sbuf_tensor_at("ob2", [128, 1024], BF16, offset=av + 2048)
    onb = nc.alloc_sbuf_tensor_at("onb", [128, 1024], BF16, offset=a_xs)

    for nm in ("qg", "kg", "vg", "QQb",
               "S", "Phi", "Sin", "qnc",
               "brb", "brows", "osb", "obl", "ob2", "dsb", "dbl", "db2", "onb",
               "onr"):
        B[nm] = Buf(nm)
    PpH = [Buf("PpH0"), Buf("PpH1")]
    PtH = [Buf("PtH0"), Buf("PtH1")]
    XsH = [Buf("XsH0"), Buf("XsH1")]
    XKH = [Buf("XKH0"), Buf("XKH1")]
    B["lgt"] = PpH[1]
    B["contribb"] = Bh[0]
    B["halo"] = Bh[1]
    B["carry"] = Bgb[0]
    B["sqtmb"] = Bgb[1]

    def hp(h):
        return (h % 2) * 64

    TGB = {"q": qg, "k": kg, "v": vg}
    TBB = {"q": B["qg"], "k": B["kg"], "v": B["vg"]}
    WNM = {"q": "wqt", "k": "wkt", "v": "wvt"}

    def emit_chunk(g, ci2, sub=99, pend=None):
        pend = list(pend or ())

        def drain(n=1):
            for _ in range(min(n, len(pend))):
                pend.pop(0)()
        c = g * 2 + ci2
        base = 3 + ci2 * CH
        cb0 = ci2 * CH
        trg(bk, [(PS3[0:128, 0:16], brb[0:16, 512 + cb0:512 + cb0 + CH],
                  CBr("ident", 0, 16, 0, 16))],
            reads=[B["brb"], B["cst"]], writes=bkb(6))
        bk.op("ac", lambda sc: sc.activation(sqtm[:, 0:16], PS3[0:128, 0:16], AF.Copy),
              reads=bkb(6), writes=[B["sqtmb"]])
        drain(2)
        trs = [(PS3[:, fc * 128:(fc + 1) * 128],
                kg[:, fc * 259 + base: fc * 259 + base + CH], CB("ident", 128))
               for fc in range(NFC)]
        trg(bk, trs, reads=[B["kg"], B["cst"]], writes=bkb(6))
        drain(3)

        def ft1(ve):
            inst = None
            for h in range(NH):
                inst = ve.tensor_scalar(
                    XK[:, h * 128 + 64:(h + 1) * 128],
                    PS3[:, (h // 2) * 128 + (h % 2) * 64:(h // 2) * 128 + (h % 2) * 64 + 64],
                    sqtm[:, h:h + 1], None, ALU.mult)
            return inst
        bk.op("dv", ft1, reads=bkb(6) + [B["sqtmb"]], writes=[XKH[0], XKH[1]])
        drain()
        trs = [(PS3[:, fc * 128:(fc + 1) * 128],
                vg[:, fc * 259 + base: fc * 259 + base + CH], CB("ident", 128))
               for fc in range(NFC)]
        trg(bk, trs, reads=[B["vg"], B["cst"]], writes=bkb(6))

        def ft2(ve):
            inst = None
            for h in range(NH):
                inst = ve.tensor_scalar(
                    XK[:, h * 128:h * 128 + 64],
                    PS3[:, (h // 2) * 128 + (h % 2) * 64:(h // 2) * 128 + (h % 2) * 64 + 64],
                    sqtm[:, h:h + 1], None, ALU.mult)
            return inst
        bk.op("dv", ft2, reads=bkb(6) + [B["sqtmb"]], writes=[XKH[0], XKH[1]])
        drain()
        for r in range(2):
            trs = [(PS3[0:64, i * 128:(i + 1) * 128],
                    XK[:, (r * 8 + i) * 128 + 64:(r * 8 + i) * 128 + 128],
                    CB("ident", 128)) for i in range(8)]
            trg(bk, trs, reads=[XKH[r], B["cst"]], writes=bkb(6))
            bk.op("ac", lambda sc, r=r: sc.activation(
                kbc[0:64, r * 1024:(r + 1) * 1024], PS3[0:64, 0:1024], AF.Copy),
                reads=bkb(6), writes=[B["QQb"]])
            drain()
        trs = [(PS3[:, fc * 128:(fc + 1) * 128],
                qg[:, fc * 259 + base: fc * 259 + base + CH], CB("ident", 128))
               for fc in range(NFC)]
        trg(bk, trs, reads=[B["qg"], B["cst"]], writes=bkb(6))
        bk.op("dv", lambda ve: ve.tensor_copy(qtm[:, :], PS3[:, 0:1024]),
              reads=bkb(6), writes=[XsH[0]])
        drain()
        for r in range(2):
            trs = [(PS3[0:64, i * 128:(i + 1) * 128],
                    qtm[:, (r * 8 + i) * 64:(r * 8 + i) * 64 + 64],
                    CB("ident", 128)) for i in range(8)]
            trg(bk, trs, reads=[XsH[0], B["cst"]], writes=bkb(6))
            eng = "ac" if r == 0 else "dv"
            def ev5(e, r=r):
                if r == 0:
                    return e.activation(qnc[0:64, 0:1024], PS3[0:64, 0:1024], AF.Copy)
                return e.tensor_copy(qnc[0:64, 1024:2048], PS3[0:64, 0:1024])
            bk.op(eng, ev5, reads=bkb(6), writes=[B["qnc"]])
            drain()
        drain(99)
        if sub <= 1:
            return []
        for half in range(2):
            w0 = half * 1024
            mms = []
            for h in range(half * 8, half * 8 + 8):
                kb_h = kbc[0:64, h * 128:(h + 1) * 128]
                mms.append((ps1(w0 + (h % 8) * 128, w0 + (h % 8) * 128 + 128),
                            kb_h, kb_h, True, True))
            mmg(bk, mms, reads=[B["QQb"]], writes=bkb(half * 2, half * 2 + 1))
            for j4 in range(4):
                j = half * 4 + j4
                bk.op("dv", lambda ve, j=j: ve.tensor_mul(
                    Pp[:, j * 256:(j + 1) * 256], ps1(j * 256, (j + 1) * 256),
                    CB("mask_nl", 256)),
                    reads=bkb(j // 2) + [B["cst"], PpH[half]], writes=[PpH[half]])
                bk.op("dv", lambda ve, j=j: ve.tensor_mul(
                    Pt[:, j * 256:(j + 1) * 256], ps1(j * 256, (j + 1) * 256),
                    CB("mask_nu", 256)),
                    reads=bkb(j // 2) + [PtH[half]], writes=[PtH[half]])
        if sub <= 2:
            return []

        def apply_lev(lev):
            if lev == 0:
                src_t, src_b = XK, XKH
            else:
                src_t, src_b = Xs, XsH
            mms = []
            for h in range(8):
                mms.append((ps1((h % 8) * 128, (h % 8) * 128 + 128),
                            Pt[:, h * 128:(h + 1) * 128],
                            src_t[:, h * 128:(h + 1) * 128], True, True))
            mmg(bk, mms, reads=[src_b[0], PtH[0]], writes=bkb(0, 1))
            mms = []
            for h in range(8, 16):
                mms.append((ps1(1024 + (h % 8) * 128, 1024 + (h % 8) * 128 + 128),
                            CB("ident", 128),
                            src_t[:, h * 128:(h + 1) * 128], True, False))
                mms.append((ps1(1024 + (h % 8) * 128, 1024 + (h % 8) * 128 + 128),
                            Pt[:, h * 128:(h + 1) * 128],
                            src_t[:, h * 128:(h + 1) * 128], False, True))
            mmg(bk, mms, reads=[src_b[1], PtH[1], B["cst"]], writes=bkb(2, 3))
            bk.op("dv", lambda ve, src_t=src_t: ve.tensor_add(
                Xs[:, 0:1024], src_t[:, 0:1024], ps1(0, 1024)),
                reads=bkb(0, 1) + [src_b[0]], writes=[XsH[0]])
            bk.op("ac", lambda sc: sc.activation(
                Xs[:, 1024:2048], ps1(1024, 2048), AF.Copy),
                reads=bkb(2, 3), writes=[XsH[1]])
        apply_lev(0)
        for lev in range(1, 7):
            for half in range(2):
                hs = list(range(half * 8, half * 8 + 8))
                wA = 1024 if (lev == 6 and half == 1) else 0
                bA = wA // 512
                mms = []
                for h in hs:
                    mms.append((ps1(wA + (h % 8) * 128, wA + (h % 8) * 128 + 128),
                                Pt[:, h * 128:(h + 1) * 128],
                                Pp[:, h * 128:(h + 1) * 128], True, True))
                mmg(bk, mms, reads=[PpH[half], PtH[half]], writes=bkb(bA, bA + 1))
                if lev < 6:
                    mms = []
                    for h in hs:
                        mms.append((ps1(1024 + (h % 8) * 128, 1024 + (h % 8) * 128 + 128),
                                    Pp[:, h * 128:(h + 1) * 128],
                                    Pt[:, h * 128:(h + 1) * 128], True, True))
                    mmg(bk, mms, reads=[PpH[half], PtH[half]], writes=bkb(2, 3))
                engp = "ac" if half == 0 else "dv"

                def evp(e, half=half, wA=wA, engp=engp):
                    if engp == "ac":
                        return e.activation(Pp[:, half * 1024:(half + 1) * 1024],
                                            ps1(wA, wA + 1024), AF.Copy)
                    return e.tensor_copy(Pp[:, half * 1024:(half + 1) * 1024],
                                         ps1(wA, wA + 1024))
                bk.op(engp, evp, reads=bkb(bA, bA + 1) + [PpH[half]], writes=[PpH[half]])
                if lev < 6:
                    engt = "dv" if half == 0 else "ac"

                    def evt(e, half=half, engt=engt):
                        if engt == "ac":
                            return e.activation(Pt[:, half * 1024:(half + 1) * 1024],
                                                ps1(1024, 2048), AF.Copy)
                        return e.tensor_copy(Pt[:, half * 1024:(half + 1) * 1024],
                                             ps1(1024, 2048))
                    bk.op(engt, evt, reads=bkb(2, 3) + [PtH[half]], writes=[PtH[half]])
            if lev == 6:
                for half in range(2):
                    hs = list(range(half * 8, half * 8 + 8))
                    trs = [(PS3[:, (h % 8) * 128:((h % 8) + 1) * 128],
                            Pp[:, h * 128:(h + 1) * 128], CB("ident", 128))
                           for h in hs]
                    trg(bk, trs, reads=[PpH[half], B["cst"]], writes=bkb(6))
                    engt = "ac" if half == 0 else "dv"

                    def evt6(e, half=half, engt=engt):
                        if engt == "ac":
                            return e.activation(Pt[:, half * 1024:(half + 1) * 1024],
                                                PS3[:, 0:1024], AF.Copy)
                        return e.tensor_copy(Pt[:, half * 1024:(half + 1) * 1024],
                                             PS3[:, 0:1024])
                    bk.op(engt, evt6, reads=bkb(6) + [PtH[half]], writes=[PtH[half]])
            apply_lev(lev)
        if sub <= 3:
            return []
        for half in range(2):
            w0 = half * 1024
            mms = []
            for h in range(half * 8, half * 8 + 8):
                mms.append((ps1(w0 + (h % 8) * 128, w0 + (h % 8) * 128 + 128),
                            kbc[0:64, h * 128:(h + 1) * 128],
                            qnc[0:64, h * 128:(h + 1) * 128], True, True))
            mmg(bk, mms, reads=[B["QQb"], B["qnc"]], writes=bkb(half * 2, half * 2 + 1))
            for j4 in range(4):
                j = half * 4 + j4
                bk.op("dv", lambda ve, j=j: ve.tensor_mul(
                    Mb[:, j * 256:(j + 1) * 256], ps1(j * 256, (j + 1) * 256),
                    CB("mask_ui", 256)),
                    reads=bkb(j // 2) + [B["cst"], PtH[half]], writes=[PtH[half]])
        if sub <= 4:
            return []
        mms = []
        for h in range(NH):
            mms.append((PS2[hp(h):hp(h) + 64, (h // 2) * 128:(h // 2) * 128 + 128],
                        Xs[:, h * 128:h * 128 + 64], Mb[:, h * 128:(h + 1) * 128],
                        True, True))
        mmg(bk, mms, reads=[XsH[0], XsH[1], PtH[0], PtH[1]], writes=bkb(4, 5))
        for half in range(2):
            w0 = half * 1024
            mms = []
            for h in range(half * 8, half * 8 + 8):
                mms.append((ps1(w0 + (h % 8) * 128, w0 + (h % 8) * 128 + 128)[0:64, :],
                            Xs[:, h * 128 + 64:(h + 1) * 128], Mb[:, h * 128:(h + 1) * 128],
                            True, True))
            mmg(bk, mms, reads=[XsH[half], PtH[half]], writes=bkb(half * 2, half * 2 + 1))
            bk.op("dv", lambda ve, w0=w0: ve.tensor_sub(
                QQ[0:64, w0:w0 + 1024], qnc[0:64, w0:w0 + 1024],
                ps1(w0, w0 + 1024)[0:64, :]),
                reads=[B["qnc"]] + bkb(half * 2, half * 2 + 1), writes=[B["QQb"]])
        if sub <= 5:
            return []
        mms = []
        for h in range(NH):
            mms.append((ps1(h * 64, (h + 1) * 64)[0:64, :],
                        Xs[:, h * 128 + 64:(h + 1) * 128], XK[:, h * 128 + 64:(h + 1) * 128],
                        True, True))
        mmg(bk, mms, reads=[XsH[0], XsH[1], XKH[0], XKH[1]], writes=bkb(0, 1))
        bk.op("dv", lambda ve: ve.tensor_scalar(
            HTn[0:64, :], ps1(0, 1024)[0:64, :], -1.0, None, ALU.mult),
            reads=bkb(0, 1) + [PtH[0], PtH[1]], writes=[PtH[0], PtH[1]])

        tail = []

        def t_oseq():
            mms = []
            for h in range(NH):
                mms.append((ps1(1024 + (h // 2) * 128, 1024 + (h // 2) * 128 + 128)[hp(h):hp(h) + 64, :],
                            Ssb[0:64, h * 64:(h + 1) * 64],
                            QQ[0:64, h * 128:(h + 1) * 128],
                            True, True))
            mmg(bk, mms, reads=[B["S"], B["QQb"]], writes=bkb(2, 3))
        tail.append(t_oseq)

        def t_d():
            mms = []
            for h in range(NH):
                mms.append((ps1((h // 2) * 128, (h // 2) * 128 + 128)[hp(h):hp(h) + 64, :],
                            Phi[0:64, h * 64:(h + 1) * 64],
                            QQ[0:64, h * 128:(h + 1) * 128],
                            True, True))
            mmg(bk, mms, reads=[B["Phi"], B["QQb"]], writes=bkb(0, 1))
        tail.append(t_d)
        tail.append(lambda: bk.op("ac", lambda sc: sc.activation(
            osb[:, :], PS2[:, 0:1024], AF.Copy),
            reads=bkb(4, 5), writes=[B["osb"]]))
        tail.append(lambda: bk.op("dv", lambda ve: ve.tensor_add(
            osb[:, :], osb[:, :], ps1(1024, 2048)),
            reads=bkb(2, 3) + [B["osb"]], writes=[B["osb"]]))

        def t_g0():
            mms = []
            for h in range(NH):
                mms.append((PS2[0:64, h * 64:(h + 1) * 64],
                            XK[:, h * 128 + 64:(h + 1) * 128], Xs[:, h * 128:h * 128 + 64],
                            True, True))
            mmg(bk, mms, reads=[XKH[0], XKH[1], XsH[0], XsH[1]], writes=bkb(4, 5))
        tail.append(t_g0)
        tail.append(lambda: bk.op("sp", lambda sp, c=c: sp.dma_start(
            out=D["ot"][c * 128:(c + 1) * 128, :], in_=osb[:, :]),
            reads=[B["osb"]], writes=[B["ot"]], sem=bk.S("svo"), inc=16))
        tail.append(lambda: bk.op("dv", lambda ve: ve.tensor_copy(
            dsb[:, :], ps1(0, 1024)),
            reads=bkb(0, 1), writes=[B["dsb"]]))
        tail.append(lambda: bk.op("sp", lambda sp, c=c: sp.dma_start(
            out=D["dt"][c * 128:(c + 1) * 128, :], in_=dsb[:, :]),
            reads=[B["dsb"]], writes=[B["dt"]], sem=bk.S("svd"), inc=16))

        def t_ds():
            mms = []
            for h in range(NH):
                mms.append((ps1(h * 64, (h + 1) * 64)[0:64, :],
                            HTn[0:64, h * 64:(h + 1) * 64],
                            Ssb[0:64, h * 64:(h + 1) * 64],
                            True, True))
            mmg(bk, mms, reads=[PtH[0], PtH[1], B["S"]], writes=bkb(0, 1))
        tail.append(t_ds)

        def t_dphi():
            mms = []
            for h in range(NH):
                mms.append((ps1(1024 + h * 64, 1024 + (h + 1) * 64)[0:64, :],
                            HTn[0:64, h * 64:(h + 1) * 64],
                            Phi[0:64, h * 64:(h + 1) * 64],
                            True, True))
            mmg(bk, mms, reads=[PtH[0], PtH[1], B["Phi"]], writes=bkb(2, 3))
        tail.append(t_dphi)
        tail.append(lambda: bk.op("dv", lambda ve: ve.tensor_add(
            Ssb[0:64, :], Ssb[0:64, :], PS2[0:64, 0:1024]),
            reads=bkb(4, 5) + [B["S"]], writes=[B["S"]]))
        tail.append(lambda: bk.op("dv", lambda ve: ve.tensor_add(
            Ssb[0:64, :], Ssb[0:64, :], ps1(0, 1024)[0:64, :]),
            reads=bkb(0, 1) + [B["S"]], writes=[B["S"]]))
        tail.append(lambda: bk.op("dv", lambda ve: ve.tensor_add(
            Phi[0:64, :], Phi[0:64, :], ps1(1024, 2048)[0:64, :]),
            reads=bkb(2, 3) + [B["Phi"]], writes=[B["Phi"]]))
        if sub < 99:
            for f in tail:
                f()
            return []
        return tail

    def emit_attention(stop="full"):
        xnr = [B[f"xn{i}"] for i in range(NFC)]
        bk.op("dv", lambda ve: ve.engine_nop(),
              reads=[Bhgt[0], Bhgt[1], B["grow"]],
              writes=[B["QQb"], B["osb"], B["dsb"]])
        bk.op("dv", lambda ve: ve.memset(Ssb[:, :], 0.0), writes=[B["S"]])
        bk.op("dv", lambda ve: ve.tensor_copy(
            Phi[0:64, :], CBr("phiinit", 0, 64, 0, 1024)), reads=[B["cst"]], writes=[B["Phi"]])
        mini = {t: [ring_add("sq", (WNM[t], hf)) for hf in range(2)] for t in "qkv"}
        emit_ln(TPC - 3, 3)
        for ti, t in enumerate("qkv"):
            for dc in range(8):
                slot, slb = ring_use(mini[t][dc // 4])
                mms = []
                for fc in range(NFC):
                    mms.append((PS4[:, 0:3],
                                slot[:, fc * 512 + (dc % 4) * 128: fc * 512 + (dc % 4) * 128 + 128],
                                xn[:, fc * TPC + TPC - 3: fc * TPC + TPC],
                                fc == 0, fc == 7))
                mmg(bk, mms, reads=[slb] + xnr, writes=bkb(7))
                bk.op("ac", lambda sc, ti=ti, dc=dc: sc.activation(
                    contrib[:, ti * 24 + dc * 3: ti * 24 + dc * 3 + 3], PS4[:, 0:3],
                    AF.Copy), reads=bkb(7), writes=[B["contribb"]])
        bk.op("dv", lambda ve: ve.tensor_scalar(
            contrib[:, :], contrib[:, :], pcm[:, 0:1], None, ALU.mult),
            reads=[B["contribb"], B["cst"]], writes=[B["contribb"]])
        bk.op("gp", lambda gs: gs.dma_start(out=D["harc"], in_=contrib[:, :]),
              reads=[B["contribb"]], writes=[B["harc"]], sem=bk.S("gq"), inc=16)
        bk.op("gp", lambda gs: gs.collective_compute(
            "AllReduce", ALU.add, replica_groups=PAIRS,
            ins=[D["harc"].opt()], outs=[D["haro"].opt()]),
            reads=[B["harc"]], writes=[B["haro"]], sem=bk.S("cc"), inc=1)
        bk.op("gp", lambda gs: gs.dma_start(out=halo[:, :], in_=D["haro"]),
              reads=[B["haro"]], writes=[B["halo"]], sem="gq", inc=16)
        bk.op("dv", lambda ve: ve.tensor_scalar(
            halo[:, :], halo[:, :], pcm[:, 1:2], None, ALU.mult),
            reads=[B["halo"], B["cst"]], writes=[B["halo"]])
        if stop == "halo":
            return
        ngrp_emit = 1 if (stop == "grp0" or str(stop).startswith("ch")) else NGRP
        for g in range(ngrp_emit):
            t0 = g * GT
            emit_ln(t0, GT)
            gch = {t: [ring_add("sq", (WNM[t], hf)) for hf in range(2)] for t in "qkv"}
            mb_ci = ring_add("miscb", None)
            for ti, t in enumerate("qkv"):
                srcb = halo if g == 0 else carry
                srcbuf = B["halo"] if g == 0 else B["carry"]

                def fci(ve, ti=ti, t=t, srcb=srcb):
                    return ve.tensor_copy(
                        TGB[t][:, :].rearrange("p (f c) -> p f c", f=8)[:, :, 0:3],
                        srcb[:, ti * 24:(ti + 1) * 24].rearrange("p (f c) -> p f c", f=8))
                bk.op("dv", fci, reads=[srcbuf], writes=[TBB[t]])
            for t in "qkv":
                for dc in range(8):
                    slot, slb = ring_use(gch[t][dc // 4])
                    pb = dc % 2
                    mms = []
                    for fc in range(NFC):
                        mms.append((ps1(pb * 512, pb * 512 + GT),
                                    slot[:, fc * 512 + (dc % 4) * 128: fc * 512 + (dc % 4) * 128 + 128],
                                    xn[:, fc * TPC + t0: fc * TPC + t0 + GT],
                                    fc == 0, fc == 7))
                    mmg(bk, mms, reads=[slb] + xnr, writes=bkb(pb))
                    bk.op("ac", lambda sc, t=t, dc=dc, pb=pb: sc.activation(
                        TGB[t][:, dc * 259 + 3: dc * 259 + 3 + GT],
                        ps1(pb * 512, pb * 512 + GT), AF.Copy),
                        reads=bkb(pb), writes=[TBB[t]])
            slot, slb = ring_use(mb_ci)
            mms = []
            for fc in range(NFC):
                mms.append((PS4[0:16, 0:GT], slot[:, 64 + fc * 16: 64 + fc * 16 + 16],
                            xn[:, fc * TPC + t0: fc * TPC + t0 + GT], fc == 0, fc == 7))
            mmg(bk, mms, reads=[slb] + xnr, writes=bkb(7))
            bk.op("ac", lambda sc: sc.activation(brows[0:16, 0:GT], PS4[0:16, 0:GT],
                                                 AF.Sigmoid),
                  reads=bkb(7), writes=[B["brows"]])
            bk.op("ac", lambda sc: sc.activation(brows[0:16, 0:GT],
                                                 brows[0:16, 0:GT], AF.Sqrt),
                  reads=[B["brows"]], writes=[B["brows"]])
            for ti, t in enumerate("qkv"):
                for fc in range(NFC):
                    for k4 in range(4):
                        bk.op("dv", lambda ve, ti=ti, fc=fc, k4=k4: ve.tensor_scalar(
                            cwd[:, k4 * 128:(k4 + 1) * 128], CB("ident", 128),
                            cw[:, fc * 12 + ti * 4 + k4: fc * 12 + ti * 4 + k4 + 1],
                            None, ALU.mult),
                            reads=[B["cst"]], writes=[PpH[0]])
                    mms = [(ps1(1024, 1024 + GT), cwd[:, k4 * 128:(k4 + 1) * 128],
                            TGB[t][:, fc * 259 + k4: fc * 259 + k4 + GT],
                            k4 == 0, k4 == 3) for k4 in range(4)]
                    mmg(bk, mms, reads=[PpH[0], TBB[t]], writes=bkb(2))
                    bk.op("dv", lambda ve, ti=ti, t=t, fc=fc: ve.tensor_copy(
                        carry[:, ti * 24 + fc * 3: ti * 24 + fc * 3 + 3],
                        TGB[t][:, fc * 259 + GT: fc * 259 + GT + 3]),
                        reads=[TBB[t]], writes=[B["carry"]])
                    bk.op("ac", lambda sc, t=t, fc=fc: sc.activation(
                        TGB[t][:, fc * 259 + 3: fc * 259 + 3 + GT], ps1(1024, 1024 + GT),
                        AF.Silu), reads=bkb(2), writes=[TBB[t]])
            bk.op("ac", lambda sc: sc.activation(
                brb[0:16, 512:512 + GT], brows[0:16, 0:GT], AF.Copy),
                reads=[B["brows"]], writes=[B["brb"]])
            for t, rc in (("q", 256), ("k", 0)):
                for fc in range(NFC):
                    par = fc % 2
                    bk.op("ac", lambda sc, t=t, fc=fc, par=par: sc.activation(
                        xsq_t[par][:, 0:GT], TGB[t][:, fc * 259 + 3: fc * 259 + 3 + GT],
                        AF.Square), reads=[TBB[t]], writes=[Bxsq[par]])
                    mmg(bk, [(PS4[0:16, 256:256 + GT], CB("efc", 16, fc * 16),
                              xsq_t[par][:, 0:GT], fc == 0, fc == 7)],
                        reads=[Bxsq[par], B["cst"]], writes=bkb(7))
                bk.op("ac", lambda sc, rc=rc: sc.activation(
                    brows[0:16, rc:rc + GT], PS4[0:16, 256:256 + GT], AF.Sqrt),
                    reads=bkb(7), writes=[B["brows"]])
                bk.op("dv", lambda ve, rc=rc: ve.tensor_scalar(
                    brows[0:16, rc:rc + GT], brows[0:16, rc:rc + GT], 1e-6, None,
                    ALU.max), reads=[B["brows"]], writes=[B["brows"]])
                bk.op("dv", lambda ve, rc=rc: ve.reciprocal(
                    brows[0:16, rc:rc + GT], brows[0:16, rc:rc + GT]),
                    reads=[B["brows"]], writes=[B["brows"]])
            bk.op("dv", lambda ve: ve.tensor_scalar(
                brows[0:16, 256:256 + GT], brows[0:16, 256:256 + GT], 0.125, None,
                ALU.mult), reads=[B["brows"]], writes=[B["brows"]])
            for s_c, d_c in ((256, 0), (0, 256)):
                bk.op("ac", lambda sc, s=s_c, d=d_c: sc.activation(
                    brb[0:16, d:d + GT], brows[0:16, s:s + GT], AF.Copy),
                    reads=[B["brows"]], writes=[B["brb"]])
            for t, rc in (("q", 0), ("k", 256)):
                for fc in range(NFC):
                    pb = fc % 2
                    mmg(bk, [(ps1(pb * 512, pb * 512 + GT),
                              CBr("eft", 0, 16, fc * 128, 128),
                              brb[0:16, rc:rc + GT], True, True)],
                        reads=[B["brb"], B["cst"]], writes=bkb(pb))
                    bk.op("dv", lambda ve, t=t, fc=fc, pb=pb: ve.tensor_mul(
                        TGB[t][:, fc * 259 + 3: fc * 259 + 3 + GT],
                        TGB[t][:, fc * 259 + 3: fc * 259 + 3 + GT],
                        ps1(pb * 512, pb * 512 + GT)),
                        reads=[TBB[t]] + bkb(pb), writes=[TBB[t]])
            if stop == "noch":
                continue
            sub = 99
            if isinstance(stop, str) and stop.startswith("ch"):
                sub = int(stop[2:])
            pend = []
            for ci2 in range(2):
                pend = emit_chunk(g, ci2, sub=sub, pend=pend)
            for f in pend:
                f()

    def emit_corr():
        bk.op("dv", lambda ve: ve.tensor_scalar(
            Phi[0:64, :], Ssb[0:64, :], pcm[0:64, 0:1], None, ALU.mult),
            reads=[B["S"], B["cst"], B["Phi"]], writes=[B["Phi"]])
        bk.op("gp", lambda gs: gs.dma_start(out=D["sarc"], in_=Phi[0:64, :]),
              reads=[B["Phi"]], writes=[B["sarc"]], sem="gq", inc=16)
        bk.op("gp", lambda gs: gs.collective_compute(
            "AllReduce", ALU.add, replica_groups=PAIRS,
            ins=[D["sarc"].opt()], outs=[D["saro"].opt()]),
            reads=[B["sarc"]], writes=[B["saro"]], sem="cc", inc=1)
        bk.op("gp", lambda gs: gs.dma_start(out=Sin[0:64, :], in_=D["saro"]),
              reads=[B["saro"], PpH[0]], writes=[PpH[0]], sem="gq", inc=16)
        bk.op("dv", lambda ve: ve.tensor_scalar(
            Sin[0:64, :], Sin[0:64, :], pcm[0:64, 1:2], None, ALU.mult),
            reads=[PpH[0], B["cst"]], writes=[PpH[0]])
        wo_ci = [ring_add("sq", ("wot", hf)) for hf in range(2)]
        bk.op("dv", lambda ve: ve.engine_nop(),
              reads=[B["qg"], B["kg"], B["vg"], XsH[0], XsH[1]],
              writes=[B["obl"], B["ob2"], B["dbl"], B["db2"], B["onb"], B["onr"]])
        for c in range(NCH):
            par = c % 2
            oc, ocb = (obl, B["obl"]) if par == 0 else (ob2, B["ob2"])
            dc_, dcb = (dbl, B["dbl"]) if par == 0 else (db2, B["db2"])
            bk.op("sp", lambda sp, oc=oc, c=c: sp.dma_start(
                out=oc[:, :], in_=D["ot"][c * 128:(c + 1) * 128, :]),
                reads=[B["ot"]], writes=[ocb], sem=bk.S(f"clo{par}"), inc=16)

            def fdl(sp, dc_=dc_, c=c):
                return sp.dma_start(
                    out=dc_[0:64, :].rearrange("q (h2 blk j) -> q h2 blk j", h2=2, j=128),
                    in_=D["dt"][c * 128:(c + 1) * 128, :]
                    .rearrange("(h2 q) (blk j) -> q h2 blk j", h2=2, j=128))
            bk.op("sp", fdl, reads=[B["dt"]], writes=[dcb], sem=bk.S(f"cld{par}"), inc=16)
            mms = []
            for h in range(NH):
                mms.append((PS2[hp(h):hp(h) + 64, (h // 2) * 128:(h // 2) * 128 + 128],
                            Sin[0:64, h * 64:(h + 1) * 64],
                            dc_[0:64, (h % 2) * 1024 + (h // 2) * 128:(h % 2) * 1024 + (h // 2) * 128 + 128],
                            True, True))
            mmg(bk, mms, reads=[PpH[0], dcb], writes=bkb(4, 5))
            bk.op("dv", lambda ve, oc=oc: ve.tensor_add(onb[:, :], oc[:, :],
                                                        PS2[:, 0:1024]),
                  reads=[ocb] + bkb(4, 5) + [B["onb"]], writes=[B["onb"]])
            bk.op("ac", lambda sc: sc.activation(XK[:, 0:1024], onb[:, :], AF.Square),
                  reads=[B["onb"], XKH[0]], writes=[XKH[0]])
            mms = [(ps1(0, 512)[0:2, :], CB("oneslh", 2), XK[:, 0:512], True, True),
                   (ps1(512, 1024)[0:2, :], CB("oneslh", 2), XK[:, 512:1024], True, True)]
            mmg(bk, mms, reads=[XKH[0], B["cst"]], writes=bkb(0, 1))
            for hv in range(2):
                bk.op("ac", lambda sc, hv=hv: sc.activation(
                    brows[0:2, 0:512], ps1(hv * 512, (hv + 1) * 512)[0:2, :], AF.Sqrt,
                    bias=CFr("eps_a", 0, 2, 0, 1), scale=CFr("inv_hd", 0, 2, 0, 1)),
                    reads=bkb(hv) + [B["cst"], B["brows"]], writes=[B["brows"]])
                bk.op("dv", lambda ve: ve.reciprocal(brows[0:2, 0:512], brows[0:2, 0:512]),
                      reads=[B["brows"]], writes=[B["brows"]])
                bk.op("ac", lambda sc, hv=hv: sc.activation(
                    Xs[0:2, 1024 + hv * 512:1536 + hv * 512], brows[0:2, 0:512], AF.Copy),
                    reads=[B["brows"], B["onr"]], writes=[B["onr"]])
            mms = [(ps1(1024, 1536), CBr("e2", 0, 2, 0, 128), Xs[0:2, 1024:1536], True, True),
                   (ps1(1536, 2048), CBr("e2", 0, 2, 0, 128), Xs[0:2, 1536:2048], True, True)]
            mmg(bk, mms, reads=[B["onr"], B["cst"]], writes=bkb(2, 3))
            bk.op("dv", lambda ve: ve.tensor_mul(onb[:, :], onb[:, :], ps1(1024, 2048)),
                  reads=[B["onb"]] + bkb(2, 3), writes=[B["onb"]])
            slot0, slb0 = ring_use(wo_ci[0])
            slot1, slb1 = ring_use(wo_ci[1])
            mms = []
            for dc2 in range(8):
                slot = slot0 if dc2 < 4 else slot1
                for fci_ in range(8):
                    mms.append((ps1(dc2 * 128, (dc2 + 1) * 128),
                                slot[:, fci_ * 512 + (dc2 % 4) * 128: fci_ * 512 + (dc2 % 4) * 128 + 128],
                                onb[:, fci_ * 128:(fci_ + 1) * 128],
                                fci_ == 0, fci_ == 7))
            mmg(bk, mms, reads=[B["onb"], slb0, slb1], writes=bkb(0, 1))

            def fadd(ve, c=c):
                xo = x_sb[:, :].rearrange("p (fc t) -> p fc t", fc=8)[:, :, c * 128:(c + 1) * 128]
                return ve.tensor_add(xo, xo, ps1(0, 1024).rearrange("p (f t) -> p f t", f=8))
            bk.op("dv", fadd, reads=bkb(0, 1) + [B[f"x{i}"] for i in range(NFC)],
                  writes=[B[f"x{i}"] for i in range(NFC)])


    emit_ffn()
    if stop not in ("ln1", "gates", "ffn1"):
        emit_attention(stop=stop)
        if stop not in ("halo", "grp0", "noch", "groups") and not str(stop).startswith("ch"):
            emit_corr()
        if stop in ("attn", "halo", "grp0", "noch", "groups") or str(stop).startswith("ch"):
            bk.op("sp", lambda sp: sp.dma_start(out=D["dbg"], in_=x_sb[:, :]),
                  reads=[B[f"x{i}"] for i in range(NFC)], sem=bk.S("so"), inc=16)
        else:
            bk.op("dv", lambda ve: ve.engine_nop(),
                  reads=[B["QQb"], B["osb"], B["obl"], B["ob2"], B["onr"],
                         B["dsb"], B["dbl"], B["db2"], B["onb"],
                         PpH[0], PpH[1], PtH[0], PtH[1],
                         XsH[0], XsH[1], XKH[0], XKH[1], B["qnc"]],
                  writes=[Bhgt[0], Bhgt[1], Bh[0], Bh[1], Bgb[0], Bgb[1],
                          B["qg"], B["kg"], B["vg"], B["grow"], XsH[0], XsH[1]])
            emit_ffn()

    if stop == "ffn1":
        bk.op("sp", lambda sp: sp.dma_start(
            out=D["dbg"], in_=x_sb[:, :]),
            reads=[B[f"x{i}"] for i in range(NFC)], sem=bk.S("so"), inc=16)
    elif stop == "ln1":
        bk.op("gp", lambda gs: gs.dma_start(out=D["dbg"], in_=xn[:, :]),
              reads=[B[f"xn{i}"] for i in range(NFC)], sem=bk.S("so"), inc=16)
    elif stop == "gates":
        bk.op("gp", lambda gs: gs.dma_start(out=D["dbg"][0:8, 0:TPC], in_=grow[:, :]),
              reads=[B["grow"]], sem=bk.S("so"), inc=16)

    bk.op("sp", lambda sp: sp.dma_start(
        out=D["y"].rearrange("(fc p) t -> p fc t", p=128),
        in_=x_sb[:, :].rearrange("p (fc t) -> p fc t", fc=8)),
        reads=[B[f"x{i}"] for i in range(NFC)], sem=bk.S("so"), inc=16)
    final_waits = [bk.now(s) for s in list(bk.cnt)]
    bk.op("sp", lambda sp: None, waits=final_waits, sem="so", inc=0)

    sems = {}
    import contextlib as _cl
    semctx = _cl.ExitStack()
    for name in bk.cnt:
        sems[name] = semctx.enter_context(nc.semaphore(name))
    with nc.Block() as block:
        bk.replay(block, sems)
    semctx.close()
    ctx.close()
    return nc



def _stage_inputs(inputs):
    x = np.asarray(inputs["L0_attn_x"], np.float32)
    shared = {
        "w1t": np.ascontiguousarray(
            np.asarray(inputs["L0_ffn_w1"], np.float32).transpose(0, 2, 1)).reshape(8 * 1024, 2048),
        "w2t": np.ascontiguousarray(
            np.asarray(inputs["L0_ffn_w2"], np.float32).transpose(0, 2, 1)).reshape(8 * 2048, 1024),
        "wqt": np.ascontiguousarray(np.asarray(inputs["L0_attn_Wq"], np.float32).T),
        "wkt": np.ascontiguousarray(np.asarray(inputs["L0_attn_Wk"], np.float32).T),
        "wvt": np.ascontiguousarray(np.asarray(inputs["L0_attn_Wv"], np.float32).T),
        "wot": np.ascontiguousarray(np.asarray(inputs["L0_attn_Wo"], np.float32).T),
        "wbt": np.ascontiguousarray(np.asarray(inputs["L0_attn_Wb"], np.float32).T),
        "rwt": np.ascontiguousarray(np.asarray(inputs["L0_ffn_router_w"], np.float32).T),
        "cwt": np.ascontiguousarray(np.concatenate(
            [np.asarray(inputs["L0_attn_conv_q"], np.float32),
             np.asarray(inputs["L0_attn_conv_k"], np.float32),
             np.asarray(inputs["L0_attn_conv_v"], np.float32)], 1)),
        "onw": np.ascontiguousarray(
            np.tile(np.asarray(inputs["L0_attn_o_norm_w"], np.float32), 2))[:, None],
        "cbf": CBF_HOST,
        "cf": CF_HOST,
    }
    ims = []
    for c in range(8):
        b, hf = c // 2, c % 2
        pcmask = np.zeros((128, 2), np.float32)
        pcmask[:, 0] = 1.0 if c % 2 == 0 else 0.0
        pcmask[:, 1] = 1.0 if c % 2 == 1 else 0.0
        im = dict(shared)
        im["xt"] = np.ascontiguousarray(x[b, hf * TPC:(hf + 1) * TPC].T)
        im["pcm"] = pcmask
        ims.append(im)
    return ims


def kernel(**inputs):
    ims = _stage_inputs(inputs)
    nc = build_nc()
    res = run_bass_kernel_spmd(nc, ims, core_ids=list(range(8)))
    out = np.empty((4, 4096, 1024), np.float32)
    for c in range(8):
        b, hf = c // 2, c % 2
        out[b, hf * TPC:(hf + 1) * TPC] = np.asarray(res.results[c]["y"], np.float32).T
    return out


# version 04
# speedup vs baseline: 1.1152x; 1.0343x over previous
import numpy as np
import ml_dtypes
import concourse.bass as bass
import concourse.mybir as mybir
from concourse.bass_utils import run_bass_kernel_spmd

F32 = mybir.dt.float32
BF16 = mybir.dt.bfloat16
AF = mybir.ActivationFunctionType
ALU = mybir.AluOpType
AX = mybir.AxisListType.X
HB = ml_dtypes.bfloat16

NFC = 8
TPC = 2048
GT = 256
NGRP = TPC // GT
CH = 128
NCH = TPC // CH
NH = 16
HD = 64
NE = 8
FTG = 256
NFT = TPC // FTG

ENG2SEM = {"pe": "pe", "ac": "ac", "dv": "dv", "gp": "gp", "sp": "sq"}


class Buf:
    __slots__ = ("w", "rs", "nm")

    def __init__(self, nm=""):
        self.w = None
        self.rs = {}
        self.nm = nm


class BK:
    def __init__(self, nc):
        self.nc = nc
        self.cnt = {}
        self.recs = {"pe": [], "ac": [], "dv": [], "gp": [], "sp": []}

    def S(self, name):
        self.cnt.setdefault(name, 0)
        return name

    def now(self, name):
        return (name, self.cnt.get(name, 0))

    def op(self, eng, fn, reads=(), writes=(), sem=None, inc=1, waits=()):
        w = {}

        def merge(tok):
            if tok is None:
                return
            s, v = tok
            if v > 0 and w.get(s, 0) < v:
                w[s] = v

        for b in reads:
            merge(b.w)
        for b in writes:
            merge(b.w)
            for s, v in b.rs.items():
                merge((s, v))
        for t in waits:
            merge(t)
        sem = sem or ENG2SEM[eng]
        self.S(sem)
        self.cnt[sem] += inc
        tok = (sem, self.cnt[sem])
        wset = set(map(id, writes))
        for b in writes:
            b.w = tok
            b.rs = {}
        for b in reads:
            if id(b) not in wset:
                if b.rs.get(tok[0], 0) < tok[1]:
                    b.rs[tok[0]] = tok[1]
        self.recs[eng].append((sorted(w.items()), fn, (sem, inc)))
        return tok

    def replay(self, block, sems):
        def mk(recs):
            def body(eng):
                lastw = {}
                for waits, fn, (s, n) in recs:
                    for sname, v in waits:
                        if lastw.get(sname, -1) < v:
                            eng.wait_ge(sems[sname], v)
                            lastw[sname] = v
                    inst = fn(eng)
                    if inst is not None and n > 0:
                        inst.then_inc(sems[s], n)
            return body

        block.tensor(mk(self.recs["pe"]))
        block.scalar(mk(self.recs["ac"]))
        block.vector(mk(self.recs["dv"]))
        block.gpsimd(mk(self.recs["gp"]))
        block.sync(mk(self.recs["sp"]))


def mmg(bk, mms, reads, writes, waits=()):
    def fn(te, mms=tuple(mms)):
        inst = None
        for (o, l, r, st, sp) in mms:
            inst = te.matmul(o, l, r, start=st, stop=sp)
        return inst
    return bk.op("pe", fn, reads=reads, writes=writes, waits=waits)


def trg(bk, trs, reads, writes, waits=()):
    def fn(te, trs=tuple(trs)):
        inst = None
        for (o, i, d) in trs:
            inst = te.transpose(o, i, d)
        return inst
    return bk.op("pe", fn, reads=reads, writes=writes, waits=waits)



def _host_consts():
    cbf_cols = {}
    parts = []

    def add(name, arr):
        arr = np.asarray(arr, np.float32)
        a = np.zeros((128, arr.shape[1]), np.float32)
        a[: arr.shape[0]] = arr
        cbf_cols[name] = sum(p.shape[1] for p in parts)
        parts.append(a)

    ident = np.eye(128, dtype=np.float32)
    add("ident", ident)
    t = np.tril(np.ones((128, 128), np.float32), -1)
    add("mask_nl", np.concatenate([-t, -t], 1))
    add("mask_nu", np.concatenate([-t.T, -t.T], 1))
    u = np.triu(np.ones((128, 128), np.float32), 0)
    add("mask_ui", np.concatenate([u, u], 1))
    e8 = np.zeros((8, 8 * 128), np.float32)
    for e in range(8):
        e8[e, e * 128:(e + 1) * 128] = 1.0
    add("e8sel", e8)
    efc = np.zeros((128, 8 * 16), np.float32)
    for fc in range(8):
        for p in range(128):
            efc[p, fc * 16 + 2 * fc + (1 if p >= 64 else 0)] = 1.0
    add("efc", efc)
    eft = np.zeros((16, 8 * 128), np.float32)
    for fc in range(8):
        for p in range(128):
            eft[2 * fc + (1 if p >= 64 else 0), fc * 128 + p] = 1.0
    add("eft", eft)
    add("ones", np.ones((128, 1), np.float32))
    lo = np.zeros((128, 1), np.float32); lo[:64] = 1.0
    hi = np.zeros((128, 1), np.float32); hi[64:] = 1.0
    add("oneslh", np.concatenate([lo, hi], 1))
    e2 = np.zeros((2, 128), np.float32)
    e2[0, :64] = 1.0
    e2[1, 64:] = 1.0
    add("e2", e2)
    phii = np.zeros((64, 16 * 64), np.float32)
    for p in range(64):
        for j in range(16):
            phii[p, j * 64 + p] = 1.0
    add("phiinit", phii)
    cbf = np.concatenate(parts, 1).astype(HB)

    cf_cols = {}
    fparts = []

    def addf(name, arr):
        arr = np.asarray(arr, np.float32)
        a = np.zeros((128, arr.shape[1]), np.float32)
        a[: arr.shape[0]] = arr
        cf_cols[name] = sum(p.shape[1] for p in fparts)
        fparts.append(a)

    addf("identf", ident)
    addf("onesf", np.ones((128, 1), np.float32))
    addf("onesrow", np.ones((1, 128), np.float32))
    addf("eps_ln", np.full((128, 1), 1e-5, np.float32))
    addf("eps_a", np.full((128, 1), 1e-6, np.float32))
    addf("inv_hd", np.full((128, 1), 1.0 / 64.0, np.float32))
    cf = np.concatenate(fparts, 1).astype(np.float32)
    return cbf, cbf_cols, cf, cf_cols


CBF_HOST, CBF_COLS, CF_HOST, CF_COLS = _host_consts()



def build_nc(stop="full"):
    import contextlib
    nc = bass.Bass("TRN2", target_bir_lowering=False)
    D = {}
    D["xt"] = nc.dram_tensor("xt", [1024, TPC], F32, kind="ExternalInput").ap()
    D["w1t"] = nc.dram_tensor("w1t", [8 * 1024, 2048], F32, kind="ExternalInput").ap()
    D["w2t"] = nc.dram_tensor("w2t", [8 * 2048, 1024], F32, kind="ExternalInput").ap()
    for nm in ("wqt", "wkt", "wvt", "wot"):
        D[nm] = nc.dram_tensor(nm, [1024, 1024], F32, kind="ExternalInput").ap()
    D["wbt"] = nc.dram_tensor("wbt", [1024, 16], F32, kind="ExternalInput").ap()
    D["rwt"] = nc.dram_tensor("rwt", [1024, 8], F32, kind="ExternalInput").ap()
    D["cwt"] = nc.dram_tensor("cwt", [1024, 12], F32, kind="ExternalInput").ap()
    D["onw"] = nc.dram_tensor("onw", [128, 1], F32, kind="ExternalInput").ap()
    D["cbf"] = nc.dram_tensor("cbf", list(CBF_HOST.shape), BF16, kind="ExternalInput").ap()
    D["cf"] = nc.dram_tensor("cf", list(CF_HOST.shape), F32, kind="ExternalInput").ap()
    D["pcm"] = nc.dram_tensor("pcm", [128, 2], F32, kind="ExternalInput").ap()
    D["y"] = nc.dram_tensor("y", [1024, TPC], F32, kind="ExternalOutput").ap()
    if stop != "full":
        D["dbg"] = nc.dram_tensor("dbg", [128, NFC * TPC], F32, kind="ExternalOutput").ap()
    D["harc"] = nc.dram_tensor("harc", [128, 72], F32, kind="Internal").ap()
    D["haro"] = nc.dram_tensor("haro", [128, 72], F32, kind="Internal").ap()
    D["sarc"] = nc.dram_tensor("sarc", [64, 1024], F32, kind="Internal").ap()
    D["saro"] = nc.dram_tensor("saro", [64, 1024], F32, kind="Internal").ap()
    D["ot"] = nc.dram_tensor("ot", [TPC, 1024], BF16, kind="Internal").ap()
    D["dt"] = nc.dram_tensor("dt", [TPC, 1024], BF16, kind="Internal").ap()

    bk = BK(nc)
    PAIRS = [[0, 1], [2, 3], [4, 5], [6, 7]]
    ctx = contextlib.ExitStack()
    sb = lambda nm, sh, dt: ctx.enter_context(nc.sbuf_tensor(nm, sh, dt))
    pst = lambda nm, sh, dt: ctx.enter_context(nc.psum_tensor(nm, sh, dt))

    x_sb = sb("x_sb", [128, NFC * TPC], F32)
    xn = sb("xn", [128, NFC * TPC], BF16)
    slots = [sb(f"wsl{i}", [128, 4096], BF16) for i in range(5)]
    cbf = sb("cbfs", [128, CBF_HOST.shape[1]], BF16)
    cf = sb("cfs", [128, CF_HOST.shape[1]], F32)
    cw = sb("cw_s", [128, 96], F32)
    onw = sb("onw_s", [128, 1], F32)
    pcm = sb("pcm_s", [128, 2], F32)
    grow = sb("grow", [8, TPC], BF16)
    lnrow = sb("lnrow", [128, 512], F32)
    hsb_t = [sb(f"hsb{i}", [128, FTG], BF16) for i in range(2)]
    gbb_t = [sb(f"gbb{i}", [128, FTG], BF16) for i in range(2)]
    hgt_t = [sb(f"hgt{i}", [128, 8 * FTG], BF16) for i in range(2)]
    _xsq0 = sb("xsq0", [128, FTG], BF16)
    xsq_t = [_xsq0, _xsq0]

    PS1 = pst("PS1", [128, 2048], F32)
    PS2 = pst("PS2", [128, 1024], F32)
    PS3 = pst("PS3", [128, 1024], BF16)
    PS4 = pst("PS4", [128, 512], F32)

    B = {}
    for i in range(NFC):
        B[f"x{i}"] = Buf(f"x{i}")
        B[f"xn{i}"] = Buf(f"xn{i}")
    for i in range(5):
        B[f"sl{i}"] = Buf(f"sl{i}")
    for nm in ("cst", "grow", "lnrow", "onwb", "cwd",
               "harc", "haro", "sarc", "saro", "ot", "dt"):
        B[nm] = Buf(nm)
    for i in range(8):
        B[f"bk{i}"] = Buf(f"bk{i}")
    Bh = [Buf("h0"), Buf("h1")]
    Bgb = [Buf("gb0"), Buf("gb1")]
    Bhgt = [Buf("hgt0"), Buf("hgt1")]
    _bx = Buf("xsq0")
    Bxsq = [_bx, _bx]

    def bkb(*idx):
        return [B[f"bk{i}"] for i in idx]

    def ps1(c0, c1):
        return PS1[:, c0:c1]

    def CB(name, w=1, c0=0):
        c = CBF_COLS[name]
        return cbf[:, c + c0:c + c0 + w]

    def CBr(name, r0, r1, c0, w):
        c = CBF_COLS[name]
        return cbf[r0:r1, c + c0:c + c0 + w]

    def CF_(name, w=1):
        c = CF_COLS[name]
        return cf[:, c:c + w]

    def CFr(name, r0, r1, c0, w):
        c = CF_COLS[name]
        return cf[r0:r1, c + c0:c + c0 + w]

    for i in range(5):
        B[f"ini{i}"] = Buf(f"ini{i}")
    bk.op("sp", lambda sp: sp.dma_start(out=cbf[:, :], in_=D["cbf"]), writes=[B["ini0"]], inc=16)
    bk.op("sp", lambda sp: sp.dma_start(out=cf[:, :], in_=D["cf"]), writes=[B["ini1"]], inc=16)
    bk.op("sp", lambda sp: sp.dma_start(
        out=cw[:, :].rearrange("p (fc t) -> p fc t", fc=8),
        in_=D["cwt"].rearrange("(fc p) t -> p fc t", p=128)), writes=[B["ini2"]], inc=16)
    bk.op("sp", lambda sp: sp.dma_start(out=onw[:, :], in_=D["onw"]), writes=[B["ini3"]], inc=16)
    bk.op("sp", lambda sp: sp.dma_start(out=pcm[:, :], in_=D["pcm"]), writes=[B["ini4"]], inc=16)
    for i in range(NFC):
        bk.op("sp", lambda sp, i=i: sp.dma_start(
            out=x_sb[:, i * TPC:(i + 1) * TPC],
            in_=D["xt"][i * 128:(i + 1) * 128, :]),
            writes=[B[f"x{i}"]], inc=16)
    init_tok = bk.now("sq")
    for nm in ["cst", "onwb"] + [f"x{i}" for i in range(NFC)]:
        B[nm].w = init_tok

    ring = {"next": 0, "chunks": []}

    def ring_add(kind, arg):
        ring["chunks"].append((kind, arg))
        return len(ring["chunks"]) - 1

    def _load_chunk(ci):
        kind, arg = ring["chunks"][ci]
        sl = ci % 5
        slot = slots[sl]
        sem = bk.S(f"wr{sl}")
        if kind == "w1":
            e, q = arg
            src = D["w1t"][e * 1024:(e + 1) * 1024, q * 512:(q + 1) * 512] \
                .rearrange("(f p) hd -> p f hd", p=128)
            fn = lambda gs, s=slot, src=src: gs.dma_start(
                out=s[:, :].rearrange("p (f hd) -> p f hd", f=8), in_=src)
            n = 1
        elif kind == "w2":
            e, q = arg
            src = D["w2t"][e * 2048 + q * 512: e * 2048 + (q + 1) * 512, :] \
                .rearrange("(h p) fd -> p h fd", p=128)
            fn = lambda gs, s=slot, src=src: gs.dma_start(
                out=s[:, :].rearrange("p (h fd) -> p h fd", h=4), in_=src)
            n = 1
        elif kind == "sq":
            nm, hf = arg
            src = D[nm][:, hf * 512:(hf + 1) * 512] \
                .rearrange("(f p) od -> p f od", p=128)
            fn = lambda gs, s=slot, src=src: gs.dma_start(
                out=s[:, :].rearrange("p (f od) -> p f od", f=8), in_=src)
            n = 1
        elif kind == "miscb":
            fn = lambda gs, s=slot: gs.dma_start(
                out=s[:, 64:192].rearrange("p (f h) -> p f h", f=8),
                in_=D["wbt"].rearrange("(f p) h -> p f h", p=128))
            n = 1
        elif kind == "misc":
            bk.op("gp", lambda gs, s=slot: gs.dma_start(
                out=s[:, 0:64].rearrange("p (f e) -> p f e", f=8),
                in_=D["rwt"].rearrange("(f p) e -> p f e", p=128)),
                writes=[B[f"sl{sl}"]], sem=sem, inc=16)
            fn = lambda gs, s=slot: gs.dma_start(
                out=s[:, 64:192].rearrange("p (f h) -> p f h", f=8),
                in_=D["wbt"].rearrange("(f p) h -> p f h", p=128))
            n = 1
        else:
            raise KeyError(kind)
        bk.op("gp", fn, writes=[B[f"sl{sl}"]], sem=sem, inc=16 * n)
        if kind == "sq" and arg[0] == "wot":
            bk.op("dv", lambda ve, s=slot: ve.tensor_scalar(
                s[:, :], s[:, :], onw[:, 0:1], None, ALU.mult),
                reads=[B[f"sl{sl}"], B["onwb"]], writes=[B[f"sl{sl}"]])

    def ring_use(ci):
        while ring["next"] <= min(ci + 1, len(ring["chunks"]) - 1):
            _load_chunk(ring["next"])
            ring["next"] += 1
        return slots[ci % 5], B[f"sl{ci % 5}"]

    def emit_ln(t0, w):
        for fc in range(NFC):
            xa = x_sb[:, fc * TPC + t0: fc * TPC + t0 + w]
            mmg(bk, [(ps1(1024, 1024 + w)[0:1, :], CF_("onesf"), xa, fc == 0, fc == 7)],
                reads=[B[f"x{fc}"], B["cst"]], writes=bkb(2))
        for fc in range(NFC):
            xa = x_sb[:, fc * TPC + t0: fc * TPC + t0 + w]
            par = fc % 2
            bk.op("ac", lambda sc, o=xsq_t[par][:, 0:w], i=xa: sc.activation(o, i, AF.Square),
                  reads=[B[f"x{fc}"]], writes=[Bxsq[par]])
            mmg(bk, [(ps1(1536, 1536 + w)[0:1, :], CB("ones"), xsq_t[par][:, 0:w],
                      fc == 0, fc == 7)],
                reads=[Bxsq[par], B["cst"]], writes=bkb(3))
        bk.op("dv", lambda ve: ve.tensor_scalar(
            lnrow[0:1, 0:w], ps1(1024, 1024 + w)[0:1, :], 1.0 / 1024.0, None, ALU.mult),
            reads=bkb(2), writes=[B["lnrow"]])
        bk.op("dv", lambda ve: ve.tensor_mul(
            lnrow[0:1, 256:256 + w], lnrow[0:1, 0:w], lnrow[0:1, 0:w]),
            reads=[B["lnrow"]], writes=[B["lnrow"]])
        bk.op("dv", lambda ve: ve.scalar_tensor_tensor(
            lnrow[0:1, 256:256 + w], ps1(1536, 1536 + w)[0:1, :], 1.0 / 1024.0,
            lnrow[0:1, 256:256 + w], ALU.mult, ALU.subtract),
            reads=bkb(3) + [B["lnrow"]], writes=[B["lnrow"]])
        bk.op("ac", lambda sc: sc.activation(
            lnrow[0:1, 256:256 + w], lnrow[0:1, 256:256 + w], AF.Sqrt,
            bias=CFr("eps_ln", 0, 1, 0, 1)),
            reads=[B["lnrow"], B["cst"]], writes=[B["lnrow"]])
        bk.op("dv", lambda ve: ve.reciprocal(lnrow[0:1, 256:256 + w], lnrow[0:1, 256:256 + w]),
              reads=[B["lnrow"]], writes=[B["lnrow"]])
        mmg(bk, [(ps1(0, w), CFr("onesrow", 0, 1, 0, 128), lnrow[0:1, 0:w], True, True)],
            reads=[B["lnrow"], B["cst"]], writes=bkb(0))
        mmg(bk, [(ps1(512, 512 + w), CFr("onesrow", 0, 1, 0, 128), lnrow[0:1, 256:256 + w],
                  True, True)],
            reads=[B["lnrow"], B["cst"]], writes=bkb(1))
        for fc in range(NFC):
            par = fc % 2
            xa = x_sb[:, fc * TPC + t0: fc * TPC + t0 + w]
            xo = xn[:, fc * TPC + t0: fc * TPC + t0 + w]
            bk.op("dv", lambda ve, xa=xa, par=par: ve.tensor_sub(
                xsq_t[par][:, 0:w], xa, ps1(0, w)),
                reads=[B[f"x{fc}"]] + bkb(0), writes=[Bxsq[par]])
            bk.op("dv", lambda ve, xo=xo, par=par: ve.tensor_mul(
                xo, xsq_t[par][:, 0:w], ps1(512, 512 + w)),
                reads=[Bxsq[par]] + bkb(1), writes=[B[f"xn{fc}"]])

    def emit_gates(misc_ci):
        slot, slb = ring_use(misc_ci)
        for ci in range(16):
            t0 = ci * 128
            mms = []
            for fc in range(NFC):
                mms.append((PS4[:, 0:8],
                            xn[:, fc * TPC + t0: fc * TPC + t0 + 128],
                            slot[:, fc * 8: fc * 8 + 8], fc == 0, fc == 7))
            mmg(bk, mms, reads=[B[f"xn{i}"] for i in range(NFC)] + [slb], writes=bkb(7))
            bk.op("dv", lambda ve: ve.tensor_copy(lgt[:, 0:8], PS4[:, 0:8]),
                  reads=bkb(7), writes=[B["lgt"]])

            def st(fn):
                bk.op("dv", fn, reads=[B["lgt"]], writes=[B["lgt"]])
            st(lambda ve: ve.reduce_max(lgt[:, 8:9], lgt[:, 0:8], axis=AX))
            st(lambda ve: ve.tensor_scalar(lgt[:, 16:24], lgt[:, 0:8], lgt[:, 8:9],
                                           None, ALU.is_equal))
            st(lambda ve: ve.scalar_tensor_tensor(lgt[:, 24:32], lgt[:, 16:24], -1e30,
                                                  lgt[:, 0:8], ALU.mult, ALU.add))
            st(lambda ve: ve.reduce_max(lgt[:, 9:10], lgt[:, 24:32], axis=AX))
            st(lambda ve: ve.tensor_sub(lgt[:, 10:11], lgt[:, 9:10], lgt[:, 8:9]))
            bk.op("ac", lambda sc: sc.activation(lgt[:, 11:12], lgt[:, 10:11], AF.Exp),
                  reads=[B["lgt"]], writes=[B["lgt"]])
            st(lambda ve: ve.tensor_scalar(lgt[:, 12:13], lgt[:, 11:12], 1.0, None, ALU.add))
            st(lambda ve: ve.reciprocal(lgt[:, 13:14], lgt[:, 12:13]))
            st(lambda ve: ve.tensor_scalar(lgt[:, 14:15], lgt[:, 13:14], 0.5, None, ALU.mult))
            st(lambda ve: ve.tensor_scalar(lgt[:, 15:16], lgt[:, 14:15], -1.0, 0.5,
                                           ALU.mult, ALU.add))
            st(lambda ve: ve.tensor_scalar(lgt[:, 32:40], lgt[:, 24:32], lgt[:, 9:10],
                                           None, ALU.is_equal))
            st(lambda ve: ve.tensor_scalar(lgt[:, 40:48], lgt[:, 16:24], lgt[:, 14:15],
                                           None, ALU.mult))
            st(lambda ve: ve.scalar_tensor_tensor(lgt[:, 40:48], lgt[:, 32:40],
                                                  lgt[:, 15:16], lgt[:, 40:48],
                                                  ALU.mult, ALU.add))
            trg(bk, [(PS4[0:8, 256:384], lgt[:, 40:48], CF_("identf", 128))],
                reads=[B["lgt"], B["cst"]], writes=bkb(7))
            bk.op("ac", lambda sc, t0=t0: sc.activation(
                grow[0:8, t0:t0 + 128], PS4[0:8, 256:384], AF.Copy),
                reads=bkb(7), writes=[B["grow"]])

    def emit_ffn():
        misc_ci = ring_add("misc", None)
        for tg in range(NFT):
            emit_ln(tg * FTG, FTG)
        emit_gates(misc_ci)
        ch = {}
        for e in range(NE):
            for q in range(4):
                ch[(e, "w1", q)] = ring_add("w1", (e, q))
                ch[(e, "w2", q)] = ring_add("w2", (e, q))
        xn_reads = [B[f"xn{i}"] for i in range(NFC)]
        for e in range(NE):
            for hcg in range(2):
                for tg in range(NFT):
                    gpar = tg % 2
                    mmg(bk, [(PS4[:, 0:FTG], CBr("e8sel", 0, 8, e * 128, 128),
                              grow[0:8, tg * FTG:(tg + 1) * FTG], True, True)],
                        reads=[B["grow"], B["cst"]], writes=bkb(7))
                    bk.op("ac", lambda sc, gpar=gpar: sc.activation(
                        gbb_t[gpar][:, :], PS4[:, 0:FTG], AF.Copy),
                        reads=bkb(7), writes=[Bgb[gpar]])
                    hpar = tg % 2
                    for i in range(8):
                        hc = hcg * 8 + i
                        slot, slb = ring_use(ch[(e, "w1", hc // 4)])
                        wpar = i % 2
                        c0 = (hc % 4) * 128
                        mms = []
                        for fc in range(NFC):
                            mms.append((PS2[:, wpar * 512: wpar * 512 + FTG],
                                        slot[:, fc * 512 + c0: fc * 512 + c0 + 128],
                                        xn[:, fc * TPC + tg * FTG: fc * TPC + (tg + 1) * FTG],
                                        fc == 0, fc == 7))
                        mmg(bk, mms, reads=[slb] + xn_reads, writes=bkb(4 + wpar))
                        bk.op("ac", lambda sc, wpar=wpar: sc.activation(
                            hsb_t[wpar][:, :], PS2[:, wpar * 512: wpar * 512 + FTG], AF.Gelu),
                            reads=bkb(4 + wpar), writes=[Bh[wpar]])
                        bk.op("gp", lambda gs, wpar=wpar, i=i, hpar=hpar, gpar=gpar:
                              gs.tensor_mul(hgt_t[hpar][:, i * FTG:(i + 1) * FTG],
                                            hsb_t[wpar][:, :], gbb_t[gpar][:, :]),
                              reads=[Bh[wpar], Bgb[gpar]], writes=[Bhgt[hpar]])
                    for s in range(4):
                        bp = (s % 2) * 2
                        slota, slba = ring_use(ch[(e, "w2", hcg * 2)])
                        slotb, slbb = ring_use(ch[(e, "w2", hcg * 2 + 1)])
                        mms = []
                        for i in range(8):
                            hc = hcg * 8 + i
                            slot = slota if i < 4 else slotb
                            c0 = (hc % 4) * 1024
                            for j in range(2):
                                fc = s * 2 + j
                                mms.append((ps1((bp + j) * 512, (bp + j) * 512 + FTG),
                                            slot[:, c0 + fc * 128: c0 + fc * 128 + 128],
                                            hgt_t[hpar][:, i * FTG:(i + 1) * FTG],
                                            i == 0, i == 7))
                        mmg(bk, mms, reads=[slba, slbb, Bhgt[hpar]], writes=bkb(bp, bp + 1))

                        def fev(ve, s=s, bp=bp, tg=tg):
                            xo = x_sb[:, :].rearrange("p (fc t) -> p fc t", fc=8)[
                                :, s * 2: s * 2 + 2, tg * FTG:(tg + 1) * FTG]
                            pin = ps1(bp * 512, (bp + 2) * 512).rearrange(
                                "p (j t) -> p j t", j=2)[:, :, 0:FTG]
                            return ve.tensor_add(xo, xo, pin)
                        bk.op("dv", fev,
                              reads=bkb(bp, bp + 1) + [B[f"x{s * 2}"], B[f"x{s * 2 + 1}"]],
                              writes=[B[f"x{s * 2}"], B[f"x{s * 2 + 1}"]])

    qg = sb("qg", [128, 8 * 259], BF16)
    kg = sb("kg", [128, 8 * 259], BF16)
    vg = sb("vg", [128, 8 * 259], BF16)
    XK = sb("XK", [128, 2048], BF16)
    Xs = sb("Xs", [128, 2048], BF16)
    Pp = sb("Pp", [128, 2048], BF16)
    Pt = sb("Pt", [128, 2048], BF16)

    Ssb = sb("Ssb", [64, 1024], F32)
    Phi = sb("Phi", [64, 1024], F32)
    qnc = sb("qnc", [64, 2048], BF16)
    brows = sb("brows", [128, 512], F32)
    brb = sb("brb", [128, 768], BF16)
    contrib = nc.alloc_sbuf_tensor_at("contribov", [128, 72], F32,
                                      offset=nc.lookup_mloc(hsb_t[0]).addr)
    halo = nc.alloc_sbuf_tensor_at("haloov", [128, 72], F32,
                                   offset=nc.lookup_mloc(hsb_t[1]).addr)
    carry = nc.alloc_sbuf_tensor_at("carryov", [128, 72], BF16,
                                    offset=nc.lookup_mloc(gbb_t[0]).addr)
    sqtm = nc.alloc_sbuf_tensor_at("sqtmov", [128, 16], F32,
                                   offset=nc.lookup_mloc(gbb_t[1]).addr)
    a_h0 = nc.lookup_mloc(hgt_t[0]).addr
    a_h1 = nc.lookup_mloc(hgt_t[1]).addr
    assert a_h1 == a_h0 + 4096, (a_h0, a_h1)
    QQ = nc.alloc_sbuf_tensor_at("QQov", [64, 2048], F32, offset=a_h0)
    a_pp = nc.lookup_mloc(Pp).addr
    a_pt = nc.lookup_mloc(Pt).addr
    a_xs = nc.lookup_mloc(Xs).addr
    kbc = nc.alloc_sbuf_tensor_at("kbcov", [64, 2048], BF16, offset=a_h0)
    Sin = nc.alloc_sbuf_tensor_at("Sinov", [64, 1024], BF16, offset=a_pp)
    cwd = nc.alloc_sbuf_tensor_at("cwdov", [128, 512], BF16, offset=a_pp)
    lgt = nc.alloc_sbuf_tensor_at("lgtov", [128, 48], F32, offset=a_pp + 2048)
    Mb = nc.alloc_sbuf_tensor_at("Mbov", [128, 2048], BF16, offset=a_pt)
    HTn = nc.alloc_sbuf_tensor_at("HTnov", [64, 1024], F32, offset=a_pt)
    qtm = nc.alloc_sbuf_tensor_at("qtmov", [128, 1024], BF16, offset=a_xs)
    aq = nc.lookup_mloc(qg).addr
    ak = nc.lookup_mloc(kg).addr
    av = nc.lookup_mloc(vg).addr
    ag = nc.lookup_mloc(grow).addr
    osb = nc.alloc_sbuf_tensor_at("osb", [128, 1024], BF16, offset=ag)
    dsb = nc.alloc_sbuf_tensor_at("dsb", [128, 1024], BF16, offset=ag + 2048)
    dbl = nc.alloc_sbuf_tensor_at("dbl", [64, 2048], BF16, offset=aq)
    db2 = nc.alloc_sbuf_tensor_at("db2", [64, 2048], BF16, offset=ak)
    obl = nc.alloc_sbuf_tensor_at("obl", [128, 1024], BF16, offset=av)
    ob2 = nc.alloc_sbuf_tensor_at("ob2", [128, 1024], BF16, offset=av + 2048)
    onb = nc.alloc_sbuf_tensor_at("onb", [128, 1024], BF16, offset=a_xs)

    for nm in ("qg", "kg", "vg", "QQb",
               "S", "Phi", "Sin", "qnc",
               "brb", "brows", "osb", "obl", "ob2", "dsb", "dbl", "db2", "onb",
               "onr"):
        B[nm] = Buf(nm)
    PpH = [Buf("PpH0"), Buf("PpH1")]
    PtH = [Buf("PtH0"), Buf("PtH1")]
    XsH = [Buf("XsH0"), Buf("XsH1")]
    XKH = [Buf("XKH0"), Buf("XKH1")]
    B["lgt"] = PpH[1]
    B["contribb"] = Bh[0]
    B["halo"] = Bh[1]
    B["carry"] = Bgb[0]
    B["sqtmb"] = Bgb[1]

    def hp(h):
        return (h % 2) * 64

    TGB = {"q": qg, "k": kg, "v": vg}
    TBB = {"q": B["qg"], "k": B["kg"], "v": B["vg"]}
    WNM = {"q": "wqt", "k": "wkt", "v": "wvt"}

    def emit_chunk(g, ci2, sub=99, pend=None):
        pend = list(pend or ())

        def drain(n=1):
            for _ in range(min(n, len(pend))):
                pend.pop(0)()
        c = g * 2 + ci2
        base = 3 + ci2 * CH
        cb0 = ci2 * CH
        trg(bk, [(PS3[0:128, 0:16], brb[0:16, 512 + cb0:512 + cb0 + CH],
                  CBr("ident", 0, 16, 0, 16))],
            reads=[B["brb"], B["cst"]], writes=bkb(6))
        bk.op("ac", lambda sc: sc.activation(sqtm[:, 0:16], PS3[0:128, 0:16], AF.Copy),
              reads=bkb(6), writes=[B["sqtmb"]])
        drain(2)
        trs = [(PS3[:, fc * 128:(fc + 1) * 128],
                kg[:, fc * 259 + base: fc * 259 + base + CH], CB("ident", 128))
               for fc in range(NFC)]
        trg(bk, trs, reads=[B["kg"], B["cst"]], writes=bkb(6))
        drain(3)

        def ft1(ve):
            inst = None
            for h in range(NH):
                inst = ve.tensor_scalar(
                    XK[:, h * 128 + 64:(h + 1) * 128],
                    PS3[:, (h // 2) * 128 + (h % 2) * 64:(h // 2) * 128 + (h % 2) * 64 + 64],
                    sqtm[:, h:h + 1], None, ALU.mult)
            return inst
        bk.op("dv", ft1, reads=bkb(6) + [B["sqtmb"]], writes=[XKH[0], XKH[1]])
        drain()
        trs = [(PS3[:, fc * 128:(fc + 1) * 128],
                vg[:, fc * 259 + base: fc * 259 + base + CH], CB("ident", 128))
               for fc in range(NFC)]
        trg(bk, trs, reads=[B["vg"], B["cst"]], writes=bkb(6))

        def ft2(ve):
            inst = None
            for h in range(NH):
                inst = ve.tensor_scalar(
                    XK[:, h * 128:h * 128 + 64],
                    PS3[:, (h // 2) * 128 + (h % 2) * 64:(h // 2) * 128 + (h % 2) * 64 + 64],
                    sqtm[:, h:h + 1], None, ALU.mult)
            return inst
        bk.op("dv", ft2, reads=bkb(6) + [B["sqtmb"]], writes=[XKH[0], XKH[1]])
        drain()
        for r in range(2):
            trs = [(PS3[0:64, i * 128:(i + 1) * 128],
                    XK[:, (r * 8 + i) * 128 + 64:(r * 8 + i) * 128 + 128],
                    CB("ident", 128)) for i in range(8)]
            trg(bk, trs, reads=[XKH[r], B["cst"]], writes=bkb(6))
            bk.op("ac", lambda sc, r=r: sc.activation(
                kbc[0:64, r * 1024:(r + 1) * 1024], PS3[0:64, 0:1024], AF.Copy),
                reads=bkb(6), writes=[B["QQb"]])
            drain()
        trs = [(PS3[:, fc * 128:(fc + 1) * 128],
                qg[:, fc * 259 + base: fc * 259 + base + CH], CB("ident", 128))
               for fc in range(NFC)]
        trg(bk, trs, reads=[B["qg"], B["cst"]], writes=bkb(6))
        bk.op("dv", lambda ve: ve.tensor_copy(qtm[:, :], PS3[:, 0:1024]),
              reads=bkb(6), writes=[XsH[0]])
        drain()
        for r in range(2):
            trs = [(PS3[0:64, i * 128:(i + 1) * 128],
                    qtm[:, (r * 8 + i) * 64:(r * 8 + i) * 64 + 64],
                    CB("ident", 128)) for i in range(8)]
            trg(bk, trs, reads=[XsH[0], B["cst"]], writes=bkb(6))
            eng = "ac" if r == 0 else "dv"
            def ev5(e, r=r):
                if r == 0:
                    return e.activation(qnc[0:64, 0:1024], PS3[0:64, 0:1024], AF.Copy)
                return e.tensor_copy(qnc[0:64, 1024:2048], PS3[0:64, 0:1024])
            bk.op(eng, ev5, reads=bkb(6), writes=[B["qnc"]])
            drain()
        drain(99)
        if sub <= 1:
            return []
        for half in range(2):
            w0 = half * 1024
            mms = []
            for h in range(half * 8, half * 8 + 8):
                kb_h = kbc[0:64, h * 128:(h + 1) * 128]
                mms.append((ps1(w0 + (h % 8) * 128, w0 + (h % 8) * 128 + 128),
                            kb_h, kb_h, True, True))
            mmg(bk, mms, reads=[B["QQb"]], writes=bkb(half * 2, half * 2 + 1))
            for j4 in range(4):
                j = half * 4 + j4
                bk.op("dv", lambda ve, j=j: ve.tensor_mul(
                    Pp[:, j * 256:(j + 1) * 256], ps1(j * 256, (j + 1) * 256),
                    CB("mask_nl", 256)),
                    reads=bkb(j // 2) + [B["cst"], PpH[half]], writes=[PpH[half]])
                bk.op("dv", lambda ve, j=j: ve.tensor_mul(
                    Pt[:, j * 256:(j + 1) * 256], ps1(j * 256, (j + 1) * 256),
                    CB("mask_nu", 256)),
                    reads=bkb(j // 2) + [PtH[half]], writes=[PtH[half]])
        if sub <= 2:
            return []

        def wsl(i, c0, c1):
            if i == 0:
                return ps1(c0, c1)
            if i == 1:
                return ps1(1024 + c0, 1024 + c1)
            return PS2[:, c0:c1]

        def wbf(i):
            return bkb(2 * i, 2 * i + 1)
        wst = {"i": 0, "e": 0}

        def wnext():
            i = wst["i"] % 3
            wst["i"] += 1
            return i

        def enext():
            e = ("ac", "dv")[wst["e"] % 2]
            wst["e"] += 1
            return e

        def apply_lev(lev):
            if lev == 0:
                src_t, src_b = XK, XKH
            else:
                src_t, src_b = Xs, XsH
            wi = wnext()
            mms = []
            for h in range(8):
                mms.append((wsl(wi, (h % 8) * 128, (h % 8) * 128 + 128),
                            Pt[:, h * 128:(h + 1) * 128],
                            src_t[:, h * 128:(h + 1) * 128], True, True))
            mmg(bk, mms, reads=[src_b[0], PtH[0]], writes=wbf(wi))
            wj = wnext()
            mms = []
            for h in range(8, 16):
                mms.append((wsl(wj, (h % 8) * 128, (h % 8) * 128 + 128),
                            CB("ident", 128),
                            src_t[:, h * 128:(h + 1) * 128], True, False))
                mms.append((wsl(wj, (h % 8) * 128, (h % 8) * 128 + 128),
                            Pt[:, h * 128:(h + 1) * 128],
                            src_t[:, h * 128:(h + 1) * 128], False, True))
            mmg(bk, mms, reads=[src_b[1], PtH[1], B["cst"]], writes=wbf(wj))
            bk.op("dv", lambda ve, src_t=src_t, wi=wi: ve.tensor_add(
                Xs[:, 0:1024], src_t[:, 0:1024], wsl(wi, 0, 1024)),
                reads=wbf(wi) + [src_b[0]], writes=[XsH[0]])
            bk.op("ac", lambda sc, wj=wj: sc.activation(
                Xs[:, 1024:2048], wsl(wj, 0, 1024), AF.Copy),
                reads=wbf(wj), writes=[XsH[1]])
        apply_lev(0)
        for lev in range(1, 7):
            for half in range(2):
                hs = list(range(half * 8, half * 8 + 8))
                wi = wnext()
                mms = []
                for h in hs:
                    mms.append((wsl(wi, (h % 8) * 128, (h % 8) * 128 + 128),
                                Pt[:, h * 128:(h + 1) * 128],
                                Pp[:, h * 128:(h + 1) * 128], True, True))
                mmg(bk, mms, reads=[PpH[half], PtH[half]], writes=wbf(wi))
                if lev < 6:
                    wj = wnext()
                    mms = []
                    for h in hs:
                        mms.append((wsl(wj, (h % 8) * 128, (h % 8) * 128 + 128),
                                    Pp[:, h * 128:(h + 1) * 128],
                                    Pt[:, h * 128:(h + 1) * 128], True, True))
                    mmg(bk, mms, reads=[PpH[half], PtH[half]], writes=wbf(wj))
                engp = enext()

                def evp(e, half=half, wi=wi, engp=engp):
                    if engp == "ac":
                        return e.activation(Pp[:, half * 1024:(half + 1) * 1024],
                                            wsl(wi, 0, 1024), AF.Copy)
                    return e.tensor_copy(Pp[:, half * 1024:(half + 1) * 1024],
                                         wsl(wi, 0, 1024))
                bk.op(engp, evp, reads=wbf(wi) + [PpH[half]], writes=[PpH[half]])
                if lev < 6:
                    engt = enext()

                    def evt(e, half=half, wj=wj, engt=engt):
                        if engt == "ac":
                            return e.activation(Pt[:, half * 1024:(half + 1) * 1024],
                                                wsl(wj, 0, 1024), AF.Copy)
                        return e.tensor_copy(Pt[:, half * 1024:(half + 1) * 1024],
                                             wsl(wj, 0, 1024))
                    bk.op(engt, evt, reads=wbf(wj) + [PtH[half]], writes=[PtH[half]])
            if lev == 6:
                for half in range(2):
                    hs = list(range(half * 8, half * 8 + 8))
                    trs = [(PS3[:, (h % 8) * 128:((h % 8) + 1) * 128],
                            Pp[:, h * 128:(h + 1) * 128], CB("ident", 128))
                           for h in hs]
                    trg(bk, trs, reads=[PpH[half], B["cst"]], writes=bkb(6))
                    engt = "ac" if half == 0 else "dv"

                    def evt6(e, half=half, engt=engt):
                        if engt == "ac":
                            return e.activation(Pt[:, half * 1024:(half + 1) * 1024],
                                                PS3[:, 0:1024], AF.Copy)
                        return e.tensor_copy(Pt[:, half * 1024:(half + 1) * 1024],
                                             PS3[:, 0:1024])
                    bk.op(engt, evt6, reads=bkb(6) + [PtH[half]], writes=[PtH[half]])
            apply_lev(lev)
        if sub <= 3:
            return []
        for half in range(2):
            w0 = half * 1024
            mms = []
            for h in range(half * 8, half * 8 + 8):
                mms.append((ps1(w0 + (h % 8) * 128, w0 + (h % 8) * 128 + 128),
                            kbc[0:64, h * 128:(h + 1) * 128],
                            qnc[0:64, h * 128:(h + 1) * 128], True, True))
            mmg(bk, mms, reads=[B["QQb"], B["qnc"]], writes=bkb(half * 2, half * 2 + 1))
            for j4 in range(4):
                j = half * 4 + j4
                bk.op("dv", lambda ve, j=j: ve.tensor_mul(
                    Mb[:, j * 256:(j + 1) * 256], ps1(j * 256, (j + 1) * 256),
                    CB("mask_ui", 256)),
                    reads=bkb(j // 2) + [B["cst"], PtH[half]], writes=[PtH[half]])
        if sub <= 4:
            return []
        mms = []
        for h in range(NH):
            mms.append((PS2[hp(h):hp(h) + 64, (h // 2) * 128:(h // 2) * 128 + 128],
                        Xs[:, h * 128:h * 128 + 64], Mb[:, h * 128:(h + 1) * 128],
                        True, True))
        mmg(bk, mms, reads=[XsH[0], XsH[1], PtH[0], PtH[1]], writes=bkb(4, 5))
        for half in range(2):
            w0 = half * 1024
            mms = []
            for h in range(half * 8, half * 8 + 8):
                mms.append((ps1(w0 + (h % 8) * 128, w0 + (h % 8) * 128 + 128)[0:64, :],
                            Xs[:, h * 128 + 64:(h + 1) * 128], Mb[:, h * 128:(h + 1) * 128],
                            True, True))
            mmg(bk, mms, reads=[XsH[half], PtH[half]], writes=bkb(half * 2, half * 2 + 1))
            bk.op("dv", lambda ve, w0=w0: ve.tensor_sub(
                QQ[0:64, w0:w0 + 1024], qnc[0:64, w0:w0 + 1024],
                ps1(w0, w0 + 1024)[0:64, :]),
                reads=[B["qnc"]] + bkb(half * 2, half * 2 + 1), writes=[B["QQb"]])
        if sub <= 5:
            return []
        mms = []
        for h in range(NH):
            mms.append((ps1(h * 64, (h + 1) * 64)[0:64, :],
                        Xs[:, h * 128 + 64:(h + 1) * 128], XK[:, h * 128 + 64:(h + 1) * 128],
                        True, True))
        mmg(bk, mms, reads=[XsH[0], XsH[1], XKH[0], XKH[1]], writes=bkb(0, 1))
        bk.op("dv", lambda ve: ve.tensor_scalar(
            HTn[0:64, :], ps1(0, 1024)[0:64, :], -1.0, None, ALU.mult),
            reads=bkb(0, 1) + [PtH[0], PtH[1]], writes=[PtH[0], PtH[1]])

        tail = []

        def t_oseq():
            mms = []
            for h in range(NH):
                mms.append((ps1(1024 + (h // 2) * 128, 1024 + (h // 2) * 128 + 128)[hp(h):hp(h) + 64, :],
                            Ssb[0:64, h * 64:(h + 1) * 64],
                            QQ[0:64, h * 128:(h + 1) * 128],
                            True, True))
            mmg(bk, mms, reads=[B["S"], B["QQb"]], writes=bkb(2, 3))
        tail.append(t_oseq)

        def t_d():
            mms = []
            for h in range(NH):
                mms.append((ps1((h // 2) * 128, (h // 2) * 128 + 128)[hp(h):hp(h) + 64, :],
                            Phi[0:64, h * 64:(h + 1) * 64],
                            QQ[0:64, h * 128:(h + 1) * 128],
                            True, True))
            mmg(bk, mms, reads=[B["Phi"], B["QQb"]], writes=bkb(0, 1))
        tail.append(t_d)
        tail.append(lambda: bk.op("ac", lambda sc: sc.activation(
            osb[:, :], PS2[:, 0:1024], AF.Copy),
            reads=bkb(4, 5), writes=[B["osb"]]))
        tail.append(lambda: bk.op("dv", lambda ve: ve.tensor_add(
            osb[:, :], osb[:, :], ps1(1024, 2048)),
            reads=bkb(2, 3) + [B["osb"]], writes=[B["osb"]]))

        def t_g0():
            mms = []
            for h in range(NH):
                mms.append((PS2[0:64, h * 64:(h + 1) * 64],
                            XK[:, h * 128 + 64:(h + 1) * 128], Xs[:, h * 128:h * 128 + 64],
                            True, True))
            mmg(bk, mms, reads=[XKH[0], XKH[1], XsH[0], XsH[1]], writes=bkb(4, 5))
        tail.append(t_g0)
        tail.append(lambda: bk.op("sp", lambda sp, c=c: sp.dma_start(
            out=D["ot"][c * 128:(c + 1) * 128, :], in_=osb[:, :]),
            reads=[B["osb"]], writes=[B["ot"]], sem=bk.S("svo"), inc=16))
        tail.append(lambda: bk.op("dv", lambda ve: ve.tensor_copy(
            dsb[:, :], ps1(0, 1024)),
            reads=bkb(0, 1), writes=[B["dsb"]]))
        tail.append(lambda: bk.op("sp", lambda sp, c=c: sp.dma_start(
            out=D["dt"][c * 128:(c + 1) * 128, :], in_=dsb[:, :]),
            reads=[B["dsb"]], writes=[B["dt"]], sem=bk.S("svd"), inc=16))

        def t_ds():
            mms = []
            for h in range(NH):
                mms.append((ps1(h * 64, (h + 1) * 64)[0:64, :],
                            HTn[0:64, h * 64:(h + 1) * 64],
                            Ssb[0:64, h * 64:(h + 1) * 64],
                            True, True))
            mmg(bk, mms, reads=[PtH[0], PtH[1], B["S"]], writes=bkb(0, 1))
        tail.append(t_ds)

        def t_dphi():
            mms = []
            for h in range(NH):
                mms.append((ps1(1024 + h * 64, 1024 + (h + 1) * 64)[0:64, :],
                            HTn[0:64, h * 64:(h + 1) * 64],
                            Phi[0:64, h * 64:(h + 1) * 64],
                            True, True))
            mmg(bk, mms, reads=[PtH[0], PtH[1], B["Phi"]], writes=bkb(2, 3))
        tail.append(t_dphi)
        tail.append(lambda: bk.op("dv", lambda ve: ve.tensor_add(
            Ssb[0:64, :], Ssb[0:64, :], PS2[0:64, 0:1024]),
            reads=bkb(4, 5) + [B["S"]], writes=[B["S"]]))
        tail.append(lambda: bk.op("dv", lambda ve: ve.tensor_add(
            Ssb[0:64, :], Ssb[0:64, :], ps1(0, 1024)[0:64, :]),
            reads=bkb(0, 1) + [B["S"]], writes=[B["S"]]))
        tail.append(lambda: bk.op("dv", lambda ve: ve.tensor_add(
            Phi[0:64, :], Phi[0:64, :], ps1(1024, 2048)[0:64, :]),
            reads=bkb(2, 3) + [B["Phi"]], writes=[B["Phi"]]))
        if sub < 99:
            for f in tail:
                f()
            return []
        return tail

    def emit_attention(stop="full"):
        xnr = [B[f"xn{i}"] for i in range(NFC)]
        bk.op("dv", lambda ve: ve.engine_nop(),
              reads=[Bhgt[0], Bhgt[1], B["grow"]],
              writes=[B["QQb"], B["osb"], B["dsb"]])
        bk.op("dv", lambda ve: ve.memset(Ssb[:, :], 0.0), writes=[B["S"]])
        bk.op("dv", lambda ve: ve.tensor_copy(
            Phi[0:64, :], CBr("phiinit", 0, 64, 0, 1024)), reads=[B["cst"]], writes=[B["Phi"]])
        mini = {t: [ring_add("sq", (WNM[t], hf)) for hf in range(2)] for t in "qkv"}
        emit_ln(TPC - 3, 3)
        for ti, t in enumerate("qkv"):
            for dc in range(8):
                slot, slb = ring_use(mini[t][dc // 4])
                mms = []
                for fc in range(NFC):
                    mms.append((PS4[:, 0:3],
                                slot[:, fc * 512 + (dc % 4) * 128: fc * 512 + (dc % 4) * 128 + 128],
                                xn[:, fc * TPC + TPC - 3: fc * TPC + TPC],
                                fc == 0, fc == 7))
                mmg(bk, mms, reads=[slb] + xnr, writes=bkb(7))
                bk.op("ac", lambda sc, ti=ti, dc=dc: sc.activation(
                    contrib[:, ti * 24 + dc * 3: ti * 24 + dc * 3 + 3], PS4[:, 0:3],
                    AF.Copy), reads=bkb(7), writes=[B["contribb"]])
        bk.op("dv", lambda ve: ve.tensor_scalar(
            contrib[:, :], contrib[:, :], pcm[:, 0:1], None, ALU.mult),
            reads=[B["contribb"], B["cst"]], writes=[B["contribb"]])
        bk.op("gp", lambda gs: gs.dma_start(out=D["harc"], in_=contrib[:, :]),
              reads=[B["contribb"]], writes=[B["harc"]], sem=bk.S("gq"), inc=16)
        bk.op("gp", lambda gs: gs.collective_compute(
            "AllReduce", ALU.add, replica_groups=PAIRS,
            ins=[D["harc"].opt()], outs=[D["haro"].opt()]),
            reads=[B["harc"]], writes=[B["haro"]], sem=bk.S("cc"), inc=1)
        bk.op("gp", lambda gs: gs.dma_start(out=halo[:, :], in_=D["haro"]),
              reads=[B["haro"]], writes=[B["halo"]], sem="gq", inc=16)
        bk.op("dv", lambda ve: ve.tensor_scalar(
            halo[:, :], halo[:, :], pcm[:, 1:2], None, ALU.mult),
            reads=[B["halo"], B["cst"]], writes=[B["halo"]])
        if stop == "halo":
            return
        ngrp_emit = 1 if (stop == "grp0" or str(stop).startswith("ch")) else NGRP
        for g in range(ngrp_emit):
            t0 = g * GT
            emit_ln(t0, GT)
            gch = {t: [ring_add("sq", (WNM[t], hf)) for hf in range(2)] for t in "qkv"}
            mb_ci = ring_add("miscb", None)
            for ti, t in enumerate("qkv"):
                srcb = halo if g == 0 else carry
                srcbuf = B["halo"] if g == 0 else B["carry"]

                def fci(ve, ti=ti, t=t, srcb=srcb):
                    return ve.tensor_copy(
                        TGB[t][:, :].rearrange("p (f c) -> p f c", f=8)[:, :, 0:3],
                        srcb[:, ti * 24:(ti + 1) * 24].rearrange("p (f c) -> p f c", f=8))
                bk.op("dv", fci, reads=[srcbuf], writes=[TBB[t]])
            for t in "qkv":
                for dc in range(8):
                    slot, slb = ring_use(gch[t][dc // 4])
                    pb = dc % 2
                    mms = []
                    for fc in range(NFC):
                        mms.append((ps1(pb * 512, pb * 512 + GT),
                                    slot[:, fc * 512 + (dc % 4) * 128: fc * 512 + (dc % 4) * 128 + 128],
                                    xn[:, fc * TPC + t0: fc * TPC + t0 + GT],
                                    fc == 0, fc == 7))
                    mmg(bk, mms, reads=[slb] + xnr, writes=bkb(pb))
                    bk.op("ac", lambda sc, t=t, dc=dc, pb=pb: sc.activation(
                        TGB[t][:, dc * 259 + 3: dc * 259 + 3 + GT],
                        ps1(pb * 512, pb * 512 + GT), AF.Copy),
                        reads=bkb(pb), writes=[TBB[t]])
            slot, slb = ring_use(mb_ci)
            mms = []
            for fc in range(NFC):
                mms.append((PS4[0:16, 0:GT], slot[:, 64 + fc * 16: 64 + fc * 16 + 16],
                            xn[:, fc * TPC + t0: fc * TPC + t0 + GT], fc == 0, fc == 7))
            mmg(bk, mms, reads=[slb] + xnr, writes=bkb(7))
            bk.op("ac", lambda sc: sc.activation(brows[0:16, 0:GT], PS4[0:16, 0:GT],
                                                 AF.Sigmoid),
                  reads=bkb(7), writes=[B["brows"]])
            bk.op("ac", lambda sc: sc.activation(brows[0:16, 0:GT],
                                                 brows[0:16, 0:GT], AF.Sqrt),
                  reads=[B["brows"]], writes=[B["brows"]])
            for ti, t in enumerate("qkv"):
                for fc in range(NFC):
                    for k4 in range(4):
                        bk.op("dv", lambda ve, ti=ti, fc=fc, k4=k4: ve.tensor_scalar(
                            cwd[:, k4 * 128:(k4 + 1) * 128], CB("ident", 128),
                            cw[:, fc * 12 + ti * 4 + k4: fc * 12 + ti * 4 + k4 + 1],
                            None, ALU.mult),
                            reads=[B["cst"]], writes=[PpH[0]])
                    mms = [(ps1(1024, 1024 + GT), cwd[:, k4 * 128:(k4 + 1) * 128],
                            TGB[t][:, fc * 259 + k4: fc * 259 + k4 + GT],
                            k4 == 0, k4 == 3) for k4 in range(4)]
                    mmg(bk, mms, reads=[PpH[0], TBB[t]], writes=bkb(2))
                    bk.op("dv", lambda ve, ti=ti, t=t, fc=fc: ve.tensor_copy(
                        carry[:, ti * 24 + fc * 3: ti * 24 + fc * 3 + 3],
                        TGB[t][:, fc * 259 + GT: fc * 259 + GT + 3]),
                        reads=[TBB[t]], writes=[B["carry"]])
                    bk.op("ac", lambda sc, t=t, fc=fc: sc.activation(
                        TGB[t][:, fc * 259 + 3: fc * 259 + 3 + GT], ps1(1024, 1024 + GT),
                        AF.Silu), reads=bkb(2), writes=[TBB[t]])
            bk.op("ac", lambda sc: sc.activation(
                brb[0:16, 512:512 + GT], brows[0:16, 0:GT], AF.Copy),
                reads=[B["brows"]], writes=[B["brb"]])
            for t, rc in (("q", 256), ("k", 0)):
                for fc in range(NFC):
                    par = fc % 2
                    bk.op("ac", lambda sc, t=t, fc=fc, par=par: sc.activation(
                        xsq_t[par][:, 0:GT], TGB[t][:, fc * 259 + 3: fc * 259 + 3 + GT],
                        AF.Square), reads=[TBB[t]], writes=[Bxsq[par]])
                    mmg(bk, [(PS4[0:16, 256:256 + GT], CB("efc", 16, fc * 16),
                              xsq_t[par][:, 0:GT], fc == 0, fc == 7)],
                        reads=[Bxsq[par], B["cst"]], writes=bkb(7))
                bk.op("ac", lambda sc, rc=rc: sc.activation(
                    brows[0:16, rc:rc + GT], PS4[0:16, 256:256 + GT], AF.Sqrt),
                    reads=bkb(7), writes=[B["brows"]])
                bk.op("dv", lambda ve, rc=rc: ve.tensor_scalar(
                    brows[0:16, rc:rc + GT], brows[0:16, rc:rc + GT], 1e-6, None,
                    ALU.max), reads=[B["brows"]], writes=[B["brows"]])
                bk.op("dv", lambda ve, rc=rc: ve.reciprocal(
                    brows[0:16, rc:rc + GT], brows[0:16, rc:rc + GT]),
                    reads=[B["brows"]], writes=[B["brows"]])
            bk.op("dv", lambda ve: ve.tensor_scalar(
                brows[0:16, 256:256 + GT], brows[0:16, 256:256 + GT], 0.125, None,
                ALU.mult), reads=[B["brows"]], writes=[B["brows"]])
            for s_c, d_c in ((256, 0), (0, 256)):
                bk.op("ac", lambda sc, s=s_c, d=d_c: sc.activation(
                    brb[0:16, d:d + GT], brows[0:16, s:s + GT], AF.Copy),
                    reads=[B["brows"]], writes=[B["brb"]])
            for t, rc in (("q", 0), ("k", 256)):
                for fc in range(NFC):
                    pb = fc % 2
                    mmg(bk, [(ps1(pb * 512, pb * 512 + GT),
                              CBr("eft", 0, 16, fc * 128, 128),
                              brb[0:16, rc:rc + GT], True, True)],
                        reads=[B["brb"], B["cst"]], writes=bkb(pb))
                    bk.op("dv", lambda ve, t=t, fc=fc, pb=pb: ve.tensor_mul(
                        TGB[t][:, fc * 259 + 3: fc * 259 + 3 + GT],
                        TGB[t][:, fc * 259 + 3: fc * 259 + 3 + GT],
                        ps1(pb * 512, pb * 512 + GT)),
                        reads=[TBB[t]] + bkb(pb), writes=[TBB[t]])
            if stop == "noch":
                continue
            sub = 99
            if isinstance(stop, str) and stop.startswith("ch"):
                sub = int(stop[2:])
            pend = []
            for ci2 in range(2):
                pend = emit_chunk(g, ci2, sub=sub, pend=pend)
            for f in pend:
                f()

    def emit_corr():
        bk.op("dv", lambda ve: ve.tensor_scalar(
            Phi[0:64, :], Ssb[0:64, :], pcm[0:64, 0:1], None, ALU.mult),
            reads=[B["S"], B["cst"], B["Phi"]], writes=[B["Phi"]])
        bk.op("gp", lambda gs: gs.dma_start(out=D["sarc"], in_=Phi[0:64, :]),
              reads=[B["Phi"]], writes=[B["sarc"]], sem="gq", inc=16)
        bk.op("gp", lambda gs: gs.collective_compute(
            "AllReduce", ALU.add, replica_groups=PAIRS,
            ins=[D["sarc"].opt()], outs=[D["saro"].opt()]),
            reads=[B["sarc"]], writes=[B["saro"]], sem="cc", inc=1)
        bk.op("gp", lambda gs: gs.dma_start(out=Sin[0:64, :], in_=D["saro"]),
              reads=[B["saro"], PpH[0]], writes=[PpH[0]], sem="gq", inc=16)
        bk.op("dv", lambda ve: ve.tensor_scalar(
            Sin[0:64, :], Sin[0:64, :], pcm[0:64, 1:2], None, ALU.mult),
            reads=[PpH[0], B["cst"]], writes=[PpH[0]])
        wo_ci = [ring_add("sq", ("wot", hf)) for hf in range(2)]
        bk.op("dv", lambda ve: ve.engine_nop(),
              reads=[B["qg"], B["kg"], B["vg"], XsH[0], XsH[1]],
              writes=[B["obl"], B["ob2"], B["dbl"], B["db2"], B["onb"], B["onr"]])
        for c in range(NCH):
            par = c % 2
            oc, ocb = (obl, B["obl"]) if par == 0 else (ob2, B["ob2"])
            dc_, dcb = (dbl, B["dbl"]) if par == 0 else (db2, B["db2"])
            bk.op("sp", lambda sp, oc=oc, c=c: sp.dma_start(
                out=oc[:, :], in_=D["ot"][c * 128:(c + 1) * 128, :]),
                reads=[B["ot"]], writes=[ocb], sem=bk.S(f"clo{par}"), inc=16)

            def fdl(sp, dc_=dc_, c=c):
                return sp.dma_start(
                    out=dc_[0:64, :].rearrange("q (h2 blk j) -> q h2 blk j", h2=2, j=128),
                    in_=D["dt"][c * 128:(c + 1) * 128, :]
                    .rearrange("(h2 q) (blk j) -> q h2 blk j", h2=2, j=128))
            bk.op("sp", fdl, reads=[B["dt"]], writes=[dcb], sem=bk.S(f"cld{par}"), inc=16)
            mms = []
            for h in range(NH):
                mms.append((PS2[hp(h):hp(h) + 64, (h // 2) * 128:(h // 2) * 128 + 128],
                            Sin[0:64, h * 64:(h + 1) * 64],
                            dc_[0:64, (h % 2) * 1024 + (h // 2) * 128:(h % 2) * 1024 + (h // 2) * 128 + 128],
                            True, True))
            mmg(bk, mms, reads=[PpH[0], dcb], writes=bkb(4, 5))
            bk.op("dv", lambda ve, oc=oc: ve.tensor_add(onb[:, :], oc[:, :],
                                                        PS2[:, 0:1024]),
                  reads=[ocb] + bkb(4, 5) + [B["onb"]], writes=[B["onb"]])
            bk.op("ac", lambda sc: sc.activation(XK[:, 0:1024], onb[:, :], AF.Square),
                  reads=[B["onb"], XKH[0]], writes=[XKH[0]])
            mms = [(ps1(0, 512)[0:2, :], CB("oneslh", 2), XK[:, 0:512], True, True),
                   (ps1(512, 1024)[0:2, :], CB("oneslh", 2), XK[:, 512:1024], True, True)]
            mmg(bk, mms, reads=[XKH[0], B["cst"]], writes=bkb(0, 1))
            for hv in range(2):
                bk.op("ac", lambda sc, hv=hv: sc.activation(
                    brows[0:2, 0:512], ps1(hv * 512, (hv + 1) * 512)[0:2, :], AF.Sqrt,
                    bias=CFr("eps_a", 0, 2, 0, 1), scale=CFr("inv_hd", 0, 2, 0, 1)),
                    reads=bkb(hv) + [B["cst"], B["brows"]], writes=[B["brows"]])
                bk.op("dv", lambda ve: ve.reciprocal(brows[0:2, 0:512], brows[0:2, 0:512]),
                      reads=[B["brows"]], writes=[B["brows"]])
                bk.op("ac", lambda sc, hv=hv: sc.activation(
                    Xs[0:2, 1024 + hv * 512:1536 + hv * 512], brows[0:2, 0:512], AF.Copy),
                    reads=[B["brows"], B["onr"]], writes=[B["onr"]])
            mms = [(ps1(1024, 1536), CBr("e2", 0, 2, 0, 128), Xs[0:2, 1024:1536], True, True),
                   (ps1(1536, 2048), CBr("e2", 0, 2, 0, 128), Xs[0:2, 1536:2048], True, True)]
            mmg(bk, mms, reads=[B["onr"], B["cst"]], writes=bkb(2, 3))
            bk.op("dv", lambda ve: ve.tensor_mul(onb[:, :], onb[:, :], ps1(1024, 2048)),
                  reads=[B["onb"]] + bkb(2, 3), writes=[B["onb"]])
            slot0, slb0 = ring_use(wo_ci[0])
            slot1, slb1 = ring_use(wo_ci[1])
            mms = []
            for dc2 in range(8):
                slot = slot0 if dc2 < 4 else slot1
                for fci_ in range(8):
                    mms.append((ps1(dc2 * 128, (dc2 + 1) * 128),
                                slot[:, fci_ * 512 + (dc2 % 4) * 128: fci_ * 512 + (dc2 % 4) * 128 + 128],
                                onb[:, fci_ * 128:(fci_ + 1) * 128],
                                fci_ == 0, fci_ == 7))
            mmg(bk, mms, reads=[B["onb"], slb0, slb1], writes=bkb(0, 1))

            def fadd(ve, c=c):
                xo = x_sb[:, :].rearrange("p (fc t) -> p fc t", fc=8)[:, :, c * 128:(c + 1) * 128]
                return ve.tensor_add(xo, xo, ps1(0, 1024).rearrange("p (f t) -> p f t", f=8))
            bk.op("dv", fadd, reads=bkb(0, 1) + [B[f"x{i}"] for i in range(NFC)],
                  writes=[B[f"x{i}"] for i in range(NFC)])


    emit_ffn()
    if stop not in ("ln1", "gates", "ffn1"):
        emit_attention(stop=stop)
        if stop not in ("halo", "grp0", "noch", "groups") and not str(stop).startswith("ch"):
            emit_corr()
        if stop in ("attn", "halo", "grp0", "noch", "groups") or str(stop).startswith("ch"):
            bk.op("sp", lambda sp: sp.dma_start(out=D["dbg"], in_=x_sb[:, :]),
                  reads=[B[f"x{i}"] for i in range(NFC)], sem=bk.S("so"), inc=16)
        else:
            bk.op("dv", lambda ve: ve.engine_nop(),
                  reads=[B["QQb"], B["osb"], B["obl"], B["ob2"], B["onr"],
                         B["dsb"], B["dbl"], B["db2"], B["onb"],
                         PpH[0], PpH[1], PtH[0], PtH[1],
                         XsH[0], XsH[1], XKH[0], XKH[1], B["qnc"]],
                  writes=[Bhgt[0], Bhgt[1], Bh[0], Bh[1], Bgb[0], Bgb[1],
                          B["qg"], B["kg"], B["vg"], B["grow"], XsH[0], XsH[1]])
            emit_ffn()

    if stop == "ffn1":
        bk.op("sp", lambda sp: sp.dma_start(
            out=D["dbg"], in_=x_sb[:, :]),
            reads=[B[f"x{i}"] for i in range(NFC)], sem=bk.S("so"), inc=16)
    elif stop == "ln1":
        bk.op("gp", lambda gs: gs.dma_start(out=D["dbg"], in_=xn[:, :]),
              reads=[B[f"xn{i}"] for i in range(NFC)], sem=bk.S("so"), inc=16)
    elif stop == "gates":
        bk.op("gp", lambda gs: gs.dma_start(out=D["dbg"][0:8, 0:TPC], in_=grow[:, :]),
              reads=[B["grow"]], sem=bk.S("so"), inc=16)

    bk.op("sp", lambda sp: sp.dma_start(
        out=D["y"].rearrange("(fc p) t -> p fc t", p=128),
        in_=x_sb[:, :].rearrange("p (fc t) -> p fc t", fc=8)),
        reads=[B[f"x{i}"] for i in range(NFC)], sem=bk.S("so"), inc=16)
    final_waits = [bk.now(s) for s in list(bk.cnt)]
    bk.op("sp", lambda sp: None, waits=final_waits, sem="so", inc=0)

    sems = {}
    import contextlib as _cl
    semctx = _cl.ExitStack()
    for name in bk.cnt:
        sems[name] = semctx.enter_context(nc.semaphore(name))
    with nc.Block() as block:
        bk.replay(block, sems)
    semctx.close()
    ctx.close()
    return nc



def _stage_inputs(inputs):
    x = np.asarray(inputs["L0_attn_x"], np.float32)
    shared = {
        "w1t": np.ascontiguousarray(
            np.asarray(inputs["L0_ffn_w1"], np.float32).transpose(0, 2, 1)).reshape(8 * 1024, 2048),
        "w2t": np.ascontiguousarray(
            np.asarray(inputs["L0_ffn_w2"], np.float32).transpose(0, 2, 1)).reshape(8 * 2048, 1024),
        "wqt": np.ascontiguousarray(np.asarray(inputs["L0_attn_Wq"], np.float32).T),
        "wkt": np.ascontiguousarray(np.asarray(inputs["L0_attn_Wk"], np.float32).T),
        "wvt": np.ascontiguousarray(np.asarray(inputs["L0_attn_Wv"], np.float32).T),
        "wot": np.ascontiguousarray(np.asarray(inputs["L0_attn_Wo"], np.float32).T),
        "wbt": np.ascontiguousarray(np.asarray(inputs["L0_attn_Wb"], np.float32).T),
        "rwt": np.ascontiguousarray(np.asarray(inputs["L0_ffn_router_w"], np.float32).T),
        "cwt": np.ascontiguousarray(np.concatenate(
            [np.asarray(inputs["L0_attn_conv_q"], np.float32),
             np.asarray(inputs["L0_attn_conv_k"], np.float32),
             np.asarray(inputs["L0_attn_conv_v"], np.float32)], 1)),
        "onw": np.ascontiguousarray(
            np.tile(np.asarray(inputs["L0_attn_o_norm_w"], np.float32), 2))[:, None],
        "cbf": CBF_HOST,
        "cf": CF_HOST,
    }
    ims = []
    for c in range(8):
        b, hf = c // 2, c % 2
        pcmask = np.zeros((128, 2), np.float32)
        pcmask[:, 0] = 1.0 if c % 2 == 0 else 0.0
        pcmask[:, 1] = 1.0 if c % 2 == 1 else 0.0
        im = dict(shared)
        im["xt"] = np.ascontiguousarray(x[b, hf * TPC:(hf + 1) * TPC].T)
        im["pcm"] = pcmask
        ims.append(im)
    return ims


def kernel(**inputs):
    ims = _stage_inputs(inputs)
    nc = build_nc()
    res = run_bass_kernel_spmd(nc, ims, core_ids=list(range(8)))
    out = np.empty((4, 4096, 1024), np.float32)
    for c in range(8):
        b, hf = c // 2, c % 2
        out[b, hf * TPC:(hf + 1) * TPC] = np.asarray(res.results[c]["y"], np.float32).T
    return out
